# version 15
# baseline (speedup 1.0000x reference)
import contextlib
import numpy as np
import ml_dtypes
import concourse.bass as bass
from concourse.bass import IndirectOffsetOnAxis
import concourse.mybir as mybir
from concourse.bass_utils import run_bass_kernel_spmd

F32 = mybir.dt.float32
BF16 = mybir.dt.bfloat16
I32 = mybir.dt.int32
AX = mybir.AxisListType.X
ALU = mybir.AluOpType
ACT = mybir.ActivationFunctionType

LN_EPS = 1e-5
SHIFT = 12.0
P = 128
USE_BRANCHES = True
KMARG = 64
BR_G = True
BR_T = True
BR_C = True
BR_V = True


def build_program(D=1024, F=4096, T=8192, BT=1024, OWN=2048, TGW=256, FSUB=1024,
                  NE=8):
    DC = D // P
    KN = BT // P
    NB = T // BT
    OC = OWN // P
    OWNB = OWN // BT
    QTW = min(512, BT)
    NQT = BT // QTW
    DTW = min(512, D)
    ND = D // DTW
    FCS = FSUB // P
    SC = F // FSUB
    FCT = F // P
    NTG = OWN // TGW
    TGC = TGW // P
    G4 = min(4, KN)
    GD = min(4, DC)
    NG = DC // GD
    SMAX = T - P
    MW = SMAX + OWN
    WS = OWN + (KN - 1) * P
    invD = 1.0 / D
    att_scale = 1.0 / float(np.sqrt(D))
    assert KN * BT >= DC * D

    nc = bass.Bass("TRN2", target_bir_lowering=False, detect_race_conditions=False)

    xb = nc.dram_tensor("xb", [T, D], F32, kind="ExternalInput").ap()
    xq = nc.dram_tensor("xq", [OWN, D], F32, kind="ExternalInput").ap()
    mk = nc.dram_tensor("mk", [P, MW], BF16, kind="ExternalInput").ap()
    wq = nc.dram_tensor("wq", [D, D], F32, kind="ExternalInput").ap()
    wk = nc.dram_tensor("wk", [D, D], F32, kind="ExternalInput").ap()
    wv = nc.dram_tensor("wv", [D, D], F32, kind="ExternalInput").ap()
    rw = nc.dram_tensor("rw", [D, NE], F32, kind="ExternalInput").ap()
    rb = nc.dram_tensor("rb", [1, NE], F32, kind="ExternalInput").ap()
    w1 = nc.dram_tensor("w1", [NE, D, F], F32, kind="ExternalInput").ap()
    b1 = nc.dram_tensor("b1", [P, NE * FCT], F32, kind="ExternalInput").ap()
    w2 = nc.dram_tensor("w2", [NE, F, D], F32, kind="ExternalInput").ap()
    b2 = nc.dram_tensor("b2", [NE, D], F32, kind="ExternalInput").ap()
    yo = nc.dram_tensor("y", [OWN, D], F32, kind="ExternalOutput").ap()
    x1d = nc.dram_tensor("x1d", [OWN, D], BF16).ap()
    rowd = nc.dram_tensor("rowd", [NE, OC], F32).ap()
    flid = nc.dram_tensor("flid", [NE, NTG], I32).ap()
    efld = nc.dram_tensor("efld", [NE, 1], I32).ap()

    c = {"g": 0, "s": 0, "t": 0, "v": 0, "c": 0, "gc": 0}

    with contextlib.ExitStack() as _es0:
        _e = _es0.enter_context
        O_acc = _e(nc.sbuf_tensor("O_acc", [P, OC * D], BF16))
        XQR = _e(nc.sbuf_tensor("XQR", [P, OC * D], BF16))
        Lb = _e(nc.sbuf_tensor("Lb", [P, OC], F32))
        SS = _e(nc.sbuf_tensor("SS", [P, max(KN, OC)], F32))
        SQ = _e(nc.sbuf_tensor("SQ", [P, max(KN, OC)], F32))
        MEANB = _e(nc.sbuf_tensor("MEANB", [P, max(KN, OC)], F32))
        T1B = _e(nc.sbuf_tensor("T1B", [P, max(KN, OC)], F32))
        VARPB = _e(nc.sbuf_tensor("VARPB", [P, max(KN, OC)], F32))
        RSTDB = _e(nc.sbuf_tensor("RSTDB", [P, max(KN, OC)], F32))
        NBSB = _e(nc.sbuf_tensor("NBSB", [P, max(KN, OC)], F32))
        IDENT = _e(nc.sbuf_tensor("IDENT", [P, P], BF16))
        ones1 = _e(nc.sbuf_tensor("ones1", [1, P], BF16))
        onescol = _e(nc.sbuf_tensor("onescol", [P, 1], BF16))
        GATES = _e(nc.sbuf_tensor("GATES", [P, OC * NE], F32))
        B1all = _e(nc.sbuf_tensor("B1all", [P, NE * FCT], F32))
        WrB = _e(nc.sbuf_tensor("WrB", [P, DC * NE], BF16))
        WrLO = _e(nc.sbuf_tensor("WrLO", [P, DC * NE], BF16))
        WrF = _e(nc.sbuf_tensor("WrF", [P, DC * NE], F32))
        RB = _e(nc.sbuf_tensor("RB", [1, NE], BF16))
        RBLO = _e(nc.sbuf_tensor("RBLO", [1, NE], BF16))
        RBF = _e(nc.sbuf_tensor("RBF", [1, NE], F32))
        NSH = _e(nc.sbuf_tensor("NSH", [P, 1], F32))
        LTS = _e(nc.sbuf_tensor("LTS", [P, P], BF16))
        LTS8F = _e(nc.sbuf_tensor("LTS8F", [NE, NE], F32))
        ERMP = _e(nc.sbuf_tensor("ERMP", [P, NE], F32))
        TOKC = _e(nc.sbuf_tensor("TOKC", [P, OC], F32))
        ONES8R = _e(nc.sbuf_tensor("ONES8R", [NE, OC], F32))
        IND = _e(nc.sbuf_tensor("IND", [P, OC * NE], BF16))
        TOKCH = _e(nc.sbuf_tensor("TOKCH", [P, OC], BF16))
        TOKCL = _e(nc.sbuf_tensor("TOKCL", [P, OC], BF16))
        INVF = _e(nc.sbuf_tensor("INVF", [P, OC], F32))
        BASEB = _e(nc.sbuf_tensor("BASEB", [P, NE], F32))
        CNT = _e(nc.sbuf_tensor("CNT", [NE, OC], F32))
        CUMI = _e(nc.sbuf_tensor("CUMI", [NE, OC], F32))
        CUMTOT = _e(nc.sbuf_tensor("CUMTOT", [NE, OC], F32))
        TOT = _e(nc.sbuf_tensor("TOT", [NE, 1], F32))
        BASEC = _e(nc.sbuf_tensor("BASEC", [NE, 1], F32))
        ENDC = _e(nc.sbuf_tensor("ENDC", [NE, 1], F32))
        FL8 = _e(nc.sbuf_tensor("FL8", [NE, NTG], F32))
        FL8I = _e(nc.sbuf_tensor("FL8I", [NE, NTG], I32))
        EFL8I = _e(nc.sbuf_tensor("EFL8I", [NE, 1], I32))
        T81 = _e(nc.sbuf_tensor("T81", [NE, 1], F32))
        T82 = _e(nc.sbuf_tensor("T82", [NE, 1], F32))
        FLROW = _e(nc.sbuf_tensor("FLROW", [1, NE * NTG], I32))
        EFROW = _e(nc.sbuf_tensor("EFROW", [1, NE], I32))
        ROWSF = _e(nc.sbuf_tensor("ROWSF", [1, OC * NE], F32))
        ROWSH = _e(nc.sbuf_tensor("ROWSH", [1, OC * NE], BF16))
        ROWSL = _e(nc.sbuf_tensor("ROWSL", [1, OC * NE], BF16))
        T8 = _e(nc.sbuf_tensor("T8", [P, NE], F32))
        T8B = _e(nc.sbuf_tensor("T8B", [P, OC], F32))
        GSEL = _e(nc.sbuf_tensor("GSEL", [P, OC], F32))
        DESTF = _e(nc.sbuf_tensor("DESTF", [P, OC], F32))
        ESRT = _e(nc.sbuf_tensor("ESRT", [P, OC], F32))
        SCL = _e(nc.sbuf_tensor("SCL", [P, TGC], F32))
        DUMB = _e(nc.sbuf_tensor("DUMB", [1, 8], F32))
        sg = _e(nc.semaphore("sg"))
        ssm = _e(nc.semaphore("ssm"))
        st = _e(nc.semaphore("st"))
        sv = _e(nc.semaphore("sv"))
        scm = _e(nc.semaphore("scm"))
        sgc = _e(nc.semaphore("sgc"))
        def gdma(eng, dst, src):
            eng.dma_start(dst, src).then_inc(sg, 16)
            c["g"] += 16

        def sdma(eng, dst, src):
            eng.dma_start(dst, src).then_inc(ssm, 16)
            c["s"] += 16

        def vhop(v, ins):
            ins.then_inc(sv, 1)
            c["v"] += 1
            v.wait_ge(sv, c["v"])

        def chop(s, ins):
            ins.then_inc(scm, 1)
            c["c"] += 1
            s.wait_ge(scm, c["c"])

        def batch_stats(v, src_buf, nchunk, add_eps_lsq=None):
            for i in range(nchunk):
                sl = src_buf(i)
                v.reduce_sum(SS[:, i:i + 1], sl, axis=AX)
                ins = v.scalar_tensor_tensor(TRX_cur[0][:, 0:D], sl, 1.0, sl,
                                             op0=ALU.mult, op1=ALU.mult,
                                             accum_out=SQ[:, i:i + 1])
            vhop(v, ins)
            ck = slice(0, nchunk)
            vhop(v, v.tensor_scalar_mul(MEANB[:, ck], SS[:, ck], invD))
            vhop(v, v.tensor_mul(T1B[:, ck], MEANB[:, ck], MEANB[:, ck]))
            vhop(v, v.scalar_tensor_tensor(VARPB[:, ck], SQ[:, ck], invD,
                                           T1B[:, ck], op0=ALU.mult,
                                           op1=ALU.subtract))
            if add_eps_lsq is None:
                ins = v.tensor_scalar_add(VARPB[:, ck], VARPB[:, ck], LN_EPS)
            else:
                vhop(v, v.tensor_mul(T1B[:, ck], add_eps_lsq[:, ck],
                                     add_eps_lsq[:, ck]))
                ins = v.scalar_tensor_tensor(VARPB[:, ck], T1B[:, ck], LN_EPS,
                                             VARPB[:, ck], op0=ALU.mult,
                                             op1=ALU.add)
            ins.then_inc(sv, 1)
            c["v"] += 1

        def rstd_scalar(s, nchunk):
            s.wait_ge(sv, c["v"])
            ck = slice(0, nchunk)
            chop(s, s.activation(T1B[:, ck], VARPB[:, ck], ACT.Ln))
            s.activation(RSTDB[:, ck], T1B[:, ck], ACT.Exp,
                         scale=-0.5).then_inc(scm, 1)
            c["c"] += 1

        def nbias_vec(v, nchunk):
            v.wait_ge(scm, c["c"])
            ck = slice(0, nchunk)
            v.scalar_tensor_tensor(NBSB[:, ck], MEANB[:, ck], -1.0, RSTDB[:, ck],
                                   op0=ALU.mult, op1=ALU.mult).then_inc(sv, 1)
            c["v"] += 1

        TRX_cur = [None]

        with contextlib.ExitStack() as _esA:
            _a = _esA.enter_context
            IOTF = _a(nc.sbuf_tensor("IOTF", [P, P], F32))
            Mstep = _a(nc.sbuf_tensor("Mstep", [P, WS], BF16))
            WkB = _a(nc.sbuf_tensor("WkB", [P, DC * D], BF16))
            WvB = _a(nc.sbuf_tensor("WvB", [P, DC * D], BF16))
            QFM = _a(nc.sbuf_tensor("QFM", [P, DC * OWN], BF16))
            TMP = _a(nc.sbuf_tensor("TMP", [P, KN * D], BF16))
            KBUF = _a(nc.sbuf_tensor("KBUF", [P, KN * D], BF16))
            LNFM = _a(nc.sbuf_tensor("LNFM", [P, DC * BT], BF16))
            PT = _a(nc.sbuf_tensor("PT", [P, KN * BT], BF16))
            SQT = _a(nc.sbuf_tensor("SQT", [P, D], BF16))
            psT = _a(nc.psum_tensor("psT", [P, 512], BF16))
            pp0 = _a(nc.psum_tensor("pp0", [P, 512], F32))
            pp1 = _a(nc.psum_tensor("pp1", [P, 512], F32))
            pv0 = _a(nc.psum_tensor("pv0", [P, min(D, 1024)], F32))
            pv1 = _a(nc.psum_tensor("pv1", [P, min(D, 1024)], F32))
            psL = _a(nc.psum_tensor("psL", [P, OC], F32))
            pp = [pp0, pp1]
            pv = [pv0, pv1]

            with nc.Block() as blk:
                @blk.gpsimd
                def _(g):
                    g.iota(IOTF[:, :], [[1, P]], base=0, channel_multiplier=-1,
                           allow_small_or_imprecise_dtypes=True)
                    g.iota(ERMP[:, :], [[1, NE]], base=0, channel_multiplier=0,
                           allow_small_or_imprecise_dtypes=True)
                    g.iota(TOKC[:, :], [[P, OC]], base=0, channel_multiplier=1,
                           allow_small_or_imprecise_dtypes=True).then_inc(sgc, 1)
                    c["gc"] += 1

                @blk.sync
                def _(s):
                    sdma(s, B1all[:, :], b1)
                    sdma(s, WrF[:, :].rearrange("p (dc e) -> p dc e", e=NE),
                         rw.rearrange("(dc p) e -> p dc e", p=P))
                    sdma(s, RBF[:, :], rb)
                    s.wait_ge(ssm, c["s"])
                s_init = c["s"]

                @blk.vector
                def _(v):
                    v.wait_ge(sgc, 1)
                    v.tensor_single_scalar(IDENT[:, :], IOTF[:, :], 0.0,
                                           op=ALU.is_equal)
                    vhop(v, v.tensor_single_scalar(LTS[:, :], IOTF[:, :], 0.0,
                                                   op=ALU.is_gt))
                    v.tensor_copy(LTS8F[:, :], LTS[0:NE, 0:NE])
                    v.memset(ONES8R[:, :], 1.0)
                    vhop(v, v.tensor_copy(TOKCH[:, :], TOKC[:, :]))
                    v.tensor_sub(TOKCL[:, :], TOKC[:, :], TOKCH[:, :])
                    v.memset(ones1[:, :], 1.0)
                    v.memset(onescol[:, :], 1.0)
                    v.memset(NSH[:, :], -SHIFT)
                    v.memset(O_acc[:, :], 0.0)
                    v.memset(Lb[:, :], 0.0)
                    v.wait_ge(ssm, s_init)
                    vhop(v, v.tensor_copy(WrB[:, :], WrF[:, :]))
                    v.tensor_sub(WrLO[:, :], WrF[:, :], WrB[:, :])
                    vhop(v, v.tensor_copy(RB[:, :], RBF[:, :]))
                    v.tensor_sub(RBLO[:, :], RBF[:, :], RB[:, :])

            def ln_stats_chunk_ops(g_wait, XB):
                ops = []

                def mk(kn):
                    def f(v):
                        if kn == 0:
                            v.wait_ge(sg, g_wait)
                        sl = XB[:, kn * D:(kn + 1) * D]
                        v.reduce_sum(SS[:, kn:kn + 1], sl, axis=AX)
                        v.scalar_tensor_tensor(SQT[:, 0:D], sl, 1.0, sl,
                                               op0=ALU.mult, op1=ALU.mult,
                                               accum_out=SQ[:, kn:kn + 1])
                    return f
                for kn in range(KN):
                    ops.append(mk(kn))
                return ops

            def ln_stats_tail(v):
                v.wait_ge(sv, c["v"])
                v.engine_nop().then_inc(sv, 1)
                c["v"] += 1
                v.wait_ge(sv, c["v"])
                ck = slice(0, KN)
                vhop(v, v.tensor_scalar_mul(MEANB[:, ck], SS[:, ck], invD))
                vhop(v, v.tensor_mul(T1B[:, ck], MEANB[:, ck], MEANB[:, ck]))
                vhop(v, v.scalar_tensor_tensor(VARPB[:, ck], SQ[:, ck], invD,
                                               T1B[:, ck], op0=ALU.mult,
                                               op1=ALU.subtract))
                v.tensor_scalar_add(VARPB[:, ck], VARPB[:, ck],
                                    LN_EPS).then_inc(sv, 1)
                c["v"] += 1

            def ln_stats_part(blk, g_wait, XB):
                ops = ln_stats_chunk_ops(g_wait, XB)

                @blk.vector
                def _(v):
                    for f in ops:
                        f(v)
                    ln_stats_tail(v)

            def ln_and_transpose(blk, g_wait, XB, LB, pv_guard=None,
                                 stats_done=False):
                if not stats_done:
                    ln_stats_part(blk, g_wait, XB)

                @blk.scalar
                def _(s):
                    rstd_scalar(s, KN)

                @blk.vector
                def _(v):
                    nbias_vec(v, KN)
                v_nb = c["v"]

                c_apply0 = c["c"]

                @blk.scalar
                def _(s):
                    s.wait_ge(sv, v_nb)
                    if pv_guard is not None:
                        s.wait_ge(st, pv_guard)
                    for kn in range(KN):
                        s.activation(LB[:, kn * D:(kn + 1) * D],
                                     XB[:, kn * D:(kn + 1) * D], ACT.Identity,
                                     bias=NBSB[:, kn:kn + 1],
                                     scale=RSTDB[:, kn:kn + 1]).then_inc(scm, 1)
                        c["c"] += 1

                ngrp = KN // G4
                t_base = c["t"]
                v_base = c["v"]

                @blk.tensor
                def _(t):
                    i = 0
                    for dc in range(DC):
                        for gi in range(ngrp):
                            t.wait_ge(scm, c_apply0 + (gi + 1) * G4)
                            t.wait_ge(sv, v_base + i)
                            for k4 in range(G4):
                                kn = gi * G4 + k4
                                ins = t.transpose(
                                    psT[:, k4 * P:(k4 + 1) * P],
                                    LB[:, kn * D + dc * P:kn * D + (dc + 1) * P],
                                    IDENT[:, :])
                            ins.then_inc(st, 1)
                            c["t"] += 1
                            i += 1

                @blk.vector
                def _(v):
                    i = 0
                    for dc in range(DC):
                        for gi in range(ngrp):
                            v.wait_ge(st, t_base + i + 1)
                            v.tensor_copy(
                                LNFM[:, dc * BT + gi * G4 * P:
                                     dc * BT + (gi + 1) * G4 * P],
                                psT[:, 0:G4 * P]).then_inc(sv, 1)
                            c["v"] += 1
                            i += 1

            def proj_fills(blk, fills, dsts, sg_wait=None):
                t_base = c["t"]
                c_base = c["c"]
                v_pre = c["v"]

                @blk.tensor
                def _(t):
                    if sg_wait is not None:
                        t.wait_ge(sg, sg_wait)
                    t.wait_ge(sv, v_pre)
                    for i, (mms, w) in enumerate(fills):
                        if c_base + i - 1 > 0:
                            t.wait_ge(scm, c_base + i - 1)
                        dst = pp[i % 2][:, 0:w]
                        for k, (lh, rh) in enumerate(mms):
                            ins = t.matmul(dst, lh, rh, start=(k == 0),
                                           stop=(k == len(mms) - 1))
                        ins.then_inc(st, 1)
                        c["t"] += 1

                @blk.scalar
                def _(s):
                    for i, (mms, w) in enumerate(fills):
                        s.wait_ge(st, t_base + i + 1)
                        s.activation(dsts[i], pp[i % 2][:, 0:w],
                                     ACT.Copy).then_inc(scm, 1)
                        c["c"] += 1

            wgt = {}
            tr_prev = [0]
            for ob in range(OWNB):
                with nc.Block() as blk:
                    if ob == 0:
                        @blk.gpsimd
                        def _(g):
                            for h in range(2):
                                gdma(g, XQR[:, h * KN * D:(h + 1) * KN * D]
                                     .rearrange("p (n d) -> p n d", d=D),
                                     xq[h * BT:(h + 1) * BT, :].rearrange(
                                         "(n p) d -> p n d", p=P))
                                wgt[f"x{h}"] = c["g"]
                                if h == 0:
                                    gdma(g, PT[:, 0:DC * D].rearrange(
                                             "p (dc e) -> p dc e", e=D),
                                         wq.rearrange("(dc p) e -> p dc e",
                                                      p=P))
                                    wgt["q"] = c["g"]
                    ln_and_transpose(blk, wgt[f"x{ob}"],
                                     XQR[:, ob * KN * D:(ob + 1) * KN * D],
                                     KBUF,
                                     pv_guard=(tr_prev[0] if ob else None))
                    tr_prev[0] = c["t"]

                    fills, dsts = [], []
                    for dco in range(DC):
                        for qt in range(NQT):
                            fills.append((
                                [(PT[:, dci * D + dco * P:dci * D + (dco + 1) * P],
                                  LNFM[:, dci * BT + qt * QTW:
                                       dci * BT + (qt + 1) * QTW])
                                 for dci in range(DC)], QTW))
                            dsts.append(QFM[:, dco * OWN + ob * BT + qt * QTW:
                                             dco * OWN + ob * BT + (qt + 1) * QTW])
                    proj_fills(blk, fills, dsts,
                               sg_wait=(wgt["q"] if ob == 0 else None))
            tr_prev = None

            Sdone = {}
            PVdone = {}
            Mdone = {}
            gx_next = [0]
            with nc.Block() as blk:
              for j in range(NB):
                    XB = TMP if j % 2 == 0 else KBUF
                    LB = KBUF if j % 2 == 0 else TMP

                    @blk.gpsimd
                    def _(g):
                        if j >= 1:
                            g.wait_ge(st, Sdone[j - 1])
                        gdma(g, XB[:, :].rearrange("p (n d) -> p n d", d=D),
                             xb[j * BT:(j + 1) * BT, :].rearrange(
                                 "(n p) d -> p n d", p=P))
                        if j == 0:
                            gdma(g, WkB[:, :].rearrange(
                                     "p (dc e) -> p dc e", e=D),
                                 wk.rearrange("(dc p) e -> p dc e", p=P))
                            wgt["k"] = c["g"]
                            gdma(g, WvB[:, :].rearrange(
                                     "p (dc e) -> p dc e", e=D),
                                 wv.rearrange("(dc p) e -> p dc e", p=P))
                            wgt["v"] = c["g"]
                    g_x = c["g"] - (32 if j == 0 else 0)
                    gx_next[0] = c["g"] + 16

                    @blk.sync
                    def _(s):
                        if j >= 1:
                            s.wait_ge(sv, Mdone[j - 1])
                        base = SMAX - j * BT - (KN - 1) * P
                        sdma(s, Mstep[:, :], mk[:, base:base + WS])
                    s_m = c["s"]

                    ln_and_transpose(blk, g_x, XB, LB,
                                     pv_guard=PVdone.get(j - 1),
                                     stats_done=(j >= 1))

                    fills, dsts = [], []
                    for dco in range(DC):
                        for kt in range(NQT):
                            fills.append((
                                [(WkB[:, dci * D + dco * P:dci * D + (dco + 1) * P],
                                  LNFM[:, dci * BT + kt * QTW:
                                       dci * BT + (kt + 1) * QTW])
                                 for dci in range(DC)], QTW))
                            dsts.append(LB[:, dco * BT + kt * QTW:
                                           dco * BT + (kt + 1) * QTW])
                    proj_fills(blk, fills, dsts,
                               sg_wait=(wgt["k"] if j == 0 else None))

                    fills, dsts = [], []
                    for kn in range(KN):
                        for dt in range(ND):
                            fills.append((
                                [(LNFM[:, dci * BT + kn * P:dci * BT + (kn + 1) * P],
                                  WvB[:, dci * D + dt * DTW:
                                       dci * D + (dt + 1) * DTW])
                                 for dci in range(DC)], DTW))
                            dsts.append(XB[:, kn * D + dt * DTW:
                                           kn * D + (dt + 1) * DTW])
                    proj_fills(blk, fills, dsts,
                               sg_wait=(wgt["v"] if j == 0 else None))

                    for qh in range(OWNB):
                        t_base = c["t"]
                        c_base = c["c"]
                        v_base = c["v"]

                        @blk.tensor
                        def _(t):
                            i = 0
                            for kc in range(KN):
                                for qt in range(NQT):
                                    if c_base + i - 1 > 0:
                                        t.wait_ge(scm, c_base + i - 1)
                                    dst = pp[i % 2][:, 0:QTW]
                                    for dc in range(DC):
                                        ins = t.matmul(
                                            dst,
                                            LB[:, dc * BT + kc * P:
                                               dc * BT + (kc + 1) * P],
                                            QFM[:, dc * OWN + qh * BT + qt * QTW:
                                                dc * OWN + qh * BT + (qt + 1) * QTW],
                                            start=(dc == 0), stop=(dc == DC - 1))
                                    ins.then_inc(st, 1)
                                    c["t"] += 1
                                    i += 1
                            if qh == OWNB - 1:
                                Sdone[j] = c["t"]

                        @blk.scalar
                        def _(s):
                            i = 0
                            for kc in range(KN):
                                for qt in range(NQT):
                                    s.wait_ge(st, t_base + i + 1)
                                    s.activation(
                                        PT[:, kc * BT + qt * QTW:
                                           kc * BT + (qt + 1) * QTW],
                                        pp[i % 2][:, 0:QTW], ACT.Exp,
                                        bias=NSH[:, :],
                                        scale=att_scale).then_inc(scm, 1)
                                    c["c"] += 1
                                    i += 1

                        @blk.vector
                        def _(v):
                            v.wait_ge(ssm, s_m)
                            for kc in range(KN):
                                v.wait_ge(scm, c_base + (kc + 1) * NQT)
                                moff = (KN - 1 - kc) * P + qh * BT
                                v.tensor_mul(PT[:, kc * BT:(kc + 1) * BT],
                                             PT[:, kc * BT:(kc + 1) * BT],
                                             Mstep[:, moff:moff + BT]
                                             ).then_inc(sv, 1)
                                c["v"] += 1
                            if qh == OWNB - 1:
                                Mdone[j] = c["v"]
                        pend_stats = None
                        if qh == OWNB - 1 and j + 1 < NB:
                            XB2 = TMP if (j + 1) % 2 == 0 else KBUF
                            pend_stats = ln_stats_chunk_ops(gx_next[0], XB2)
                        last_step = (j == NB - 1)

                        def o_sum(v, qcg):
                            sl = O_acc[:, qcg * D:(qcg + 1) * D]
                            v.reduce_sum(SS[:, qcg:qcg + 1], sl, axis=AX)
                            v.scalar_tensor_tensor(SQT[:, 0:D], sl, 1.0, sl,
                                                   op0=ALU.mult, op1=ALU.mult,
                                                   accum_out=SQ[:, qcg:qcg + 1])
                        v_mask = c["v"]

                        t_base2 = c["t"]
                        v_base2 = c["v"]

                        @blk.tensor
                        def _(t):
                            t.wait_ge(sv, v_mask)
                            for qc in range(KN):
                                qcg = qh * KN + qc
                                if qc >= 2:
                                    t.wait_ge(sv, v_base2 + 2 * qc - 2)
                                if qc >= 1:
                                    t.wait_ge(sv, v_base2 + 2 * qc - 1)
                                for kc in range(KN):
                                    lh = PT[:, kc * BT + qc * P:
                                            kc * BT + (qc + 1) * P]
                                    for dt in range(ND):
                                        t.matmul(
                                            pv[qc % 2][:, dt * DTW:(dt + 1) * DTW],
                                            lh, XB[:, kc * D + dt * DTW:
                                                   kc * D + (dt + 1) * DTW],
                                            start=(kc == 0), stop=(kc == KN - 1))
                                    ins = t.matmul(psL[:, qcg:qcg + 1], lh,
                                                   onescol[:, :], start=(kc == 0),
                                                   stop=(kc == KN - 1))
                                ins.then_inc(st, 1)
                                c["t"] += 1
                            if qh == OWNB - 1:
                                PVdone[j] = c["t"]

                        @blk.vector
                        def _(v):
                            for qc in range(KN):
                                qcg = qh * KN + qc
                                v.wait_ge(st, t_base2 + qc + 1)
                                v.tensor_add(Lb[:, qcg:qcg + 1],
                                             Lb[:, qcg:qcg + 1],
                                             psL[:, qcg:qcg + 1]).then_inc(sv, 1)
                                c["v"] += 1
                                for dt in range(ND):
                                    sl = O_acc[:, qcg * D + dt * DTW:
                                               qcg * D + (dt + 1) * DTW]
                                    ins = v.tensor_add(sl, sl,
                                                       pv[qc % 2][:, dt * DTW:
                                                                  (dt + 1) * DTW])
                                ins.then_inc(sv, 1)
                                c["v"] += 1
                                if pend_stats and qc >= 2:
                                    for _ in range(3):
                                        if pend_stats:
                                            pend_stats.pop(0)(v)
                                if last_step and qc >= 1:
                                    o_sum(v, qh * KN + qc - 1)
                            if pend_stats is not None:
                                while pend_stats:
                                    pend_stats.pop(0)(v)
                                ln_stats_tail(v)
                            if last_step:
                                o_sum(v, qh * KN + KN - 1)
                                if qh == OWNB - 1:
                                    v.wait_ge(sv, c["v"])
                                    v.engine_nop().then_inc(sv, 1)
                                    c["v"] += 1
                                    v.wait_ge(sv, c["v"])
                                    ck = slice(0, OC)
                                    vhop(v, v.tensor_scalar_mul(MEANB[:, ck],
                                                                SS[:, ck], invD))
                                    vhop(v, v.tensor_mul(T1B[:, ck], MEANB[:, ck],
                                                         MEANB[:, ck]))
                                    vhop(v, v.scalar_tensor_tensor(
                                        VARPB[:, ck], SQ[:, ck], invD, T1B[:, ck],
                                        op0=ALU.mult, op1=ALU.subtract))
                                    vhop(v, v.tensor_mul(T1B[:, ck], Lb[:, ck],
                                                         Lb[:, ck]))
                                    v.scalar_tensor_tensor(
                                        VARPB[:, ck], T1B[:, ck], LN_EPS,
                                        VARPB[:, ck], op0=ALU.mult,
                                        op1=ALU.add).then_inc(sv, 1)
                                    c["v"] += 1

        with contextlib.ExitStack() as _esB:
            _b = _esB.enter_context
            X3FM = _b(nc.sbuf_tensor("X3FM", [P, DC * OWN], BF16))
            JIOT = _b(nc.sbuf_tensor("JIOT", [P, OWN], F32))
            PRMB = _b(nc.sbuf_tensor("PRMB", [P, 2 * 512], BF16))
            STG = _b(nc.sbuf_tensor("STG", [P, 2 * D], F32))
            TRX = _b(nc.sbuf_tensor("TRX", [P, D], BF16))
            PRB = _b(nc.sbuf_tensor("PRB", [P, NE], F32))
            PRN = _b(nc.sbuf_tensor("PRN", [P, NE], F32))
            M8 = _b(nc.sbuf_tensor("M8", [P, NE], F32))
            SMB = _b(nc.sbuf_tensor("SMB", [P, 1], F32))
            NMB = _b(nc.sbuf_tensor("NMB", [P, 1], F32))
            RSB = _b(nc.sbuf_tensor("RSB", [P, 1], F32))
            psTE = _b(nc.psum_tensor("psTE", [P, 512], BF16))
            ph0 = _b(nc.psum_tensor("ph0", [P, TGW], F32))
            ph1 = _b(nc.psum_tensor("ph1", [P, TGW], F32))
            psR = _b(nc.psum_tensor("psR", [P, NE], F32))
            py0 = _b(nc.psum_tensor("py0", [P, 512], F32))
            py1 = _b(nc.psum_tensor("py1", [P, 512], F32))
            py2 = _b(nc.psum_tensor("py2", [P, 512], F32))
            py3 = _b(nc.psum_tensor("py3", [P, 512], F32))
            ph = [ph0, ph1]
            py = [py0, py1, py2, py3]
            TRX_cur[0] = TRX

            X1 = XQR
            with (nc.sbuf_tensor("X3LO", [P, DC * OWN], BF16) as X3LO,
                  nc.sbuf_tensor("X2TM", [P, OC * D], BF16) as X2TM,
                  nc.sbuf_tensor("TRXF", [P, D], F32) as TRXF,
                  nc.sbuf_tensor("TRXLO", [P, D], BF16) as TRXLO):
                with nc.Block() as blk:
                    @blk.gpsimd
                    def _(g):
                        g.wait_ge(st, PVdone[NB - 1])
                        g.iota(JIOT[:, :], [[1, OWN]], base=0,
                               channel_multiplier=0,
                               allow_small_or_imprecise_dtypes=True
                               ).then_inc(sgc, 1)
                        c["gc"] += 1

                    @blk.scalar
                    def _(s):
                        rstd_scalar(s, OC)

                    @blk.vector
                    def _(v):
                        nbias_vec(v, OC)

                with nc.Block() as blk:
                    v_base = c["v"]
                    c_base = c["c"]

                    def l3_sum(v, i):
                        sl = X1[:, i * D:(i + 1) * D]
                        v.reduce_sum(SS[:, i:i + 1], sl, axis=AX)
                        v.scalar_tensor_tensor(TRXF[:, 0:D], sl, 1.0, sl,
                                               op0=ALU.mult, op1=ALU.mult,
                                               accum_out=SQ[:, i:i + 1])

                    @blk.sync
                    def _(s):
                        for tc in range(OC):
                            s.wait_ge(sv, v_base + tc + 1)
                            sdma(s, x1d[tc * P:(tc + 1) * P, :],
                                 X1[:, tc * D:(tc + 1) * D])

                    @blk.scalar
                    def _(s):
                        s.wait_ge(sv, v_base)
                        for tc in range(OC):
                            if tc >= 2:
                                s.wait_ge(sv, v_base + tc - 1)
                            s.activation(STG[:, (tc % 2) * D:(tc % 2 + 1) * D],
                                         O_acc[:, tc * D:(tc + 1) * D], ACT.Identity,
                                         bias=NBSB[:, tc:tc + 1],
                                         scale=RSTDB[:, tc:tc + 1]
                                         ).then_inc(scm, 1)
                            c["c"] += 1

                    @blk.vector
                    def _(v):
                        for tc in range(OC):
                            v.wait_ge(scm, c_base + tc + 1)
                            v.tensor_add(X1[:, tc * D:(tc + 1) * D],
                                         XQR[:, tc * D:(tc + 1) * D],
                                         STG[:, (tc % 2) * D:(tc % 2 + 1) * D]
                                         ).then_inc(sv, 1)
                            c["v"] += 1
                            if tc >= 1:
                                l3_sum(v, tc - 1)
                        l3_sum(v, OC - 1)
                        v.wait_ge(sv, c["v"])
                        v.engine_nop().then_inc(sv, 1)
                        c["v"] += 1
                        v.wait_ge(sv, c["v"])
                        ck = slice(0, OC)
                        vhop(v, v.tensor_scalar_mul(MEANB[:, ck], SS[:, ck],
                                                    invD))
                        vhop(v, v.tensor_mul(T1B[:, ck], MEANB[:, ck],
                                             MEANB[:, ck]))
                        vhop(v, v.scalar_tensor_tensor(VARPB[:, ck], SQ[:, ck],
                                                       invD, T1B[:, ck],
                                                       op0=ALU.mult,
                                                       op1=ALU.subtract))
                        v.tensor_scalar_add(VARPB[:, ck], VARPB[:, ck],
                                            LN_EPS).then_inc(sv, 1)
                        c["v"] += 1

                with nc.Block() as blk:
                    @blk.scalar
                    def _(s):
                        rstd_scalar(s, OC)

                    @blk.vector
                    def _(v):
                        nbias_vec(v, OC)

                with nc.Block() as blk:
                    c_base = c["c"]
                    t_base = c["t"]
                    v_base = c["v"]
                    NG2 = 2 * NG

                    @blk.scalar
                    def _(s):
                        s.wait_ge(sv, v_base)
                        for tc in range(OC):
                            if tc >= 1:
                                s.wait_ge(sv, v_base + (tc - 1) * (NG2 + 1) + 1)
                            s.activation(TRXF[:, 0:D], X1[:, tc * D:(tc + 1) * D],
                                         ACT.Identity, bias=NBSB[:, tc:tc + 1],
                                         scale=RSTDB[:, tc:tc + 1]
                                         ).then_inc(scm, 1)
                            c["c"] += 1

                    @blk.vector
                    def _(v):
                        x3r = X3FM[:, :].rearrange("p (dc q) -> p dc q", q=OWN)
                        x3l = X3LO[:, :].rearrange("p (dc q) -> p dc q", q=OWN)
                        for tc in range(OC):
                            v.wait_ge(scm, c_base + tc + 1)
                            if tc >= 1:
                                v.wait_ge(st, t_base + tc * NG2)
                            v.tensor_copy(TRX[:, 0:D], TRXF[:, 0:D])
                            v.tensor_copy(X2TM[:, tc * D:(tc + 1) * D],
                                          TRXF[:, 0:D])
                            v.tensor_sub(TRXLO[:, 0:D], TRXF[:, 0:D],
                                         TRX[:, 0:D]).then_inc(sv, 1)
                            c["v"] += 1
                            for half in range(2):
                                dstr = x3r if half == 0 else x3l
                                for ng in range(NG):
                                    fi = tc * NG2 + half * NG + ng
                                    v.wait_ge(st, t_base + fi + 1)
                                    v.tensor_copy(
                                        dstr[:, ng * GD:(ng + 1) * GD,
                                             tc * P:(tc + 1) * P],
                                        psTE[:, 0:GD * P].rearrange(
                                            "p (k q) -> p k q", q=P)
                                    ).then_inc(sv, 1)
                                    c["v"] += 1

                    @blk.tensor
                    def _(t):
                        for tc in range(OC):
                            for half in range(2):
                                src = TRX if half == 0 else TRXLO
                                for ng in range(NG):
                                    fi = tc * NG2 + half * NG + ng
                                    t.wait_ge(sv, v_base + tc * (NG2 + 1) + 1
                                              + (fi - tc * NG2))
                                    for k4 in range(GD):
                                        dc = ng * GD + k4
                                        ins = t.transpose(
                                            psTE[:, k4 * P:(k4 + 1) * P],
                                            src[:, dc * P:(dc + 1) * P],
                                            IDENT[:, :])
                                    ins.then_inc(st, 1)
                                    c["t"] += 1
                with nc.Block() as blk:
                    t_base = c["t"]
                    v_base = c["v"]
                    c_base = c["c"]

                    @blk.tensor
                    def _(t):
                        for tc in range(OC):
                            if tc >= 1:
                                t.wait_ge(sv, v_base + tc * 6)
                            for dc in range(DC):
                                t.matmul(psR[:, :],
                                         X3FM[:, dc * OWN + tc * P:
                                              dc * OWN + (tc + 1) * P],
                                         WrB[:, dc * NE:(dc + 1) * NE],
                                         start=(dc == 0), stop=False)
                            for dc in range(DC):
                                t.matmul(psR[:, :],
                                         X3LO[:, dc * OWN + tc * P:
                                              dc * OWN + (tc + 1) * P],
                                         WrB[:, dc * NE:(dc + 1) * NE],
                                         start=False, stop=False)
                            for dc in range(DC):
                                t.matmul(psR[:, :],
                                         X3FM[:, dc * OWN + tc * P:
                                              dc * OWN + (tc + 1) * P],
                                         WrLO[:, dc * NE:(dc + 1) * NE],
                                         start=False, stop=False)
                            t.matmul(psR[:, :], ones1[:, :], RB[:, :],
                                     start=False, stop=False)
                            t.matmul(psR[:, :], ones1[:, :], RBLO[:, :],
                                     start=False, stop=True).then_inc(st, 1)
                            c["t"] += 1

                    @blk.vector
                    def _(v):
                        for tc in range(OC):
                            v.wait_ge(st, t_base + tc + 1)
                            v.reduce_max(NMB[:, :], psR[:, :], axis=AX,
                                         negate=True).then_inc(sv, 1)
                            c["v"] += 1
                            v.wait_ge(scm, c_base + tc + 1)
                            vhop(v, v.reciprocal(RSB[:, :], SMB[:, :]))
                            vhop(v, v.tensor_scalar_mul(PRN[:, :], PRB[:, :],
                                                        RSB[:, :]))
                            vhop(v, v.reduce_max(NMB[:, :], PRN[:, :], axis=AX))
                            vhop(v, v.tensor_single_scalar(M8[:, :], PRN[:, :],
                                                           NMB[:, :],
                                                           op=ALU.is_equal))
                            v.tensor_mul(GATES[:, tc * NE:(tc + 1) * NE],
                                         M8[:, :], PRN[:, :]).then_inc(sv, 1)
                            c["v"] += 1
                        v.memset(O_acc[:, :], 0.0)

                    @blk.scalar
                    def _(s):
                        for tc in range(OC):
                            s.wait_ge(sv, v_base + tc * 6 + 1)
                            s.activation(PRB[:, :], psR[:, :], ACT.Exp,
                                         bias=NMB[:, :],
                                         accum_out=SMB[:, :]).then_inc(scm, 1)
                            c["c"] += 1
                with nc.Block() as blk:
                    @blk.vector
                    def _(v):
                        vhop(v, v.tensor_single_scalar(IND[:, :], GATES[:, :], 0.0,
                                                       op=ALU.is_gt))
                        for tc in range(OC):
                            ins = v.reduce_sum(GSEL[:, tc:tc + 1],
                                               GATES[:, tc * NE:(tc + 1) * NE],
                                               axis=AX)
                        ins.then_inc(sv, 1)
                        c["v"] += 1
                    V1 = c["v"]

                    @blk.tensor
                    def _(t):
                        t.wait_ge(sv, V1)
                        for tc in range(OC):
                            ins = t.matmul(py1[0:NE, tc:tc + 1],
                                           IND[:, tc * NE:(tc + 1) * NE],
                                           onescol[:, :], start=(tc == 0),
                                           stop=(tc == OC - 1))
                        ins.then_inc(st, 1)
                        c["t"] += 1
                    T1 = c["t"]

                    @blk.vector
                    def _(v):
                        v.wait_ge(st, T1)
                        vhop(v, v.tensor_copy(CNT[:, :], py1[0:NE, 0:OC]))
                        vhop(v, v.tensor_tensor_scan(CUMI[:, :], ONES8R[:, :],
                                                     CNT[:, :], 0.0, op0=ALU.mult,
                                                     op1=ALU.add))
                        vhop(v, v.tensor_sub(CUMTOT[:, :], CUMI[:, :], CNT[:, :]))
                        vhop(v, v.tensor_copy(TOT[:, :], CUMI[:, OC - 1:OC]))
                    V2 = c["v"]

                    @blk.tensor
                    def _(t):
                        t.wait_ge(sv, V2)
                        t.matmul(py2[0:NE, 0:1], LTS8F[:, :], TOT[:, :],
                                 start=True, stop=True).then_inc(st, 1)
                        c["t"] += 1
                    T2 = c["t"]

                    @blk.vector
                    def _(v):
                        v.wait_ge(st, T2)
                        vhop(v, v.tensor_copy(BASEC[:, :], py2[0:NE, 0:1]))
                        vhop(v, v.tensor_scalar_add(CUMTOT[:, :], CUMTOT[:, :],
                                                    BASEC[:, :]))
                        vhop(v, v.tensor_add(ENDC[:, :], BASEC[:, :], TOT[:, :]))
                        for tg in range(NTG):
                            v.tensor_single_scalar(T81[:, :], BASEC[:, :],
                                                   float((tg + 1) * TGW + KMARG),
                                                   op=ALU.is_lt)
                            vhop(v, v.tensor_single_scalar(T82[:, :], ENDC[:, :],
                                                           float(tg * TGW - KMARG),
                                                           op=ALU.is_gt))
                            vhop(v, v.tensor_mul(FL8[:, tg:tg + 1], T81[:, :],
                                                 T82[:, :]))
                        vhop(v, v.tensor_copy(FL8I[:, :], FL8[:, :]))
                        vhop(v, v.reduce_max(T81[:, :], FL8[:, :], axis=AX))
                        vhop(v, v.tensor_copy(EFL8I[:, :], T81[:, :]))
                    V3 = c["v"]

                    @blk.sync
                    def _(sy):
                        sy.wait_ge(sv, V3)
                        sdma(sy, rowd, CUMTOT[:, :])
                        sdma(sy, flid, FL8I[:, :])
                        sdma(sy, efld, EFL8I[:, :])
                        sy.wait_ge(ssm, c["s"])
                        sy.dma_start(ROWSF[0:1, :].rearrange(
                            "o (t e) -> o t e", e=NE),
                            rowd.rearrange("e t -> t e"),
                            allow_slow_non_contiguous=True).then_inc(ssm, 16)
                        c["s"] += 16
                        sdma(sy, FLROW[0:1, :].rearrange("o (e t) -> o e t",
                                                         t=NTG), flid)
                        sdma(sy, EFROW[0:1, :].rearrange("o (e t) -> o e t", t=1),
                             efld)
                    S1 = c["s"]

                    @blk.vector
                    def _(v):
                        v.wait_ge(ssm, S1)
                        vhop(v, v.tensor_copy(ROWSH[:, :], ROWSF[:, :]))
                        vhop(v, v.tensor_sub(ROWSL[:, :], ROWSF[:, :],
                                             ROWSH[:, :]))
                    V4 = c["v"]
                    t_base3 = c["t"]
                    v_base4 = c["v"]

                    @blk.tensor
                    def _(t):
                        t.wait_ge(sv, V4)
                        for tc in range(OC):
                            if tc >= 2:
                                t.wait_ge(sv, v_base4 + tc - 1)
                            pb = [py0, py2][tc % 2]
                            t.matmul(pb[0:P, 0:NE], LTS[:, :],
                                     IND[:, tc * NE:(tc + 1) * NE],
                                     start=True, stop=False)
                            t.matmul(pb[0:P, 0:NE], ones1[:, :],
                                     ROWSH[0:1, tc * NE:(tc + 1) * NE],
                                     start=False, stop=False)
                            t.matmul(pb[0:P, 0:NE], ones1[:, :],
                                     ROWSL[0:1, tc * NE:(tc + 1) * NE],
                                     start=False, stop=True).then_inc(st, 1)
                            c["t"] += 1

                    @blk.vector
                    def _(v):
                        for tc in range(OC):
                            v.wait_ge(st, t_base3 + tc + 1)
                            pb = [py0, py2][tc % 2]
                            v.scalar_tensor_tensor(
                                T8[:, :], pb[0:P, 0:NE], 1.0,
                                IND[:, tc * NE:(tc + 1) * NE], op0=ALU.mult,
                                op1=ALU.mult,
                                accum_out=DESTF[:, tc:tc + 1]).then_inc(sv, 1)
                            c["v"] += 1
                        v.wait_ge(sv, c["v"])
                        v.engine_nop().then_inc(sv, 1)
                        c["v"] += 1

                with nc.Block() as blk:
                    t0 = c["t"]

                    @blk.tensor
                    def _(t):
                        t.matmul(psR[:, 0:NE], ones1[:, :], ROWSH[0:1, 0:NE],
                                 start=True, stop=False)
                        t.matmul(psR[:, 0:NE], ones1[:, :], ROWSL[0:1, 0:NE],
                                 start=False, stop=True).then_inc(st, 1)
                        c["t"] += 1
                    T1p = c["t"]

                    @blk.vector
                    def _(v):
                        v.wait_ge(st, T1p)
                        vhop(v, v.tensor_copy(BASEB[:, :], psR[0:P, 0:NE]))
                        vhop(v, v.tensor_single_scalar(ESRT[:, :], TOKC[:, :],
                                                       BASEB[:, 0:1],
                                                       op=ALU.is_ge))
                        for e2 in range(1, NE):
                            vhop(v, v.tensor_single_scalar(T8B[:, :], TOKC[:, :],
                                                           BASEB[:, e2:e2 + 1],
                                                           op=ALU.is_ge))
                            vhop(v, v.tensor_add(ESRT[:, :], ESRT[:, :],
                                                 T8B[:, :]))
                        v.tensor_scalar_add(ESRT[:, :], ESRT[:, :],
                                            -1.0).then_inc(sv, 1)
                        c["v"] += 1

                with nc.Block() as blk:
                    NST = OWN // 512
                    NSUB = 512 // P
                    DCH = DC // 2
                    t_b = c["t"]
                    v_b = c["v"]
                    c_b = c["c"]

                    VPP = OC + NSUB // 2

                    @blk.vector
                    def _(v):
                        v.wait_ge(sgc, 2)
                        k = 0
                        for stile in range(NST):
                            for half in range(2):
                                p = stile * 2 + half
                                for tc in range(OC):
                                    if k >= 2:
                                        v.wait_ge(st, t_b + k - 1)
                                    v.tensor_single_scalar(
                                        PRMB[:, (k % 2) * 512:(k % 2) * 512 + 512],
                                        JIOT[:, stile * 512:(stile + 1) * 512],
                                        DESTF[:, tc:tc + 1],
                                        op=ALU.is_equal).then_inc(sv, 1)
                                    c["v"] += 1
                                    k += 1
                                v.wait_ge(st, t_b + (p + 1) * OC)
                                for sb in range(NSUB // 2):
                                    sub = half * (NSUB // 2) + sb
                                    v.tensor_copy(
                                        INVF[:, stile * NSUB + sub:
                                             stile * NSUB + sub + 1],
                                        ph[sb][0:P, 0:1]).then_inc(sv, 1)
                                    c["v"] += 1

                    @blk.tensor
                    def _(t):
                        k = 0
                        for stile in range(NST):
                            for half in range(2):
                                p = stile * 2 + half
                                for tc in range(OC):
                                    t.wait_ge(sv, v_b + p * VPP + tc + 1)
                                    if tc == 0:
                                        if p > 0:
                                            t.wait_ge(scm, c_b + p * DCH)
                                            t.wait_ge(sv, v_b + p * VPP)
                                    sl = PRMB[:, (k % 2) * 512:
                                              (k % 2) * 512 + 512]
                                    for dcq in range(DCH):
                                        dc = half * DCH + dcq
                                        t.matmul(py[dcq % 4][0:P, 0:512],
                                                 X2TM[:, tc * D + dc * P:
                                                      tc * D + (dc + 1) * P],
                                                 sl, start=(tc == 0),
                                                 stop=(tc == OC - 1))
                                    for sb in range(NSUB // 2):
                                        sub = half * (NSUB // 2) + sb
                                        slc = PRMB[:, (k % 2) * 512 + sub * P:
                                                   (k % 2) * 512 + (sub + 1) * P]
                                        t.matmul(ph[sb][0:P, 0:1], slc,
                                                 TOKCH[:, tc:tc + 1],
                                                 start=(tc == 0), stop=False)
                                        ins = t.matmul(ph[sb][0:P, 0:1], slc,
                                                       TOKCL[:, tc:tc + 1],
                                                       start=False,
                                                       stop=(tc == OC - 1))
                                    ins.then_inc(st, 1)
                                    c["t"] += 1
                                    k += 1

                    @blk.scalar
                    def _(s):
                        for stile in range(NST):
                            for half in range(2):
                                p = stile * 2 + half
                                s.wait_ge(st, t_b + (p + 1) * OC)
                                for dcq in range(DCH):
                                    dc = half * DCH + dcq
                                    s.activation(
                                        X3FM[:, dc * OWN + stile * 512:
                                             dc * OWN + (stile + 1) * 512],
                                        py[dcq % 4][0:P, 0:512],
                                        ACT.Copy).then_inc(scm, 1)
                                    c["c"] += 1

            with contextlib.ExitStack() as _esW:
                _w = _esW.enter_context
                W1B0 = _w(nc.sbuf_tensor("W1B0", [P, DC * FSUB], BF16))
                W1B1 = _w(nc.sbuf_tensor("W1B1", [P, DC * FSUB], BF16))
                W2B0 = _w(nc.sbuf_tensor("W2B0", [P, FCS * D], BF16))
                W2B1 = _w(nc.sbuf_tensor("W2B1", [P, FCS * D], BF16))
                HB = _w(nc.sbuf_tensor("HB", [P, FCS * TGW], BF16))
                B2B = _w(nc.sbuf_tensor("B2B", [1, D], BF16))
                X1RD = _w(nc.sbuf_tensor("X1RD", [P, 2 * D], BF16))
                W1B = [W1B0, W1B1]
                W2B = [W2B0, W2B1]

                with nc.Block() as blk:
                    @blk.gpsimd
                    def _(g):
                        def _w0load(gg):
                            gg.dma_start(W1B[0][:, :].rearrange(
                                "p (dc f) -> p dc f", f=FSUB),
                                w1[0:1, :, 0:FSUB].rearrange(
                                    "o (dc p) f -> p (o dc) f", p=P)
                            ).then_inc(sg, 16)
                            gg.dma_start(W2B[0][:, :].rearrange(
                                "p (fc d) -> p fc d", d=D),
                                w2[0:1, 0:FSUB, :].rearrange(
                                    "o (fc p) d -> p (o fc) d", p=P)
                            ).then_inc(sg, 16)
                        if USE_BRANCHES and BR_G:
                            with g.register("rge") as rge:
                                g.reg_load(rge, EFROW[0:1, 0:1])
                                with g.If_ne(rge, 0):
                                    _w0load(g)
                                with g.If_eq(rge, 0):
                                    g.dma_start(DUMB[:, :],
                                                b2[0:1, 0:8]).then_inc(sg, 16)
                                    g.dma_start(DUMB[:, :],
                                                b2[0:1, 0:8]).then_inc(sg, 16)
                        else:
                            _w0load(g)
                        c["g"] += 32
                        gdma(g, B2B[:, :], b2[0:1, :])


                UPT = NTG * (FCS + 1)
                for e in range(NE):
                    with nc.Block() as blk:
                        g_w0 = c["g"]
                        t_e0 = c["t"]
                        c_e0 = c["c"]
                        v_e0 = c["v"]
                        gw = [g_w0]

                        @blk.gpsimd
                        def _(g):
                            for sci in range(SC):
                                i = e * SC + sci
                                if sci >= 1:
                                    g.wait_ge(st, t_e0 + sci * UPT)
                                if i + 1 < NE * SC:
                                    e2, sc2 = divmod(i + 1, SC)
                                    b2f = (i + 1) % 2
                                    _ctx = (g.register(f"rgw{i}")
                                            if USE_BRANCHES and BR_G else None)
                                    if USE_BRANCHES and BR_G:
                                        rgw = _ctx.__enter__()
                                        g.reg_load(rgw, EFROW[0:1, e2:e2 + 1])
                                        _ifw = g.If_ne(rgw, 0)
                                        _ifw.__enter__()
                                    g.dma_start(
                                        W1B[b2f][:, :].rearrange(
                                            "p (dc f) -> p dc f", f=FSUB),
                                        w1[e2:e2 + 1, :,
                                           sc2 * FSUB:(sc2 + 1) * FSUB]
                                        .rearrange("o (dc p) f -> p (o dc) f",
                                                   p=P)).then_inc(sg, 16)
                                    g.dma_start(
                                        W2B[b2f][:, :].rearrange(
                                            "p (fc d) -> p fc d", d=D),
                                        w2[e2:e2 + 1,
                                           sc2 * FSUB:(sc2 + 1) * FSUB, :]
                                        .rearrange("o (fc p) d -> p (o fc) d",
                                                   p=P)).then_inc(sg, 16)
                                    if USE_BRANCHES and BR_G:
                                        _ifw.__exit__(None, None, None)
                                        with g.If_eq(rgw, 0):
                                            g.dma_start(DUMB[:, :], b2[0:1, 0:8]
                                                        ).then_inc(sg, 16)
                                            g.dma_start(DUMB[:, :], b2[0:1, 0:8]
                                                        ).then_inc(sg, 16)
                                        _ctx.__exit__(None, None, None)
                                    c["g"] += 32
                                if sci == min(1, SC - 1) and e + 1 < NE:
                                    gdma(g, B2B[:, :], b2[e + 1:e + 2, :])
                                gw.append(c["g"])

                        @blk.tensor
                        def _(t):
                            t.wait_ge(sg, g_w0)
                            if USE_BRANCHES and BR_T:
                                efv = nc.values_load(
                                    EFROW[0:1, e:e + 1],
                                    engines=[mybir.EngineType.PE])
                                arm_iter = t.Switch(efv, 2)
                                _rgt = t.register(f"rtf{e}")
                                rtf = _rgt.__enter__()
                            else:
                                arm_iter = iter([1])
                            for arm in arm_iter:
                                if arm == 0:
                                    t.sem_inc(st, SC * UPT)
                                    continue
                                for sci in range(SC):
                                    i = e * SC + sci
                                    buf = i % 2
                                    c_base = c_e0 + sci * NTG * FCS
                                    v_base = v_e0 + sci * NTG * (TGC + 1)
                                    if sci >= 1:
                                        t.wait_ge(sg, gw[sci])
                                    for tg in range(NTG):
                                        if USE_BRANCHES and BR_T:
                                            t.reg_load(rtf,
                                                       FLROW[0:1, e * NTG + tg:
                                                             e * NTG + tg + 1])
                                            t.drain()
                                            _if = t.If_ne(rtf, 0)
                                            _if.__enter__()
                                        for fc in range(FCS):
                                            k = tg * FCS + fc
                                            if c_base + k - 1 > 0:
                                                t.wait_ge(scm, c_base + k - 1)
                                            dst = ph[k % 2][:, :]
                                            for dci in range(DC):
                                                ins = t.matmul(
                                                    dst,
                                                    W1B[buf][:,
                                                             dci * FSUB + fc * P:
                                                             dci * FSUB
                                                             + (fc + 1) * P],
                                                    X3FM[:,
                                                         dci * OWN + tg * TGW:
                                                         dci * OWN
                                                         + (tg + 1) * TGW],
                                                    start=(dci == 0),
                                                    stop=(dci == DC - 1))
                                            ins.then_inc(st, 1)
                                            c["t"] += 1
                                        t.wait_ge(scm, c_base + (tg + 1) * FCS)
                                        if sci > 0 or tg >= 1:
                                            t.wait_ge(sv,
                                                      v_base + tg * (TGC + 1))
                                        if sci == 0:
                                            for tc in range(TGC):
                                                for dt in range(ND):
                                                    t.matmul(
                                                        py[tc * ND + dt][0:P,
                                                                         0:DTW],
                                                        ones1[:, :],
                                                        B2B[:, dt * DTW:
                                                            (dt + 1) * DTW],
                                                        start=True, stop=False)
                                        for fc in range(FCS):
                                            for tc in range(TGC):
                                                lh = HB[:, fc * TGW + tc * P:
                                                        fc * TGW + (tc + 1) * P]
                                                for dt in range(ND):
                                                    ins = t.matmul(
                                                        py[tc * ND + dt][0:P,
                                                                         0:DTW],
                                                        lh,
                                                        W2B[buf][:,
                                                                 fc * D
                                                                 + dt * DTW:
                                                                 fc * D
                                                                 + (dt + 1)
                                                                 * DTW],
                                                        start=(sci > 0
                                                               and fc == 0),
                                                        stop=(fc == FCS - 1))
                                        ins.then_inc(st, 1)
                                        c["t"] += 1
                                        if USE_BRANCHES and BR_T:
                                            _if.__exit__(None, None, None)
                                            with t.If_eq(rtf, 0):
                                                t.sem_inc(st, FCS + 1)
                            if USE_BRANCHES and BR_T:
                                _rgt.__exit__(None, None, None)

                        @blk.scalar
                        def _(s):
                            if USE_BRANCHES and BR_C:
                                _rgc = s.register(f"rcf{e}")
                                rcf = _rgc.__enter__()
                            for sci in range(SC):
                                t_base = t_e0 + sci * UPT
                                for tg in range(NTG):
                                    if USE_BRANCHES and BR_C:
                                        s.reg_load(rcf,
                                                   FLROW[0:1, e * NTG + tg:
                                                         e * NTG + tg + 1])
                                        _if = s.If_ne(rcf, 0)
                                        _if.__enter__()
                                    for fc in range(FCS):
                                        k = tg * FCS + fc
                                        s.wait_ge(st, t_base + tg + k + 1)
                                        bc = e * FCT + sci * FCS + fc
                                        s.activation(
                                            HB[:, fc * TGW:(fc + 1) * TGW],
                                            ph[k % 2][:, :], ACT.Relu,
                                            bias=B1all[:, bc:bc + 1]
                                        ).then_inc(scm, 1)
                                        c["c"] += 1
                                    if USE_BRANCHES and BR_C:
                                        _if.__exit__(None, None, None)
                                        with s.If_eq(rcf, 0):
                                            s.sem_inc(scm, FCS)
                            if USE_BRANCHES and BR_C:
                                _rgc.__exit__(None, None, None)

                        @blk.vector
                        def _(v):
                            if USE_BRANCHES and BR_V:
                                _rgv = v.register(f"rvf{e}")
                                rvf = _rgv.__enter__()
                            for sci in range(SC):
                                t_base = t_e0 + sci * UPT
                                for tg in range(NTG):
                                    if USE_BRANCHES and BR_V:
                                        v.reg_load(rvf,
                                                   FLROW[0:1, e * NTG + tg:
                                                         e * NTG + tg + 1])
                                        _if = v.If_ne(rvf, 0)
                                        _if.__enter__()
                                    v.wait_ge(st,
                                              t_base + (tg + 1) * (FCS + 1))
                                    for tc in range(TGC):
                                        tcg = tg * TGC + tc
                                        vhop(v, v.tensor_single_scalar(
                                            SCL[:, tc:tc + 1],
                                            ESRT[:, tcg:tcg + 1], float(e),
                                            op=ALU.is_equal))
                                        for dt in range(ND):
                                            sl = O_acc[:, tcg * D + dt * DTW:
                                                       tcg * D + (dt + 1) * DTW]
                                            ins = v.scalar_tensor_tensor(
                                                sl, py[tc * ND + dt][0:P, 0:DTW],
                                                SCL[:, tc:tc + 1],
                                                sl, op0=ALU.mult, op1=ALU.add)
                                    ins.then_inc(sv, 1)
                                    c["v"] += 1
                                    if USE_BRANCHES and BR_V:
                                        _if.__exit__(None, None, None)
                                        with v.If_eq(rvf, 0):
                                            v.sem_inc(sv, TGC + 1)
                            if USE_BRANCHES and BR_V:
                                _rgv.__exit__(None, None, None)


                with nc.Block() as blk:
                    NTT = OWN // 512
                    SCC = OWN // P
                    t_b2 = c["t"]
                    v_b2 = c["v"]
                    c_b2 = c["c"]

                    @blk.vector
                    def _(v):
                        k = 0
                        for tt in range(NTT):
                            for dt in range(ND):
                                for sc in range(SCC):
                                    if k >= 2:
                                        v.wait_ge(st, t_b2 + k - 1)
                                    v.tensor_single_scalar(
                                        PRMB[:, (k % 2) * 512:
                                             (k % 2) * 512 + 512],
                                        JIOT[:, tt * 512:(tt + 1) * 512],
                                        INVF[:, sc:sc + 1],
                                        op=ALU.is_equal).then_inc(sv, 1)
                                    c["v"] += 1
                                    k += 1

                    @blk.tensor
                    def _(t):
                        k = 0
                        for tt in range(NTT):
                            for dt in range(ND):
                                p = tt * ND + dt
                                for sc in range(SCC):
                                    t.wait_ge(sv, v_b2 + k + 1)
                                    if sc == 0 and p >= 1:
                                        t.wait_ge(scm, c_b2 + p * 4)
                                    sl = PRMB[:, (k % 2) * 512:
                                              (k % 2) * 512 + 512]
                                    for tq in range(4):
                                        ins = t.matmul(
                                            py[tq][0:P, 0:DTW],
                                            sl[:, tq * P:(tq + 1) * P],
                                            O_acc[:, sc * D + dt * DTW:
                                                  sc * D + dt * DTW + DTW],
                                            start=(sc == 0),
                                            stop=(sc == SCC - 1))
                                    ins.then_inc(st, 1)
                                    c["t"] += 1
                                    k += 1

                    @blk.scalar
                    def _(s):
                        for tt in range(NTT):
                            for dt in range(ND):
                                p = tt * ND + dt
                                s.wait_ge(st, t_b2 + (p + 1) * SCC)
                                for tq in range(4):
                                    tcg = tt * 4 + tq
                                    s.activation(
                                        X3FM[:, tcg * D + dt * DTW:
                                             tcg * D + dt * DTW + DTW],
                                        py[tq][0:P, 0:DTW], ACT.Identity,
                                        scale=GSEL[:, tcg:tcg + 1]
                                    ).then_inc(scm, 1)
                                    c["c"] += 1

                with nc.Block() as blk:
                    @blk.vector
                    def _(v):
                        batch_stats(v, lambda i2: X3FM[:, i2 * D:(i2 + 1) * D], OC)

                    @blk.scalar
                    def _(s):
                        rstd_scalar(s, OC)

                    @blk.vector
                    def _(v):
                        nbias_vec(v, OC)

                with nc.Block() as blk:
                    c_base = c["c"]
                    v_base = c["v"]
                    s_base = c["s"]

                    @blk.sync
                    def _(s):
                        for tc in range(OC):
                            if tc >= 2:
                                s.wait_ge(sv, v_base + tc - 1)
                            sdma(s, X1RD[:, (tc % 2) * D:(tc % 2 + 1) * D]
                                 .rearrange("p (n d) -> p n d", d=D),
                                 x1d[tc * P:(tc + 1) * P, :].rearrange(
                                     "(n p) d -> p n d", p=P))
                            s.wait_ge(sv, v_base + tc)
                            if tc >= 1:
                                sdma(s, yo[(tc - 1) * P:tc * P, :],
                                     STG[:, ((tc - 1) % 2) * D:
                                         ((tc - 1) % 2 + 1) * D])
                        s.wait_ge(sv, v_base + OC)
                        sdma(s, yo[(OC - 1) * P:OC * P, :],
                             STG[:, ((OC - 1) % 2) * D:((OC - 1) % 2 + 1) * D])
                        s.wait_ge(ssm, c["s"])

                    @blk.scalar
                    def _(s):
                        for tc in range(OC):
                            if tc >= 2:
                                s.wait_ge(sv, v_base + tc - 1)
                                s.wait_ge(ssm, s_base + (2 * tc - 1) * 16)
                            s.activation(STG[:, (tc % 2) * D:(tc % 2 + 1) * D],
                                         X3FM[:, tc * D:(tc + 1) * D], ACT.Identity,
                                         bias=NBSB[:, tc:tc + 1],
                                         scale=RSTDB[:, tc:tc + 1]
                                         ).then_inc(scm, 1)
                            c["c"] += 1

                    @blk.vector
                    def _(v):
                        for tc in range(OC):
                            v.wait_ge(scm, c_base + tc + 1)
                            v.wait_ge(ssm, s_base + (2 * tc + 1) * 16)
                            sl = STG[:, (tc % 2) * D:(tc % 2 + 1) * D]
                            v.tensor_add(sl, sl,
                                         X1RD[:, (tc % 2) * D:(tc % 2 + 1) * D]
                                         ).then_inc(sv, 1)
                            c["v"] += 1

                    @blk.gpsimd
                    def _(g):
                        g.wait_ge(sg, c["g"])

    nc.finalize()
    return nc


_MASK_CACHE = {}


def _make_mask(qstart, T, OWN):
    key = (qstart, T, OWN)
    if key not in _MASK_CACHE:
        SMAX = T - P
        MW = SMAX + OWN
        u = np.arange(MW)[None, :]
        pr = np.arange(P)[:, None]
        _MASK_CACHE[key] = np.ascontiguousarray(
            (u >= pr + SMAX - qstart).astype(ml_dtypes.bfloat16))
    return _MASK_CACHE[key]


def kernel(L0_attn_x, L0_attn_Wq, L0_attn_Wk, L0_attn_Wv, L0_ffn_router_W,
           L0_ffn_router_b, L0_ffn_W1, L0_ffn_b1, L0_ffn_W2, L0_ffn_b2):
    x = np.asarray(L0_attn_x, dtype=np.float32)
    B, T, D = x.shape
    wq = np.ascontiguousarray(np.asarray(L0_attn_Wq, dtype=np.float32))
    wk = np.ascontiguousarray(np.asarray(L0_attn_Wk, dtype=np.float32))
    wv = np.ascontiguousarray(np.asarray(L0_attn_Wv, dtype=np.float32))
    rw = np.ascontiguousarray(np.asarray(L0_ffn_router_W, dtype=np.float32))
    rb = np.ascontiguousarray(np.asarray(L0_ffn_router_b,
                                         dtype=np.float32)).reshape(1, -1)
    w1 = np.ascontiguousarray(np.asarray(L0_ffn_W1, dtype=np.float32))
    b1 = np.asarray(L0_ffn_b1, dtype=np.float32)
    ne_, f_ = b1.shape
    b1 = np.ascontiguousarray(
        b1.reshape(ne_, f_ // 128, 128).transpose(2, 0, 1).reshape(128, -1))
    w2 = np.ascontiguousarray(np.asarray(L0_ffn_W2, dtype=np.float32))
    b2 = np.ascontiguousarray(np.asarray(L0_ffn_b2, dtype=np.float32))
    NE, _, F = w1.shape
    OWN = (B * T) // 8

    nc = build_program(D=D, F=F, T=T, OWN=OWN, NE=NE)

    in_maps = []
    for core in range(8):
        b, a = divmod(core, 4)
        qs = a * OWN
        in_maps.append({
            "xb": np.ascontiguousarray(x[b]),
            "xq": np.ascontiguousarray(x[b, qs:qs + OWN]),
            "mk": _make_mask(qs, T, OWN),
            "wq": wq, "wk": wk, "wv": wv, "rw": rw, "rb": rb,
            "w1": w1, "b1": b1, "w2": w2, "b2": b2,
        })
    res = run_bass_kernel_spmd(nc, in_maps, core_ids=list(range(8)))
    out = np.empty((B, T, D), dtype=np.float32)
    for core in range(8):
        b, a = divmod(core, 4)
        qs = a * OWN
        out[b, qs:qs + OWN] = np.asarray(res.results[core]["y"],
                                         dtype=np.float32).reshape(OWN, D)
    return out


# version 17
# speedup vs baseline: 1.0673x; 1.0673x over previous
import contextlib
import numpy as np
import ml_dtypes
import concourse.bass as bass
from concourse.bass import IndirectOffsetOnAxis
import concourse.mybir as mybir
from concourse.bass_utils import run_bass_kernel_spmd

F32 = mybir.dt.float32
BF16 = mybir.dt.bfloat16
I32 = mybir.dt.int32
AX = mybir.AxisListType.X
ALU = mybir.AluOpType
ACT = mybir.ActivationFunctionType

LN_EPS = 1e-5
SHIFT = 12.0
P = 128
USE_BRANCHES = True
KMARG = 64
BR_G = True
BR_T = True
BR_C = True
BR_V = True


def build_program(D=1024, F=4096, T=8192, BT=1024, OWN=2048, TGW=256, FSUB=1024,
                  NE=8):
    DC = D // P
    KN = BT // P
    NB = T // BT
    OC = OWN // P
    OWNB = OWN // BT
    QTW = min(512, BT)
    NQT = BT // QTW
    DTW = min(512, D)
    ND = D // DTW
    FCS = FSUB // P
    SC = F // FSUB
    FCT = F // P
    NTG = OWN // TGW
    TGC = TGW // P
    G4 = min(4, KN)
    GD = min(4, DC)
    NG = DC // GD
    SMAX = T - P
    MW = SMAX + OWN
    WS = OWN + (KN - 1) * P
    invD = 1.0 / D
    att_scale = 1.0 / float(np.sqrt(D))
    assert KN * BT >= DC * D

    nc = bass.Bass("TRN2", target_bir_lowering=False, detect_race_conditions=False)

    xb = nc.dram_tensor("xb", [T, D], F32, kind="ExternalInput").ap()
    xq = nc.dram_tensor("xq", [OWN, D], F32, kind="ExternalInput").ap()
    mk = nc.dram_tensor("mk", [P, MW], BF16, kind="ExternalInput").ap()
    wq = nc.dram_tensor("wq", [D, D], F32, kind="ExternalInput").ap()
    wk = nc.dram_tensor("wk", [D, D], F32, kind="ExternalInput").ap()
    wv = nc.dram_tensor("wv", [D, D], F32, kind="ExternalInput").ap()
    rw = nc.dram_tensor("rw", [D, NE], F32, kind="ExternalInput").ap()
    rb = nc.dram_tensor("rb", [1, NE], F32, kind="ExternalInput").ap()
    w1 = nc.dram_tensor("w1", [NE, D, F], F32, kind="ExternalInput").ap()
    b1 = nc.dram_tensor("b1", [P, NE * FCT], F32, kind="ExternalInput").ap()
    w2 = nc.dram_tensor("w2", [NE, F, D], F32, kind="ExternalInput").ap()
    b2 = nc.dram_tensor("b2", [NE, D], F32, kind="ExternalInput").ap()
    yo = nc.dram_tensor("y", [OWN, D], F32, kind="ExternalOutput").ap()
    x1d = nc.dram_tensor("x1d", [OWN, D], BF16).ap()
    rowd = nc.dram_tensor("rowd", [NE, OC], F32).ap()
    flid = nc.dram_tensor("flid", [NE, NTG], I32).ap()
    efld = nc.dram_tensor("efld", [NE, 1], I32).ap()

    c = {"g": 0, "s": 0, "t": 0, "v": 0, "c": 0, "gc": 0}

    with contextlib.ExitStack() as _es0:
        _e = _es0.enter_context
        O_acc = _e(nc.sbuf_tensor("O_acc", [P, OC * D], BF16))
        XQR = _e(nc.sbuf_tensor("XQR", [P, OC * D], BF16))
        Lb = _e(nc.sbuf_tensor("Lb", [P, OC], F32))
        SS = _e(nc.sbuf_tensor("SS", [P, max(KN, OC)], F32))
        SQ = _e(nc.sbuf_tensor("SQ", [P, max(KN, OC)], F32))
        MEANB = _e(nc.sbuf_tensor("MEANB", [P, max(KN, OC)], F32))
        T1B = _e(nc.sbuf_tensor("T1B", [P, max(KN, OC)], F32))
        VARPB = _e(nc.sbuf_tensor("VARPB", [P, max(KN, OC)], F32))
        RSTDB = _e(nc.sbuf_tensor("RSTDB", [P, max(KN, OC)], F32))
        NBSB = _e(nc.sbuf_tensor("NBSB", [P, max(KN, OC)], F32))
        IDENT = _e(nc.sbuf_tensor("IDENT", [P, P], BF16))
        ones1 = _e(nc.sbuf_tensor("ones1", [1, P], BF16))
        onescol = _e(nc.sbuf_tensor("onescol", [P, 1], BF16))
        GATES = _e(nc.sbuf_tensor("GATES", [P, OC * NE], F32))
        B1all = _e(nc.sbuf_tensor("B1all", [P, NE * FCT], F32))
        WrB = _e(nc.sbuf_tensor("WrB", [P, DC * NE], BF16))
        WrLO = _e(nc.sbuf_tensor("WrLO", [P, DC * NE], BF16))
        WrF = _e(nc.sbuf_tensor("WrF", [P, DC * NE], F32))
        RB = _e(nc.sbuf_tensor("RB", [1, NE], BF16))
        RBLO = _e(nc.sbuf_tensor("RBLO", [1, NE], BF16))
        RBF = _e(nc.sbuf_tensor("RBF", [1, NE], F32))
        NSH = _e(nc.sbuf_tensor("NSH", [P, 1], F32))
        LTS = _e(nc.sbuf_tensor("LTS", [P, P], BF16))
        LTS8F = _e(nc.sbuf_tensor("LTS8F", [NE, NE], F32))
        ERMP = _e(nc.sbuf_tensor("ERMP", [P, NE], F32))
        TOKC = _e(nc.sbuf_tensor("TOKC", [P, OC], F32))
        ONES8R = _e(nc.sbuf_tensor("ONES8R", [NE, OC], F32))
        IND = _e(nc.sbuf_tensor("IND", [P, OC * NE], BF16))
        TOKCH = _e(nc.sbuf_tensor("TOKCH", [P, OC], BF16))
        TOKCL = _e(nc.sbuf_tensor("TOKCL", [P, OC], BF16))
        INVF = _e(nc.sbuf_tensor("INVF", [P, OC], F32))
        BASEB = _e(nc.sbuf_tensor("BASEB", [P, NE], F32))
        CNT = _e(nc.sbuf_tensor("CNT", [NE, OC], F32))
        CUMI = _e(nc.sbuf_tensor("CUMI", [NE, OC], F32))
        CUMTOT = _e(nc.sbuf_tensor("CUMTOT", [NE, OC], F32))
        TOT = _e(nc.sbuf_tensor("TOT", [NE, 1], F32))
        BASEC = _e(nc.sbuf_tensor("BASEC", [NE, 1], F32))
        ENDC = _e(nc.sbuf_tensor("ENDC", [NE, 1], F32))
        FL8 = _e(nc.sbuf_tensor("FL8", [NE, NTG], F32))
        FL8I = _e(nc.sbuf_tensor("FL8I", [NE, NTG], I32))
        EFL8I = _e(nc.sbuf_tensor("EFL8I", [NE, 1], I32))
        T81 = _e(nc.sbuf_tensor("T81", [NE, 1], F32))
        T82 = _e(nc.sbuf_tensor("T82", [NE, 1], F32))
        FLROW = _e(nc.sbuf_tensor("FLROW", [1, NE * NTG], I32))
        EFROW = _e(nc.sbuf_tensor("EFROW", [1, NE], I32))
        ROWSF = _e(nc.sbuf_tensor("ROWSF", [1, OC * NE], F32))
        ROWSH = _e(nc.sbuf_tensor("ROWSH", [1, OC * NE], BF16))
        ROWSL = _e(nc.sbuf_tensor("ROWSL", [1, OC * NE], BF16))
        T8 = _e(nc.sbuf_tensor("T8", [P, NE], F32))
        T8B = _e(nc.sbuf_tensor("T8B", [P, OC], F32))
        GSEL = _e(nc.sbuf_tensor("GSEL", [P, OC], F32))
        DESTF = _e(nc.sbuf_tensor("DESTF", [P, OC], F32))
        ESRT = _e(nc.sbuf_tensor("ESRT", [P, OC], F32))
        SCL = _e(nc.sbuf_tensor("SCL", [P, TGC], F32))
        DUMB = _e(nc.sbuf_tensor("DUMB", [1, 8], F32))
        sg = _e(nc.semaphore("sg"))
        ssm = _e(nc.semaphore("ssm"))
        st = _e(nc.semaphore("st"))
        sv = _e(nc.semaphore("sv"))
        scm = _e(nc.semaphore("scm"))
        sgc = _e(nc.semaphore("sgc"))
        def gdma(eng, dst, src):
            eng.dma_start(dst, src).then_inc(sg, 16)
            c["g"] += 16

        def sdma(eng, dst, src):
            eng.dma_start(dst, src).then_inc(ssm, 16)
            c["s"] += 16

        def vhop(v, ins):
            ins.then_inc(sv, 1)
            c["v"] += 1
            v.wait_ge(sv, c["v"])

        def chop(s, ins):
            ins.then_inc(scm, 1)
            c["c"] += 1
            s.wait_ge(scm, c["c"])

        def batch_stats(v, src_buf, nchunk, add_eps_lsq=None):
            for i in range(nchunk):
                sl = src_buf(i)
                v.reduce_sum(SS[:, i:i + 1], sl, axis=AX)
                ins = v.scalar_tensor_tensor(TRX_cur[0][:, 0:D], sl, 1.0, sl,
                                             op0=ALU.mult, op1=ALU.mult,
                                             accum_out=SQ[:, i:i + 1])
            vhop(v, ins)
            ck = slice(0, nchunk)
            vhop(v, v.tensor_scalar_mul(MEANB[:, ck], SS[:, ck], invD))
            vhop(v, v.tensor_mul(T1B[:, ck], MEANB[:, ck], MEANB[:, ck]))
            vhop(v, v.scalar_tensor_tensor(VARPB[:, ck], SQ[:, ck], invD,
                                           T1B[:, ck], op0=ALU.mult,
                                           op1=ALU.subtract))
            if add_eps_lsq is None:
                ins = v.tensor_scalar_add(VARPB[:, ck], VARPB[:, ck], LN_EPS)
            else:
                vhop(v, v.tensor_mul(T1B[:, ck], add_eps_lsq[:, ck],
                                     add_eps_lsq[:, ck]))
                ins = v.scalar_tensor_tensor(VARPB[:, ck], T1B[:, ck], LN_EPS,
                                             VARPB[:, ck], op0=ALU.mult,
                                             op1=ALU.add)
            ins.then_inc(sv, 1)
            c["v"] += 1

        def rstd_scalar(s, nchunk):
            s.wait_ge(sv, c["v"])
            ck = slice(0, nchunk)
            chop(s, s.activation(T1B[:, ck], VARPB[:, ck], ACT.Ln))
            s.activation(RSTDB[:, ck], T1B[:, ck], ACT.Exp,
                         scale=-0.5).then_inc(scm, 1)
            c["c"] += 1

        def nbias_vec(v, nchunk):
            v.wait_ge(scm, c["c"])
            ck = slice(0, nchunk)
            v.scalar_tensor_tensor(NBSB[:, ck], MEANB[:, ck], -1.0, RSTDB[:, ck],
                                   op0=ALU.mult, op1=ALU.mult).then_inc(sv, 1)
            c["v"] += 1

        TRX_cur = [None]

        with contextlib.ExitStack() as _esA:
            _a = _esA.enter_context
            IOTF = _a(nc.sbuf_tensor("IOTF", [P, P], F32))
            Mstep = _a(nc.sbuf_tensor("Mstep", [P, WS], BF16))
            WkB = _a(nc.sbuf_tensor("WkB", [P, DC * D], BF16))
            WvB = _a(nc.sbuf_tensor("WvB", [P, DC * D], BF16))
            QFM = _a(nc.sbuf_tensor("QFM", [P, DC * OWN], BF16))
            TMP = _a(nc.sbuf_tensor("TMP", [P, KN * D], BF16))
            KBUF = _a(nc.sbuf_tensor("KBUF", [P, KN * D], BF16))
            LNFM = _a(nc.sbuf_tensor("LNFM", [P, DC * BT], BF16))
            PT = _a(nc.sbuf_tensor("PT", [P, KN * BT], BF16))
            SQT = _a(nc.sbuf_tensor("SQT", [P, D], BF16))
            psT = _a(nc.psum_tensor("psT", [P, 512], BF16))
            pp0 = _a(nc.psum_tensor("pp0", [P, 512], F32))
            pp1 = _a(nc.psum_tensor("pp1", [P, 512], F32))
            pv0 = _a(nc.psum_tensor("pv0", [P, min(D, 1024)], F32))
            pv1 = _a(nc.psum_tensor("pv1", [P, min(D, 1024)], F32))
            psL = _a(nc.psum_tensor("psL", [P, OC], F32))
            pp = [pp0, pp1]
            pv = [pv0, pv1]

            with nc.Block() as blk:
                @blk.gpsimd
                def _(g):
                    g.iota(IOTF[:, :], [[1, P]], base=0, channel_multiplier=-1,
                           allow_small_or_imprecise_dtypes=True)
                    g.iota(ERMP[:, :], [[1, NE]], base=0, channel_multiplier=0,
                           allow_small_or_imprecise_dtypes=True)
                    g.iota(TOKC[:, :], [[P, OC]], base=0, channel_multiplier=1,
                           allow_small_or_imprecise_dtypes=True).then_inc(sgc, 1)
                    c["gc"] += 1

                @blk.sync
                def _(s):
                    sdma(s, B1all[:, :], b1)
                    sdma(s, WrF[:, :].rearrange("p (dc e) -> p dc e", e=NE),
                         rw.rearrange("(dc p) e -> p dc e", p=P))
                    sdma(s, RBF[:, :], rb)
                    s.wait_ge(ssm, c["s"])
                s_init = c["s"]

                @blk.vector
                def _(v):
                    v.wait_ge(sgc, 1)
                    v.tensor_single_scalar(IDENT[:, :], IOTF[:, :], 0.0,
                                           op=ALU.is_equal)
                    vhop(v, v.tensor_single_scalar(LTS[:, :], IOTF[:, :], 0.0,
                                                   op=ALU.is_gt))
                    v.tensor_copy(LTS8F[:, :], LTS[0:NE, 0:NE])
                    v.memset(ONES8R[:, :], 1.0)
                    vhop(v, v.tensor_copy(TOKCH[:, :], TOKC[:, :]))
                    v.tensor_sub(TOKCL[:, :], TOKC[:, :], TOKCH[:, :])
                    v.memset(ones1[:, :], 1.0)
                    v.memset(onescol[:, :], 1.0)
                    v.memset(NSH[:, :], -SHIFT)
                    v.memset(O_acc[:, :], 0.0)
                    v.memset(Lb[:, :], 0.0)
                    v.wait_ge(ssm, s_init)
                    vhop(v, v.tensor_copy(WrB[:, :], WrF[:, :]))
                    v.tensor_sub(WrLO[:, :], WrF[:, :], WrB[:, :])
                    vhop(v, v.tensor_copy(RB[:, :], RBF[:, :]))
                    v.tensor_sub(RBLO[:, :], RBF[:, :], RB[:, :])

            def ln_stats_chunk_ops(g_wait, XB):
                ops = []

                def mk(kn):
                    def f(v):
                        if kn == 0:
                            v.wait_ge(sg, g_wait)
                        sl = XB[:, kn * D:(kn + 1) * D]
                        v.reduce_sum(SS[:, kn:kn + 1], sl, axis=AX)
                        v.scalar_tensor_tensor(SQT[:, 0:D], sl, 1.0, sl,
                                               op0=ALU.mult, op1=ALU.mult,
                                               accum_out=SQ[:, kn:kn + 1])
                    return f
                for kn in range(KN):
                    ops.append(mk(kn))
                return ops

            def ln_stats_tail(v):
                v.wait_ge(sv, c["v"])
                v.engine_nop().then_inc(sv, 1)
                c["v"] += 1
                v.wait_ge(sv, c["v"])
                ck = slice(0, KN)
                vhop(v, v.tensor_scalar_mul(MEANB[:, ck], SS[:, ck], invD))
                vhop(v, v.tensor_mul(T1B[:, ck], MEANB[:, ck], MEANB[:, ck]))
                vhop(v, v.scalar_tensor_tensor(VARPB[:, ck], SQ[:, ck], invD,
                                               T1B[:, ck], op0=ALU.mult,
                                               op1=ALU.subtract))
                v.tensor_scalar_add(VARPB[:, ck], VARPB[:, ck],
                                    LN_EPS).then_inc(sv, 1)
                c["v"] += 1

            def ln_stats_part(blk, g_wait, XB):
                ops = ln_stats_chunk_ops(g_wait, XB)

                @blk.vector
                def _(v):
                    for f in ops:
                        f(v)
                    ln_stats_tail(v)

            def ln_and_transpose(blk, g_wait, XB, LB, pv_guard=None,
                                 stats_done=False):
                if not stats_done:
                    ln_stats_part(blk, g_wait, XB)

                @blk.scalar
                def _(s):
                    rstd_scalar(s, KN)

                @blk.vector
                def _(v):
                    nbias_vec(v, KN)
                v_nb = c["v"]

                c_apply0 = c["c"]

                @blk.scalar
                def _(s):
                    s.wait_ge(sv, v_nb)
                    if pv_guard is not None:
                        s.wait_ge(st, pv_guard)
                    for kn in range(KN):
                        s.activation(LB[:, kn * D:(kn + 1) * D],
                                     XB[:, kn * D:(kn + 1) * D], ACT.Identity,
                                     bias=NBSB[:, kn:kn + 1],
                                     scale=RSTDB[:, kn:kn + 1]).then_inc(scm, 1)
                        c["c"] += 1

                ngrp = KN // G4
                t_base = c["t"]
                v_base = c["v"]

                @blk.tensor
                def _(t):
                    i = 0
                    for dc in range(DC):
                        for gi in range(ngrp):
                            t.wait_ge(scm, c_apply0 + (gi + 1) * G4)
                            t.wait_ge(sv, v_base + i)
                            for k4 in range(G4):
                                kn = gi * G4 + k4
                                ins = t.transpose(
                                    psT[:, k4 * P:(k4 + 1) * P],
                                    LB[:, kn * D + dc * P:kn * D + (dc + 1) * P],
                                    IDENT[:, :])
                            ins.then_inc(st, 1)
                            c["t"] += 1
                            i += 1

                @blk.vector
                def _(v):
                    i = 0
                    for dc in range(DC):
                        for gi in range(ngrp):
                            v.wait_ge(st, t_base + i + 1)
                            v.tensor_copy(
                                LNFM[:, dc * BT + gi * G4 * P:
                                     dc * BT + (gi + 1) * G4 * P],
                                psT[:, 0:G4 * P]).then_inc(sv, 1)
                            c["v"] += 1
                            i += 1

            def proj_fills(blk, fills, dsts, sg_wait=None):
                t_base = c["t"]
                c_base = c["c"]
                v_pre = c["v"]

                @blk.tensor
                def _(t):
                    if sg_wait is not None:
                        t.wait_ge(sg, sg_wait)
                    t.wait_ge(sv, v_pre)
                    for i, (mms, w) in enumerate(fills):
                        if c_base + i - 1 > 0:
                            t.wait_ge(scm, c_base + i - 1)
                        dst = pp[i % 2][:, 0:w]
                        for k, (lh, rh) in enumerate(mms):
                            ins = t.matmul(dst, lh, rh, start=(k == 0),
                                           stop=(k == len(mms) - 1))
                        ins.then_inc(st, 1)
                        c["t"] += 1

                @blk.scalar
                def _(s):
                    for i, (mms, w) in enumerate(fills):
                        s.wait_ge(st, t_base + i + 1)
                        s.activation(dsts[i], pp[i % 2][:, 0:w],
                                     ACT.Copy).then_inc(scm, 1)
                        c["c"] += 1

            wgt = {}
            tr_prev = [0]
            for ob in range(OWNB):
                with nc.Block() as blk:
                    if ob == 0:
                        @blk.gpsimd
                        def _(g):
                            for h in range(2):
                                gdma(g, XQR[:, h * KN * D:(h + 1) * KN * D]
                                     .rearrange("p (n d) -> p n d", d=D),
                                     xq[h * BT:(h + 1) * BT, :].rearrange(
                                         "(n p) d -> p n d", p=P))
                                wgt[f"x{h}"] = c["g"]
                                if h == 0:
                                    gdma(g, PT[:, 0:DC * D].rearrange(
                                             "p (dc e) -> p dc e", e=D),
                                         wq.rearrange("(dc p) e -> p dc e",
                                                      p=P))
                                    wgt["q"] = c["g"]
                                else:
                                    gdma(g, TMP[:, :].rearrange(
                                             "p (n d) -> p n d", d=D),
                                         xb[0:BT, :].rearrange(
                                             "(n p) d -> p n d", p=P))
                                    wgt["xkv"] = c["g"]
                                    gdma(g, WkB[:, :].rearrange(
                                             "p (dc e) -> p dc e", e=D),
                                         wk.rearrange("(dc p) e -> p dc e",
                                                      p=P))
                                    wgt["k"] = c["g"]
                                    gdma(g, WvB[:, :].rearrange(
                                             "p (dc e) -> p dc e", e=D),
                                         wv.rearrange("(dc p) e -> p dc e",
                                                      p=P))
                                    wgt["v"] = c["g"]
                    ln_and_transpose(blk, wgt[f"x{ob}"],
                                     XQR[:, ob * KN * D:(ob + 1) * KN * D],
                                     KBUF,
                                     pv_guard=(tr_prev[0] if ob else None))
                    tr_prev[0] = c["t"]

                    fills, dsts = [], []
                    for dco in range(DC):
                        for qt in range(NQT):
                            fills.append((
                                [(PT[:, dci * D + dco * P:dci * D + (dco + 1) * P],
                                  LNFM[:, dci * BT + qt * QTW:
                                       dci * BT + (qt + 1) * QTW])
                                 for dci in range(DC)], QTW))
                            dsts.append(QFM[:, dco * OWN + ob * BT + qt * QTW:
                                             dco * OWN + ob * BT + (qt + 1) * QTW])
                    proj_fills(blk, fills, dsts,
                               sg_wait=(wgt["q"] if ob == 0 else None))
            tr_prev = None

            Sdone = {}
            PVdone = {}
            Mdone = {}
            gx_next = [0]
            with nc.Block() as blk:
              for j in range(NB):
                    XB = TMP if j % 2 == 0 else KBUF
                    LB = KBUF if j % 2 == 0 else TMP

                    if j >= 1:
                        @blk.gpsimd
                        def _(g):
                            g.wait_ge(st, Sdone[j - 1])
                            gdma(g, XB[:, :].rearrange("p (n d) -> p n d", d=D),
                                 xb[j * BT:(j + 1) * BT, :].rearrange(
                                     "(n p) d -> p n d", p=P))
                    g_x = c["g"] if j >= 1 else wgt["xkv"]
                    gx_next[0] = c["g"] + 16

                    @blk.sync
                    def _(s):
                        if j >= 1:
                            s.wait_ge(sv, Mdone[j - 1])
                        base = SMAX - j * BT - (KN - 1) * P
                        sdma(s, Mstep[:, :], mk[:, base:base + WS])
                    s_m = c["s"]

                    ln_and_transpose(blk, g_x, XB, LB,
                                     pv_guard=PVdone.get(j - 1),
                                     stats_done=(j >= 1))

                    fills, dsts = [], []
                    for dco in range(DC):
                        for kt in range(NQT):
                            fills.append((
                                [(WkB[:, dci * D + dco * P:dci * D + (dco + 1) * P],
                                  LNFM[:, dci * BT + kt * QTW:
                                       dci * BT + (kt + 1) * QTW])
                                 for dci in range(DC)], QTW))
                            dsts.append(LB[:, dco * BT + kt * QTW:
                                           dco * BT + (kt + 1) * QTW])
                    proj_fills(blk, fills, dsts,
                               sg_wait=(wgt["k"] if j == 0 else None))

                    fills, dsts = [], []
                    for kn in range(KN):
                        for dt in range(ND):
                            fills.append((
                                [(LNFM[:, dci * BT + kn * P:dci * BT + (kn + 1) * P],
                                  WvB[:, dci * D + dt * DTW:
                                       dci * D + (dt + 1) * DTW])
                                 for dci in range(DC)], DTW))
                            dsts.append(XB[:, kn * D + dt * DTW:
                                           kn * D + (dt + 1) * DTW])
                    proj_fills(blk, fills, dsts,
                               sg_wait=(wgt["v"] if j == 0 else None))

                    for qh in range(OWNB):
                        t_base = c["t"]
                        c_base = c["c"]
                        v_base = c["v"]

                        @blk.tensor
                        def _(t):
                            i = 0
                            for kc in range(KN):
                                for qt in range(NQT):
                                    if c_base + i - 1 > 0:
                                        t.wait_ge(scm, c_base + i - 1)
                                    dst = pp[i % 2][:, 0:QTW]
                                    for dc in range(DC):
                                        ins = t.matmul(
                                            dst,
                                            LB[:, dc * BT + kc * P:
                                               dc * BT + (kc + 1) * P],
                                            QFM[:, dc * OWN + qh * BT + qt * QTW:
                                                dc * OWN + qh * BT + (qt + 1) * QTW],
                                            start=(dc == 0), stop=(dc == DC - 1))
                                    ins.then_inc(st, 1)
                                    c["t"] += 1
                                    i += 1
                            if qh == OWNB - 1:
                                Sdone[j] = c["t"]

                        @blk.scalar
                        def _(s):
                            i = 0
                            for kc in range(KN):
                                for qt in range(NQT):
                                    s.wait_ge(st, t_base + i + 1)
                                    s.activation(
                                        PT[:, kc * BT + qt * QTW:
                                           kc * BT + (qt + 1) * QTW],
                                        pp[i % 2][:, 0:QTW], ACT.Exp,
                                        bias=NSH[:, :],
                                        scale=att_scale).then_inc(scm, 1)
                                    c["c"] += 1
                                    i += 1

                        @blk.vector
                        def _(v):
                            v.wait_ge(ssm, s_m)
                            for kc in range(KN):
                                v.wait_ge(scm, c_base + (kc + 1) * NQT)
                                moff = (KN - 1 - kc) * P + qh * BT
                                v.tensor_mul(PT[:, kc * BT:(kc + 1) * BT],
                                             PT[:, kc * BT:(kc + 1) * BT],
                                             Mstep[:, moff:moff + BT]
                                             ).then_inc(sv, 1)
                                c["v"] += 1
                            if qh == OWNB - 1:
                                Mdone[j] = c["v"]
                        pend_stats = None
                        if qh == OWNB - 1 and j + 1 < NB:
                            XB2 = TMP if (j + 1) % 2 == 0 else KBUF
                            pend_stats = ln_stats_chunk_ops(gx_next[0], XB2)
                        last_step = (j == NB - 1)

                        def o_sum(v, qcg):
                            sl = O_acc[:, qcg * D:(qcg + 1) * D]
                            v.reduce_sum(SS[:, qcg:qcg + 1], sl, axis=AX)
                            v.scalar_tensor_tensor(SQT[:, 0:D], sl, 1.0, sl,
                                                   op0=ALU.mult, op1=ALU.mult,
                                                   accum_out=SQ[:, qcg:qcg + 1])
                        v_mask = c["v"]

                        t_base2 = c["t"]
                        v_base2 = c["v"]

                        @blk.tensor
                        def _(t):
                            t.wait_ge(sv, v_mask)
                            for qc in range(KN):
                                qcg = qh * KN + qc
                                if qc >= 2:
                                    t.wait_ge(sv, v_base2 + 2 * qc - 2)
                                if qc >= 1:
                                    t.wait_ge(sv, v_base2 + 2 * qc - 1)
                                for kc in range(KN):
                                    lh = PT[:, kc * BT + qc * P:
                                            kc * BT + (qc + 1) * P]
                                    for dt in range(ND):
                                        t.matmul(
                                            pv[qc % 2][:, dt * DTW:(dt + 1) * DTW],
                                            lh, XB[:, kc * D + dt * DTW:
                                                   kc * D + (dt + 1) * DTW],
                                            start=(kc == 0), stop=(kc == KN - 1))
                                    ins = t.matmul(psL[:, qcg:qcg + 1], lh,
                                                   onescol[:, :], start=(kc == 0),
                                                   stop=(kc == KN - 1))
                                ins.then_inc(st, 1)
                                c["t"] += 1
                            if qh == OWNB - 1:
                                PVdone[j] = c["t"]

                        @blk.vector
                        def _(v):
                            for qc in range(KN):
                                qcg = qh * KN + qc
                                v.wait_ge(st, t_base2 + qc + 1)
                                v.tensor_add(Lb[:, qcg:qcg + 1],
                                             Lb[:, qcg:qcg + 1],
                                             psL[:, qcg:qcg + 1]).then_inc(sv, 1)
                                c["v"] += 1
                                for dt in range(ND):
                                    sl = O_acc[:, qcg * D + dt * DTW:
                                               qcg * D + (dt + 1) * DTW]
                                    ins = v.tensor_add(sl, sl,
                                                       pv[qc % 2][:, dt * DTW:
                                                                  (dt + 1) * DTW])
                                ins.then_inc(sv, 1)
                                c["v"] += 1
                                if pend_stats and qc >= 2:
                                    for _ in range(3):
                                        if pend_stats:
                                            pend_stats.pop(0)(v)
                                if last_step and qc >= 1:
                                    o_sum(v, qh * KN + qc - 1)
                            if pend_stats is not None:
                                while pend_stats:
                                    pend_stats.pop(0)(v)
                                ln_stats_tail(v)
                            if last_step:
                                o_sum(v, qh * KN + KN - 1)
                                if qh == OWNB - 1:
                                    v.wait_ge(sv, c["v"])
                                    v.engine_nop().then_inc(sv, 1)
                                    c["v"] += 1
                                    v.wait_ge(sv, c["v"])
                                    ck = slice(0, OC)
                                    vhop(v, v.tensor_scalar_mul(MEANB[:, ck],
                                                                SS[:, ck], invD))
                                    vhop(v, v.tensor_mul(T1B[:, ck], MEANB[:, ck],
                                                         MEANB[:, ck]))
                                    vhop(v, v.scalar_tensor_tensor(
                                        VARPB[:, ck], SQ[:, ck], invD, T1B[:, ck],
                                        op0=ALU.mult, op1=ALU.subtract))
                                    vhop(v, v.tensor_mul(T1B[:, ck], Lb[:, ck],
                                                         Lb[:, ck]))
                                    v.scalar_tensor_tensor(
                                        VARPB[:, ck], T1B[:, ck], LN_EPS,
                                        VARPB[:, ck], op0=ALU.mult,
                                        op1=ALU.add).then_inc(sv, 1)
                                    c["v"] += 1

        with contextlib.ExitStack() as _esB:
            _b = _esB.enter_context
            X3FM = _b(nc.sbuf_tensor("X3FM", [P, DC * OWN], BF16))
            JIOT = _b(nc.sbuf_tensor("JIOT", [P, OWN], F32))
            PRMB = _b(nc.sbuf_tensor("PRMB", [P, 2 * 512], BF16))
            STG = _b(nc.sbuf_tensor("STG", [P, 2 * D], F32))
            TRX = _b(nc.sbuf_tensor("TRX", [P, D], BF16))
            PRB = _b(nc.sbuf_tensor("PRB", [P, NE], F32))
            PRN = _b(nc.sbuf_tensor("PRN", [P, NE], F32))
            M8 = _b(nc.sbuf_tensor("M8", [P, NE], F32))
            SMB = _b(nc.sbuf_tensor("SMB", [P, 1], F32))
            NMB = _b(nc.sbuf_tensor("NMB", [P, 1], F32))
            RSB = _b(nc.sbuf_tensor("RSB", [P, 1], F32))
            psTE = _b(nc.psum_tensor("psTE", [P, 512], BF16))
            ph0 = _b(nc.psum_tensor("ph0", [P, TGW], F32))
            ph1 = _b(nc.psum_tensor("ph1", [P, TGW], F32))
            psR = _b(nc.psum_tensor("psR", [P, NE], F32))
            py0 = _b(nc.psum_tensor("py0", [P, 512], F32))
            py1 = _b(nc.psum_tensor("py1", [P, 512], F32))
            py2 = _b(nc.psum_tensor("py2", [P, 512], F32))
            py3 = _b(nc.psum_tensor("py3", [P, 512], F32))
            ph = [ph0, ph1]
            py = [py0, py1, py2, py3]
            TRX_cur[0] = TRX

            X1 = XQR
            with (nc.sbuf_tensor("X3LO", [P, DC * OWN], BF16) as X3LO,
                  nc.sbuf_tensor("X2TM", [P, OC * D], BF16) as X2TM,
                  nc.sbuf_tensor("TRXF", [P, D], F32) as TRXF,
                  nc.sbuf_tensor("TRXLO", [P, D], BF16) as TRXLO):
                with nc.Block() as blk:
                    @blk.gpsimd
                    def _(g):
                        g.wait_ge(st, PVdone[NB - 1])
                        g.iota(JIOT[:, :], [[1, OWN]], base=0,
                               channel_multiplier=0,
                               allow_small_or_imprecise_dtypes=True
                               ).then_inc(sgc, 1)
                        c["gc"] += 1

                    @blk.scalar
                    def _(s):
                        rstd_scalar(s, OC)

                    @blk.vector
                    def _(v):
                        nbias_vec(v, OC)

                with nc.Block() as blk:
                    v_base = c["v"]
                    c_base = c["c"]
                    ap_inc = {}

                    @blk.scalar
                    def _(s):
                        s.wait_ge(sv, v_base)
                        for tc in range(OC):
                            if tc >= 2:
                                s.wait_ge(sv, v_base + tc - 1)
                            s.activation(STG[:, (tc % 2) * D:(tc % 2 + 1) * D],
                                         O_acc[:, tc * D:(tc + 1) * D], ACT.Identity,
                                         bias=NBSB[:, tc:tc + 1],
                                         scale=RSTDB[:, tc:tc + 1]
                                         ).then_inc(scm, 1)
                            c["c"] += 1
                            ap_inc[tc] = c["c"]
                            if tc >= 2:
                                s.wait_ge(sv, v_base + tc - 1)
                                s.activation(TRXF[:, 0:D],
                                             X1[:, (tc - 2) * D:(tc - 1) * D],
                                             ACT.Square,
                                             accum_out=SQ[:, tc - 2:tc - 1]
                                             ).then_inc(scm, 1)
                                c["c"] += 1
                        for tc in (OC - 2, OC - 1):
                            s.wait_ge(sv, v_base + tc + 1)
                            s.activation(TRXF[:, 0:D],
                                         X1[:, tc * D:(tc + 1) * D], ACT.Square,
                                         accum_out=SQ[:, tc:tc + 1]
                                         ).then_inc(scm, 1)
                            c["c"] += 1

                    @blk.vector
                    def _(v):
                        for tc in range(OC):
                            v.wait_ge(scm, ap_inc[tc])
                            v.scalar_tensor_tensor(
                                X1[:, tc * D:(tc + 1) * D],
                                XQR[:, tc * D:(tc + 1) * D], 1.0,
                                STG[:, (tc % 2) * D:(tc % 2 + 1) * D],
                                op0=ALU.mult, op1=ALU.add,
                                accum_out=SS[:, tc:tc + 1]).then_inc(sv, 1)
                            c["v"] += 1
                        v.wait_ge(scm, c["c"])
                        v.wait_ge(sv, c["v"])
                        v.engine_nop().then_inc(sv, 1)
                        c["v"] += 1
                        v.wait_ge(sv, c["v"])
                        ck = slice(0, OC)
                        vhop(v, v.tensor_scalar_mul(MEANB[:, ck], SS[:, ck],
                                                    invD))
                        vhop(v, v.tensor_mul(T1B[:, ck], MEANB[:, ck],
                                             MEANB[:, ck]))
                        vhop(v, v.scalar_tensor_tensor(VARPB[:, ck], SQ[:, ck],
                                                       invD, T1B[:, ck],
                                                       op0=ALU.mult,
                                                       op1=ALU.subtract))
                        v.tensor_scalar_add(VARPB[:, ck], VARPB[:, ck],
                                            LN_EPS).then_inc(sv, 1)
                        c["v"] += 1

                with nc.Block() as blk:
                    @blk.scalar
                    def _(s):
                        rstd_scalar(s, OC)

                    @blk.vector
                    def _(v):
                        nbias_vec(v, OC)

                with nc.Block() as blk:
                    c_base = c["c"]
                    t_base = c["t"]
                    v_base = c["v"]
                    NG2 = 2 * NG

                    @blk.scalar
                    def _(s):
                        s.wait_ge(sv, v_base)
                        for tc in range(OC):
                            if tc >= 1:
                                s.wait_ge(sv, v_base + (tc - 1) * (NG2 + 1) + 1)
                            s.activation(TRXF[:, 0:D], X1[:, tc * D:(tc + 1) * D],
                                         ACT.Identity, bias=NBSB[:, tc:tc + 1],
                                         scale=RSTDB[:, tc:tc + 1]
                                         ).then_inc(scm, 1)
                            c["c"] += 1

                    @blk.vector
                    def _(v):
                        x3r = X3FM[:, :].rearrange("p (dc q) -> p dc q", q=OWN)
                        x3l = X3LO[:, :].rearrange("p (dc q) -> p dc q", q=OWN)
                        for tc in range(OC):
                            v.wait_ge(scm, c_base + tc + 1)
                            if tc >= 1:
                                v.wait_ge(st, t_base + tc * NG2)
                            v.tensor_copy(TRX[:, 0:D], TRXF[:, 0:D])
                            v.tensor_copy(X2TM[:, tc * D:(tc + 1) * D],
                                          TRXF[:, 0:D])
                            v.tensor_sub(TRXLO[:, 0:D], TRXF[:, 0:D],
                                         TRX[:, 0:D]).then_inc(sv, 1)
                            c["v"] += 1
                            for half in range(2):
                                dstr = x3r if half == 0 else x3l
                                for ng in range(NG):
                                    fi = tc * NG2 + half * NG + ng
                                    v.wait_ge(st, t_base + fi + 1)
                                    v.tensor_copy(
                                        dstr[:, ng * GD:(ng + 1) * GD,
                                             tc * P:(tc + 1) * P],
                                        psTE[:, 0:GD * P].rearrange(
                                            "p (k q) -> p k q", q=P)
                                    ).then_inc(sv, 1)
                                    c["v"] += 1

                    @blk.tensor
                    def _(t):
                        for tc in range(OC):
                            for half in range(2):
                                src = TRX if half == 0 else TRXLO
                                for ng in range(NG):
                                    fi = tc * NG2 + half * NG + ng
                                    t.wait_ge(sv, v_base + tc * (NG2 + 1) + 1
                                              + (fi - tc * NG2))
                                    for k4 in range(GD):
                                        dc = ng * GD + k4
                                        ins = t.transpose(
                                            psTE[:, k4 * P:(k4 + 1) * P],
                                            src[:, dc * P:(dc + 1) * P],
                                            IDENT[:, :])
                                    ins.then_inc(st, 1)
                                    c["t"] += 1
                with nc.Block() as blk:
                    t_base = c["t"]
                    v_base = c["v"]
                    c_base = c["c"]

                    @blk.tensor
                    def _(t):
                        for tc in range(OC):
                            if tc >= 1:
                                t.wait_ge(sv, v_base + tc * 6)
                            for dc in range(DC):
                                t.matmul(psR[:, :],
                                         X3FM[:, dc * OWN + tc * P:
                                              dc * OWN + (tc + 1) * P],
                                         WrB[:, dc * NE:(dc + 1) * NE],
                                         start=(dc == 0), stop=False)
                            for dc in range(DC):
                                t.matmul(psR[:, :],
                                         X3LO[:, dc * OWN + tc * P:
                                              dc * OWN + (tc + 1) * P],
                                         WrB[:, dc * NE:(dc + 1) * NE],
                                         start=False, stop=False)
                            for dc in range(DC):
                                t.matmul(psR[:, :],
                                         X3FM[:, dc * OWN + tc * P:
                                              dc * OWN + (tc + 1) * P],
                                         WrLO[:, dc * NE:(dc + 1) * NE],
                                         start=False, stop=False)
                            t.matmul(psR[:, :], ones1[:, :], RB[:, :],
                                     start=False, stop=False)
                            t.matmul(psR[:, :], ones1[:, :], RBLO[:, :],
                                     start=False, stop=True).then_inc(st, 1)
                            c["t"] += 1

                    @blk.vector
                    def _(v):
                        for tc in range(OC):
                            v.wait_ge(st, t_base + tc + 1)
                            v.reduce_max(NMB[:, :], psR[:, :], axis=AX,
                                         negate=True).then_inc(sv, 1)
                            c["v"] += 1
                            v.wait_ge(scm, c_base + tc + 1)
                            vhop(v, v.reciprocal(RSB[:, :], SMB[:, :]))
                            vhop(v, v.tensor_scalar_mul(PRN[:, :], PRB[:, :],
                                                        RSB[:, :]))
                            vhop(v, v.reduce_max(NMB[:, :], PRN[:, :], axis=AX))
                            vhop(v, v.tensor_single_scalar(M8[:, :], PRN[:, :],
                                                           NMB[:, :],
                                                           op=ALU.is_equal))
                            v.tensor_mul(GATES[:, tc * NE:(tc + 1) * NE],
                                         M8[:, :], PRN[:, :]).then_inc(sv, 1)
                            c["v"] += 1
                        v.memset(O_acc[:, :], 0.0)

                    @blk.scalar
                    def _(s):
                        for tc in range(OC):
                            s.wait_ge(sv, v_base + tc * 6 + 1)
                            s.activation(PRB[:, :], psR[:, :], ACT.Exp,
                                         bias=NMB[:, :],
                                         accum_out=SMB[:, :]).then_inc(scm, 1)
                            c["c"] += 1
                with nc.Block() as blk:
                    @blk.vector
                    def _(v):
                        vhop(v, v.tensor_single_scalar(IND[:, :], GATES[:, :], 0.0,
                                                       op=ALU.is_gt))
                        for tc in range(OC):
                            ins = v.reduce_sum(GSEL[:, tc:tc + 1],
                                               GATES[:, tc * NE:(tc + 1) * NE],
                                               axis=AX)
                        ins.then_inc(sv, 1)
                        c["v"] += 1
                    V1 = c["v"]

                    @blk.tensor
                    def _(t):
                        t.wait_ge(sv, V1)
                        for tc in range(OC):
                            ins = t.matmul(py1[0:NE, tc:tc + 1],
                                           IND[:, tc * NE:(tc + 1) * NE],
                                           onescol[:, :], start=(tc == 0),
                                           stop=(tc == OC - 1))
                        ins.then_inc(st, 1)
                        c["t"] += 1
                    T1 = c["t"]

                    @blk.vector
                    def _(v):
                        v.wait_ge(st, T1)
                        vhop(v, v.tensor_copy(CNT[:, :], py1[0:NE, 0:OC]))
                        vhop(v, v.tensor_tensor_scan(CUMI[:, :], ONES8R[:, :],
                                                     CNT[:, :], 0.0, op0=ALU.mult,
                                                     op1=ALU.add))
                        vhop(v, v.tensor_sub(CUMTOT[:, :], CUMI[:, :], CNT[:, :]))
                        vhop(v, v.tensor_copy(TOT[:, :], CUMI[:, OC - 1:OC]))
                    V2 = c["v"]

                    @blk.tensor
                    def _(t):
                        t.wait_ge(sv, V2)
                        t.matmul(py2[0:NE, 0:1], LTS8F[:, :], TOT[:, :],
                                 start=True, stop=True).then_inc(st, 1)
                        c["t"] += 1
                    T2 = c["t"]

                    @blk.vector
                    def _(v):
                        v.wait_ge(st, T2)
                        vhop(v, v.tensor_copy(BASEC[:, :], py2[0:NE, 0:1]))
                        vhop(v, v.tensor_scalar_add(CUMTOT[:, :], CUMTOT[:, :],
                                                    BASEC[:, :]))
                        vhop(v, v.tensor_add(ENDC[:, :], BASEC[:, :], TOT[:, :]))
                        for tg in range(NTG):
                            v.tensor_single_scalar(T81[:, :], BASEC[:, :],
                                                   float((tg + 1) * TGW + KMARG),
                                                   op=ALU.is_lt)
                            vhop(v, v.tensor_single_scalar(T82[:, :], ENDC[:, :],
                                                           float(tg * TGW - KMARG),
                                                           op=ALU.is_gt))
                            vhop(v, v.tensor_mul(FL8[:, tg:tg + 1], T81[:, :],
                                                 T82[:, :]))
                        vhop(v, v.tensor_copy(FL8I[:, :], FL8[:, :]))
                        vhop(v, v.reduce_max(T81[:, :], FL8[:, :], axis=AX))
                        vhop(v, v.tensor_copy(EFL8I[:, :], T81[:, :]))
                    V3 = c["v"]

                    @blk.sync
                    def _(sy):
                        sy.wait_ge(sv, V3)
                        sdma(sy, rowd, CUMTOT[:, :])
                        sdma(sy, flid, FL8I[:, :])
                        sdma(sy, efld, EFL8I[:, :])
                        sy.wait_ge(ssm, c["s"])
                        sy.dma_start(ROWSF[0:1, :].rearrange(
                            "o (t e) -> o t e", e=NE),
                            rowd.rearrange("e t -> t e"),
                            allow_slow_non_contiguous=True).then_inc(ssm, 16)
                        c["s"] += 16
                        sdma(sy, FLROW[0:1, :].rearrange("o (e t) -> o e t",
                                                         t=NTG), flid)
                        sdma(sy, EFROW[0:1, :].rearrange("o (e t) -> o e t", t=1),
                             efld)
                    S1 = c["s"]

                    @blk.vector
                    def _(v):
                        v.wait_ge(ssm, S1)
                        vhop(v, v.tensor_copy(ROWSH[:, :], ROWSF[:, :]))
                        vhop(v, v.tensor_sub(ROWSL[:, :], ROWSF[:, :],
                                             ROWSH[:, :]))
                    V4 = c["v"]
                    t_base3 = c["t"]
                    v_base4 = c["v"]

                    @blk.tensor
                    def _(t):
                        t.wait_ge(sv, V4)
                        for tc in range(OC):
                            if tc >= 2:
                                t.wait_ge(sv, v_base4 + tc - 1)
                            pb = [py0, py2][tc % 2]
                            t.matmul(pb[0:P, 0:NE], LTS[:, :],
                                     IND[:, tc * NE:(tc + 1) * NE],
                                     start=True, stop=False)
                            t.matmul(pb[0:P, 0:NE], ones1[:, :],
                                     ROWSH[0:1, tc * NE:(tc + 1) * NE],
                                     start=False, stop=False)
                            t.matmul(pb[0:P, 0:NE], ones1[:, :],
                                     ROWSL[0:1, tc * NE:(tc + 1) * NE],
                                     start=False, stop=True).then_inc(st, 1)
                            c["t"] += 1

                    @blk.vector
                    def _(v):
                        for tc in range(OC):
                            v.wait_ge(st, t_base3 + tc + 1)
                            pb = [py0, py2][tc % 2]
                            v.scalar_tensor_tensor(
                                T8[:, :], pb[0:P, 0:NE], 1.0,
                                IND[:, tc * NE:(tc + 1) * NE], op0=ALU.mult,
                                op1=ALU.mult,
                                accum_out=DESTF[:, tc:tc + 1]).then_inc(sv, 1)
                            c["v"] += 1
                        v.wait_ge(sv, c["v"])
                        v.engine_nop().then_inc(sv, 1)
                        c["v"] += 1

                with nc.Block() as blk:
                    t0 = c["t"]

                    @blk.tensor
                    def _(t):
                        t.matmul(psR[:, 0:NE], ones1[:, :], ROWSH[0:1, 0:NE],
                                 start=True, stop=False)
                        t.matmul(psR[:, 0:NE], ones1[:, :], ROWSL[0:1, 0:NE],
                                 start=False, stop=True).then_inc(st, 1)
                        c["t"] += 1
                    T1p = c["t"]

                    @blk.vector
                    def _(v):
                        v.wait_ge(st, T1p)
                        vhop(v, v.tensor_copy(BASEB[:, :], psR[0:P, 0:NE]))
                        vhop(v, v.tensor_single_scalar(ESRT[:, :], TOKC[:, :],
                                                       BASEB[:, 0:1],
                                                       op=ALU.is_ge))
                        for e2 in range(1, NE):
                            vhop(v, v.tensor_single_scalar(T8B[:, :], TOKC[:, :],
                                                           BASEB[:, e2:e2 + 1],
                                                           op=ALU.is_ge))
                            vhop(v, v.tensor_add(ESRT[:, :], ESRT[:, :],
                                                 T8B[:, :]))
                        v.tensor_scalar_add(ESRT[:, :], ESRT[:, :],
                                            -1.0).then_inc(sv, 1)
                        c["v"] += 1

                with nc.Block() as blk:
                    NST = OWN // 512
                    NSUB = 512 // P
                    DCH = DC // 2
                    t_b = c["t"]
                    v_b = c["v"]
                    c_b = c["c"]

                    VPP = OC + NSUB // 2

                    @blk.vector
                    def _(v):
                        v.wait_ge(sgc, 2)
                        k = 0
                        for stile in range(NST):
                            for half in range(2):
                                p = stile * 2 + half
                                for tc in range(OC):
                                    if k >= 2:
                                        v.wait_ge(st, t_b + k - 1)
                                    v.tensor_single_scalar(
                                        PRMB[:, (k % 2) * 512:(k % 2) * 512 + 512],
                                        JIOT[:, stile * 512:(stile + 1) * 512],
                                        DESTF[:, tc:tc + 1],
                                        op=ALU.is_equal).then_inc(sv, 1)
                                    c["v"] += 1
                                    k += 1
                                v.wait_ge(st, t_b + (p + 1) * OC)
                                for sb in range(NSUB // 2):
                                    sub = half * (NSUB // 2) + sb
                                    v.tensor_copy(
                                        INVF[:, stile * NSUB + sub:
                                             stile * NSUB + sub + 1],
                                        ph[sb][0:P, 0:1]).then_inc(sv, 1)
                                    c["v"] += 1

                    @blk.tensor
                    def _(t):
                        k = 0
                        for stile in range(NST):
                            for half in range(2):
                                p = stile * 2 + half
                                for tc in range(OC):
                                    t.wait_ge(sv, v_b + p * VPP + tc + 1)
                                    if tc == 0:
                                        if p > 0:
                                            t.wait_ge(scm, c_b + p * DCH)
                                            t.wait_ge(sv, v_b + p * VPP)
                                    sl = PRMB[:, (k % 2) * 512:
                                              (k % 2) * 512 + 512]
                                    for dcq in range(DCH):
                                        dc = half * DCH + dcq
                                        t.matmul(py[dcq % 4][0:P, 0:512],
                                                 X2TM[:, tc * D + dc * P:
                                                      tc * D + (dc + 1) * P],
                                                 sl, start=(tc == 0),
                                                 stop=(tc == OC - 1))
                                    for sb in range(NSUB // 2):
                                        sub = half * (NSUB // 2) + sb
                                        slc = PRMB[:, (k % 2) * 512 + sub * P:
                                                   (k % 2) * 512 + (sub + 1) * P]
                                        t.matmul(ph[sb][0:P, 0:1], slc,
                                                 TOKCH[:, tc:tc + 1],
                                                 start=(tc == 0), stop=False)
                                        ins = t.matmul(ph[sb][0:P, 0:1], slc,
                                                       TOKCL[:, tc:tc + 1],
                                                       start=False,
                                                       stop=(tc == OC - 1))
                                    ins.then_inc(st, 1)
                                    c["t"] += 1
                                    k += 1

                    @blk.scalar
                    def _(s):
                        for stile in range(NST):
                            for half in range(2):
                                p = stile * 2 + half
                                s.wait_ge(st, t_b + (p + 1) * OC)
                                for dcq in range(DCH):
                                    dc = half * DCH + dcq
                                    s.activation(
                                        X3FM[:, dc * OWN + stile * 512:
                                             dc * OWN + (stile + 1) * 512],
                                        py[dcq % 4][0:P, 0:512],
                                        ACT.Copy).then_inc(scm, 1)
                                    c["c"] += 1

            with contextlib.ExitStack() as _esW:
                _w = _esW.enter_context
                NWB = 2
                W1B = [_w(nc.sbuf_tensor(f"W1B{i}", [P, DC * FSUB], BF16))
                       for i in range(NWB)]
                W2B = [_w(nc.sbuf_tensor(f"W2B{i}", [P, FCS * D], BF16))
                       for i in range(NWB)]
                HB = _w(nc.sbuf_tensor("HB", [P, FCS * TGW], BF16))
                B2B = _w(nc.sbuf_tensor("B2B", [1, D], BF16))

                UPT = NTG * (FCS + 1)
                t0f = c["t"]
                gwu = {}
                with nc.Block() as blk:
                    @blk.gpsimd
                    def _(g):
                        for u in range(NE * SC):
                            eu, scu = divmod(u, SC)
                            if u >= NWB:
                                g.wait_ge(st, t0f + (u - NWB + 1) * UPT)
                            wb = u % NWB

                            def _wload(gg):
                                gg.dma_start(
                                    W1B[wb][:, :].rearrange(
                                        "p (dc f) -> p dc f", f=FSUB),
                                    w1[eu:eu + 1, :,
                                       scu * FSUB:(scu + 1) * FSUB].rearrange(
                                        "o (dc p) f -> p (o dc) f", p=P)
                                ).then_inc(sg, 16)
                                gg.dma_start(
                                    W2B[wb][:, :].rearrange(
                                        "p (fc d) -> p fc d", d=D),
                                    w2[eu:eu + 1,
                                       scu * FSUB:(scu + 1) * FSUB, :].rearrange(
                                        "o (fc p) d -> p (o fc) d", p=P)
                                ).then_inc(sg, 16)
                            if USE_BRANCHES and BR_G:
                                with g.register(f"rgw{u}") as rgw:
                                    g.reg_load(rgw, EFROW[0:1, eu:eu + 1])
                                    with g.If_ne(rgw, 0):
                                        _wload(g)
                                    with g.If_eq(rgw, 0):
                                        g.dma_start(DUMB[:, :], b2[0:1, 0:8]
                                                    ).then_inc(sg, 16)
                                        g.dma_start(DUMB[:, :], b2[0:1, 0:8]
                                                    ).then_inc(sg, 16)
                            else:
                                _wload(g)
                            c["g"] += 32
                            gwu[u] = c["g"]
                            if u == 0:
                                gdma(g, B2B[:, :], b2[0:1, :])
                            if scu == min(1, SC - 1) and eu + 1 < NE:
                                g.wait_ge(st, t0f + (eu * SC + 1) * UPT)
                                gdma(g, B2B[:, :], b2[eu + 1:eu + 2, :])

                    for e in range(NE):
                        t_e0 = c["t"]
                        c_e0 = c["c"]
                        v_e0 = c["v"]

                        @blk.tensor
                        def _(t):
                            t.wait_ge(sg, gwu[e * SC])
                            if USE_BRANCHES and BR_T:
                                efv = nc.values_load(
                                    EFROW[0:1, e:e + 1],
                                    engines=[mybir.EngineType.PE])
                                arm_iter = t.Switch(efv, 2)
                                _rgt = t.register(f"rtf{e}")
                                rtf = _rgt.__enter__()
                            else:
                                arm_iter = iter([1])
                            for arm in arm_iter:
                                if arm == 0:
                                    t.sem_inc(st, SC * UPT)
                                    continue
                                for sci in range(SC):
                                    i = e * SC + sci
                                    buf = i % NWB
                                    c_base = c_e0 + sci * NTG * FCS
                                    v_base = v_e0 + sci * NTG * (TGC + 1)
                                    if sci >= 1:
                                        t.wait_ge(sg, gwu[i])
                                    for tg in range(NTG):
                                        if USE_BRANCHES and BR_T:
                                            t.reg_load(rtf,
                                                       FLROW[0:1, e * NTG + tg:
                                                             e * NTG + tg + 1])
                                            t.drain()
                                            _if = t.If_ne(rtf, 0)
                                            _if.__enter__()
                                        for fc in range(FCS):
                                            k = tg * FCS + fc
                                            if c_base + k - 1 > 0:
                                                t.wait_ge(scm, c_base + k - 1)
                                            dst = ph[k % 2][:, :]
                                            for dci in range(DC):
                                                ins = t.matmul(
                                                    dst,
                                                    W1B[buf][:,
                                                             dci * FSUB + fc * P:
                                                             dci * FSUB
                                                             + (fc + 1) * P],
                                                    X3FM[:,
                                                         dci * OWN + tg * TGW:
                                                         dci * OWN
                                                         + (tg + 1) * TGW],
                                                    start=(dci == 0),
                                                    stop=(dci == DC - 1))
                                            ins.then_inc(st, 1)
                                            c["t"] += 1
                                        t.wait_ge(scm, c_base + (tg + 1) * FCS)
                                        if sci > 0 or tg >= 1:
                                            t.wait_ge(sv,
                                                      v_base + tg * (TGC + 1))
                                        elif e >= 1:
                                            t.wait_ge(sv, v_e0)
                                        if sci == 0:
                                            for tc in range(TGC):
                                                for dt in range(ND):
                                                    t.matmul(
                                                        py[tc * ND + dt][0:P,
                                                                         0:DTW],
                                                        ones1[:, :],
                                                        B2B[:, dt * DTW:
                                                            (dt + 1) * DTW],
                                                        start=True, stop=False)
                                        for fc in range(FCS):
                                            for tc in range(TGC):
                                                lh = HB[:, fc * TGW + tc * P:
                                                        fc * TGW + (tc + 1) * P]
                                                for dt in range(ND):
                                                    ins = t.matmul(
                                                        py[tc * ND + dt][0:P,
                                                                         0:DTW],
                                                        lh,
                                                        W2B[buf][:,
                                                                 fc * D
                                                                 + dt * DTW:
                                                                 fc * D
                                                                 + (dt + 1)
                                                                 * DTW],
                                                        start=(sci > 0
                                                               and fc == 0),
                                                        stop=(fc == FCS - 1))
                                        ins.then_inc(st, 1)
                                        c["t"] += 1
                                        if USE_BRANCHES and BR_T:
                                            _if.__exit__(None, None, None)
                                            with t.If_eq(rtf, 0):
                                                t.sem_inc(st, FCS + 1)
                            if USE_BRANCHES and BR_T:
                                _rgt.__exit__(None, None, None)

                        @blk.scalar
                        def _(s):
                            if USE_BRANCHES and BR_C:
                                _rgc = s.register(f"rcf{e}")
                                rcf = _rgc.__enter__()
                            for sci in range(SC):
                                t_base = t_e0 + sci * UPT
                                for tg in range(NTG):
                                    if USE_BRANCHES and BR_C:
                                        s.reg_load(rcf,
                                                   FLROW[0:1, e * NTG + tg:
                                                         e * NTG + tg + 1])
                                        _if = s.If_ne(rcf, 0)
                                        _if.__enter__()
                                    for fc in range(FCS):
                                        k = tg * FCS + fc
                                        s.wait_ge(st, t_base + tg + k + 1)
                                        bc = e * FCT + sci * FCS + fc
                                        s.activation(
                                            HB[:, fc * TGW:(fc + 1) * TGW],
                                            ph[k % 2][:, :], ACT.Relu,
                                            bias=B1all[:, bc:bc + 1]
                                        ).then_inc(scm, 1)
                                        c["c"] += 1
                                    if USE_BRANCHES and BR_C:
                                        _if.__exit__(None, None, None)
                                        with s.If_eq(rcf, 0):
                                            s.sem_inc(scm, FCS)
                            if USE_BRANCHES and BR_C:
                                _rgc.__exit__(None, None, None)

                        @blk.vector
                        def _(v):
                            if USE_BRANCHES and BR_V:
                                _rgv = v.register(f"rvf{e}")
                                rvf = _rgv.__enter__()
                            for sci in range(SC):
                                t_base = t_e0 + sci * UPT
                                for tg in range(NTG):
                                    if USE_BRANCHES and BR_V:
                                        v.reg_load(rvf,
                                                   FLROW[0:1, e * NTG + tg:
                                                         e * NTG + tg + 1])
                                        _if = v.If_ne(rvf, 0)
                                        _if.__enter__()
                                    v.wait_ge(st,
                                              t_base + (tg + 1) * (FCS + 1))
                                    for tc in range(TGC):
                                        tcg = tg * TGC + tc
                                        vhop(v, v.tensor_single_scalar(
                                            SCL[:, tc:tc + 1],
                                            ESRT[:, tcg:tcg + 1], float(e),
                                            op=ALU.is_equal))
                                        for dt in range(ND):
                                            sl = O_acc[:, tcg * D + dt * DTW:
                                                       tcg * D + (dt + 1) * DTW]
                                            ins = v.scalar_tensor_tensor(
                                                sl, py[tc * ND + dt][0:P, 0:DTW],
                                                SCL[:, tc:tc + 1],
                                                sl, op0=ALU.mult, op1=ALU.add)
                                    ins.then_inc(sv, 1)
                                    c["v"] += 1
                                    if USE_BRANCHES and BR_V:
                                        _if.__exit__(None, None, None)
                                        with v.If_eq(rvf, 0):
                                            v.sem_inc(sv, TGC + 1)
                            if USE_BRANCHES and BR_V:
                                _rgv.__exit__(None, None, None)


                with nc.Block() as blk:
                    NTT = OWN // 512
                    SCC = OWN // P
                    t_b2 = c["t"]
                    v_b2 = c["v"]
                    c_b2 = c["c"]

                    @blk.vector
                    def _(v):
                        k = 0
                        for tt in range(NTT):
                            for dt in range(ND):
                                for sc in range(SCC):
                                    if k >= 2:
                                        v.wait_ge(st, t_b2 + k - 1)
                                    v.tensor_single_scalar(
                                        PRMB[:, (k % 2) * 512:
                                             (k % 2) * 512 + 512],
                                        JIOT[:, tt * 512:(tt + 1) * 512],
                                        INVF[:, sc:sc + 1],
                                        op=ALU.is_equal).then_inc(sv, 1)
                                    c["v"] += 1
                                    k += 1

                    @blk.tensor
                    def _(t):
                        k = 0
                        for tt in range(NTT):
                            for dt in range(ND):
                                p = tt * ND + dt
                                for sc in range(SCC):
                                    t.wait_ge(sv, v_b2 + k + 1)
                                    if sc == 0 and p >= 1:
                                        t.wait_ge(scm, c_b2 + p * 4)
                                    sl = PRMB[:, (k % 2) * 512:
                                              (k % 2) * 512 + 512]
                                    for tq in range(4):
                                        ins = t.matmul(
                                            py[tq][0:P, 0:DTW],
                                            sl[:, tq * P:(tq + 1) * P],
                                            O_acc[:, sc * D + dt * DTW:
                                                  sc * D + dt * DTW + DTW],
                                            start=(sc == 0),
                                            stop=(sc == SCC - 1))
                                    ins.then_inc(st, 1)
                                    c["t"] += 1
                                    k += 1

                    @blk.scalar
                    def _(s):
                        for tt in range(NTT):
                            for dt in range(ND):
                                p = tt * ND + dt
                                s.wait_ge(st, t_b2 + (p + 1) * SCC)
                                for tq in range(4):
                                    tcg = tt * 4 + tq
                                    s.activation(
                                        X3FM[:, tcg * D + dt * DTW:
                                             tcg * D + dt * DTW + DTW],
                                        py[tq][0:P, 0:DTW], ACT.Identity,
                                        scale=GSEL[:, tcg:tcg + 1]
                                    ).then_inc(scm, 1)
                                    c["c"] += 1

                    c_u0 = c_b2
                    v_l40 = c["v"]

                    @blk.vector
                    def _(v):
                        for tt in range(NTT):
                            v.wait_ge(scm, c_u0 + (tt + 1) * ND * 4)
                            for tq in range(4):
                                i2 = tt * 4 + tq
                                sl = X3FM[:, i2 * D:(i2 + 1) * D]
                                v.reduce_sum(SS[:, i2:i2 + 1], sl, axis=AX)
                                v.scalar_tensor_tensor(
                                    TRX[:, 0:D], sl, 1.0, sl, op0=ALU.mult,
                                    op1=ALU.mult, accum_out=SQ[:, i2:i2 + 1])
                        v.wait_ge(sv, c["v"])
                        v.engine_nop().then_inc(sv, 1)
                        c["v"] += 1
                        v.wait_ge(sv, c["v"])
                        ck = slice(0, OC)
                        vhop(v, v.tensor_scalar_mul(MEANB[:, ck], SS[:, ck],
                                                    invD))
                        vhop(v, v.tensor_mul(T1B[:, ck], MEANB[:, ck],
                                             MEANB[:, ck]))
                        vhop(v, v.scalar_tensor_tensor(VARPB[:, ck], SQ[:, ck],
                                                       invD, T1B[:, ck],
                                                       op0=ALU.mult,
                                                       op1=ALU.subtract))
                        v.tensor_scalar_add(VARPB[:, ck], VARPB[:, ck],
                                            LN_EPS).then_inc(sv, 1)
                        c["v"] += 1

                    @blk.scalar
                    def _(s):
                        rstd_scalar(s, OC)

                    @blk.vector
                    def _(v):
                        nbias_vec(v, OC)

                    c_base = c["c"]
                    v_base = c["v"]
                    s_base = c["s"]

                    @blk.sync
                    def _(s):
                        for tc in range(OC):
                            s.wait_ge(sv, v_base + tc + 1)
                            sdma(s, yo[tc * P:(tc + 1) * P, :],
                                 STG[:, (tc % 2) * D:(tc % 2 + 1) * D])
                        s.wait_ge(ssm, c["s"])

                    @blk.scalar
                    def _(s):
                        s.wait_ge(sv, v_base)
                        for tc in range(OC):
                            if tc >= 2:
                                s.wait_ge(ssm, s_base + (tc - 1) * 16)
                            s.activation(STG[:, (tc % 2) * D:(tc % 2 + 1) * D],
                                         X3FM[:, tc * D:(tc + 1) * D], ACT.Identity,
                                         bias=NBSB[:, tc:tc + 1],
                                         scale=RSTDB[:, tc:tc + 1]
                                         ).then_inc(scm, 1)
                            c["c"] += 1

                    @blk.vector
                    def _(v):
                        for tc in range(OC):
                            v.wait_ge(scm, c_base + tc + 1)
                            sl = STG[:, (tc % 2) * D:(tc % 2 + 1) * D]
                            v.tensor_add(sl, sl,
                                         XQR[:, tc * D:(tc + 1) * D]
                                         ).then_inc(sv, 1)
                            c["v"] += 1

                    @blk.gpsimd
                    def _(g):
                        g.wait_ge(sg, c["g"])

    nc.finalize()
    return nc


_MASK_CACHE = {}


def _make_mask(qstart, T, OWN):
    key = (qstart, T, OWN)
    if key not in _MASK_CACHE:
        SMAX = T - P
        MW = SMAX + OWN
        u = np.arange(MW)[None, :]
        pr = np.arange(P)[:, None]
        _MASK_CACHE[key] = np.ascontiguousarray(
            (u >= pr + SMAX - qstart).astype(ml_dtypes.bfloat16))
    return _MASK_CACHE[key]


def kernel(L0_attn_x, L0_attn_Wq, L0_attn_Wk, L0_attn_Wv, L0_ffn_router_W,
           L0_ffn_router_b, L0_ffn_W1, L0_ffn_b1, L0_ffn_W2, L0_ffn_b2):
    x = np.asarray(L0_attn_x, dtype=np.float32)
    B, T, D = x.shape
    wq = np.ascontiguousarray(np.asarray(L0_attn_Wq, dtype=np.float32))
    wk = np.ascontiguousarray(np.asarray(L0_attn_Wk, dtype=np.float32))
    wv = np.ascontiguousarray(np.asarray(L0_attn_Wv, dtype=np.float32))
    rw = np.ascontiguousarray(np.asarray(L0_ffn_router_W, dtype=np.float32))
    rb = np.ascontiguousarray(np.asarray(L0_ffn_router_b,
                                         dtype=np.float32)).reshape(1, -1)
    w1 = np.ascontiguousarray(np.asarray(L0_ffn_W1, dtype=np.float32))
    b1 = np.asarray(L0_ffn_b1, dtype=np.float32)
    ne_, f_ = b1.shape
    b1 = np.ascontiguousarray(
        b1.reshape(ne_, f_ // 128, 128).transpose(2, 0, 1).reshape(128, -1))
    w2 = np.ascontiguousarray(np.asarray(L0_ffn_W2, dtype=np.float32))
    b2 = np.ascontiguousarray(np.asarray(L0_ffn_b2, dtype=np.float32))
    NE, _, F = w1.shape
    OWN = (B * T) // 8

    nc = build_program(D=D, F=F, T=T, OWN=OWN, NE=NE)

    in_maps = []
    for core in range(8):
        b, a = divmod(core, 4)
        qs = a * OWN
        in_maps.append({
            "xb": np.ascontiguousarray(x[b]),
            "xq": np.ascontiguousarray(x[b, qs:qs + OWN]),
            "mk": _make_mask(qs, T, OWN),
            "wq": wq, "wk": wk, "wv": wv, "rw": rw, "rb": rb,
            "w1": w1, "b1": b1, "w2": w2, "b2": b2,
        })
    res = run_bass_kernel_spmd(nc, in_maps, core_ids=list(range(8)))
    out = np.empty((B, T, D), dtype=np.float32)
    for core in range(8):
        b, a = divmod(core, 4)
        qs = a * OWN
        out[b, qs:qs + OWN] = np.asarray(res.results[core]["y"],
                                         dtype=np.float32).reshape(OWN, D)
    return out


# version 18
# speedup vs baseline: 1.0792x; 1.0111x over previous
import contextlib
import numpy as np
import ml_dtypes
import concourse.bass as bass
from concourse.bass import IndirectOffsetOnAxis
import concourse.mybir as mybir
from concourse.bass_utils import run_bass_kernel_spmd

F32 = mybir.dt.float32
BF16 = mybir.dt.bfloat16
I32 = mybir.dt.int32
AX = mybir.AxisListType.X
ALU = mybir.AluOpType
ACT = mybir.ActivationFunctionType

LN_EPS = 1e-5
SHIFT = 12.0
P = 128
USE_BRANCHES = True
KMARG = 64
BR_G = True
BR_T = True
BR_C = True
BR_V = True


def build_program(D=1024, F=4096, T=8192, BT=1024, OWN=2048, TGW=256, FSUB=1024,
                  NE=8):
    DC = D // P
    KN = BT // P
    NB = T // BT
    OC = OWN // P
    OWNB = OWN // BT
    QTW = min(512, BT)
    NQT = BT // QTW
    DTW = min(512, D)
    ND = D // DTW
    FCS = FSUB // P
    SC = F // FSUB
    FCT = F // P
    NTG = OWN // TGW
    TGC = TGW // P
    G4 = min(4, KN)
    GD = min(4, DC)
    NG = DC // GD
    SMAX = T - P
    MW = SMAX + OWN
    WS = OWN + (KN - 1) * P
    invD = 1.0 / D
    att_scale = 1.0 / float(np.sqrt(D))
    assert KN * BT >= DC * D

    nc = bass.Bass("TRN2", target_bir_lowering=False, detect_race_conditions=False)

    xb = nc.dram_tensor("xb", [T, D], F32, kind="ExternalInput").ap()
    xq = nc.dram_tensor("xq", [OWN, D], F32, kind="ExternalInput").ap()
    mk = nc.dram_tensor("mk", [P, MW], BF16, kind="ExternalInput").ap()
    wq = nc.dram_tensor("wq", [D, D], F32, kind="ExternalInput").ap()
    wk = nc.dram_tensor("wk", [D, D], F32, kind="ExternalInput").ap()
    wv = nc.dram_tensor("wv", [D, D], F32, kind="ExternalInput").ap()
    rw = nc.dram_tensor("rw", [D, NE], F32, kind="ExternalInput").ap()
    rb = nc.dram_tensor("rb", [1, NE], F32, kind="ExternalInput").ap()
    w1 = nc.dram_tensor("w1", [NE, D, F], F32, kind="ExternalInput").ap()
    b1 = nc.dram_tensor("b1", [P, NE * FCT], F32, kind="ExternalInput").ap()
    w2 = nc.dram_tensor("w2", [NE, F, D], F32, kind="ExternalInput").ap()
    b2 = nc.dram_tensor("b2", [NE, D], F32, kind="ExternalInput").ap()
    yo = nc.dram_tensor("y", [OWN, D], F32, kind="ExternalOutput").ap()
    x1d = nc.dram_tensor("x1d", [OWN, D], BF16).ap()
    rowd = nc.dram_tensor("rowd", [NE, OC], F32).ap()
    flid = nc.dram_tensor("flid", [NE, NTG], I32).ap()
    efld = nc.dram_tensor("efld", [NE, 1], I32).ap()

    c = {"g": 0, "s": 0, "t": 0, "v": 0, "c": 0, "gc": 0}

    with contextlib.ExitStack() as _es0:
        _e = _es0.enter_context
        O_acc = _e(nc.sbuf_tensor("O_acc", [P, OC * D], BF16))
        XQR = _e(nc.sbuf_tensor("XQR", [P, OC * D], BF16))
        Lb = _e(nc.sbuf_tensor("Lb", [P, OC], F32))
        SS = _e(nc.sbuf_tensor("SS", [P, max(KN, OC)], F32))
        SQ = _e(nc.sbuf_tensor("SQ", [P, max(KN, OC)], F32))
        MEANB = _e(nc.sbuf_tensor("MEANB", [P, max(KN, OC)], F32))
        T1B = _e(nc.sbuf_tensor("T1B", [P, max(KN, OC)], F32))
        VARPB = _e(nc.sbuf_tensor("VARPB", [P, max(KN, OC)], F32))
        RSTDB = _e(nc.sbuf_tensor("RSTDB", [P, max(KN, OC)], F32))
        NBSB = _e(nc.sbuf_tensor("NBSB", [P, max(KN, OC)], F32))
        IDENT = _e(nc.sbuf_tensor("IDENT", [P, P], BF16))
        ones1 = _e(nc.sbuf_tensor("ones1", [1, P], BF16))
        onescol = _e(nc.sbuf_tensor("onescol", [P, 1], BF16))
        GATES = _e(nc.sbuf_tensor("GATES", [P, OC * NE], F32))
        B1all = _e(nc.sbuf_tensor("B1all", [P, NE * FCT], F32))
        WrB = _e(nc.sbuf_tensor("WrB", [P, DC * NE], BF16))
        WrLO = _e(nc.sbuf_tensor("WrLO", [P, DC * NE], BF16))
        WrF = _e(nc.sbuf_tensor("WrF", [P, DC * NE], F32))
        RB = _e(nc.sbuf_tensor("RB", [1, NE], BF16))
        RBLO = _e(nc.sbuf_tensor("RBLO", [1, NE], BF16))
        RBF = _e(nc.sbuf_tensor("RBF", [1, NE], F32))
        NSH = _e(nc.sbuf_tensor("NSH", [P, 1], F32))
        LTS = _e(nc.sbuf_tensor("LTS", [P, P], BF16))
        LTS8F = _e(nc.sbuf_tensor("LTS8F", [NE, NE], F32))
        ERMP = _e(nc.sbuf_tensor("ERMP", [P, NE], F32))
        TOKC = _e(nc.sbuf_tensor("TOKC", [P, OC], F32))
        ONES8R = _e(nc.sbuf_tensor("ONES8R", [NE, OC], F32))
        IND = _e(nc.sbuf_tensor("IND", [P, OC * NE], BF16))
        TOKCH = _e(nc.sbuf_tensor("TOKCH", [P, OC], BF16))
        TOKCL = _e(nc.sbuf_tensor("TOKCL", [P, OC], BF16))
        INVF = _e(nc.sbuf_tensor("INVF", [P, OC], F32))
        BASEB = _e(nc.sbuf_tensor("BASEB", [P, NE], F32))
        CNT = _e(nc.sbuf_tensor("CNT", [NE, OC], F32))
        CUMI = _e(nc.sbuf_tensor("CUMI", [NE, OC], F32))
        CUMTOT = _e(nc.sbuf_tensor("CUMTOT", [NE, OC], F32))
        TOT = _e(nc.sbuf_tensor("TOT", [NE, 1], F32))
        BASEC = _e(nc.sbuf_tensor("BASEC", [NE, 1], F32))
        ENDC = _e(nc.sbuf_tensor("ENDC", [NE, 1], F32))
        FL8 = _e(nc.sbuf_tensor("FL8", [NE, NTG], F32))
        FL8I = _e(nc.sbuf_tensor("FL8I", [NE, NTG], I32))
        EFL8I = _e(nc.sbuf_tensor("EFL8I", [NE, 1], I32))
        T81 = _e(nc.sbuf_tensor("T81", [NE, 1], F32))
        T82 = _e(nc.sbuf_tensor("T82", [NE, 1], F32))
        FLROW = _e(nc.sbuf_tensor("FLROW", [1, NE * NTG], I32))
        EFROW = _e(nc.sbuf_tensor("EFROW", [1, NE], I32))
        ROWSF = _e(nc.sbuf_tensor("ROWSF", [1, OC * NE], F32))
        ROWSH = _e(nc.sbuf_tensor("ROWSH", [1, OC * NE], BF16))
        ROWSL = _e(nc.sbuf_tensor("ROWSL", [1, OC * NE], BF16))
        T8 = _e(nc.sbuf_tensor("T8", [P, NE], F32))
        T8B = _e(nc.sbuf_tensor("T8B", [P, OC], F32))
        GSEL = _e(nc.sbuf_tensor("GSEL", [P, OC], F32))
        DESTF = _e(nc.sbuf_tensor("DESTF", [P, OC], F32))
        ESRT = _e(nc.sbuf_tensor("ESRT", [P, OC], F32))
        SCL = _e(nc.sbuf_tensor("SCL", [P, TGC], F32))
        DUMB = _e(nc.sbuf_tensor("DUMB", [1, 8], F32))
        sg = _e(nc.semaphore("sg"))
        ssm = _e(nc.semaphore("ssm"))
        st = _e(nc.semaphore("st"))
        sv = _e(nc.semaphore("sv"))
        scm = _e(nc.semaphore("scm"))
        sgc = _e(nc.semaphore("sgc"))
        def gdma(eng, dst, src):
            eng.dma_start(dst, src).then_inc(sg, 16)
            c["g"] += 16

        def sdma(eng, dst, src):
            eng.dma_start(dst, src).then_inc(ssm, 16)
            c["s"] += 16

        def vhop(v, ins):
            ins.then_inc(sv, 1)
            c["v"] += 1
            v.wait_ge(sv, c["v"])

        def chop(s, ins):
            ins.then_inc(scm, 1)
            c["c"] += 1
            s.wait_ge(scm, c["c"])

        def batch_stats(v, src_buf, nchunk, add_eps_lsq=None):
            for i in range(nchunk):
                sl = src_buf(i)
                v.reduce_sum(SS[:, i:i + 1], sl, axis=AX)
                ins = v.scalar_tensor_tensor(TRX_cur[0][:, 0:D], sl, 1.0, sl,
                                             op0=ALU.mult, op1=ALU.mult,
                                             accum_out=SQ[:, i:i + 1])
            vhop(v, ins)
            ck = slice(0, nchunk)
            vhop(v, v.tensor_scalar_mul(MEANB[:, ck], SS[:, ck], invD))
            vhop(v, v.tensor_mul(T1B[:, ck], MEANB[:, ck], MEANB[:, ck]))
            vhop(v, v.scalar_tensor_tensor(VARPB[:, ck], SQ[:, ck], invD,
                                           T1B[:, ck], op0=ALU.mult,
                                           op1=ALU.subtract))
            if add_eps_lsq is None:
                ins = v.tensor_scalar_add(VARPB[:, ck], VARPB[:, ck], LN_EPS)
            else:
                vhop(v, v.tensor_mul(T1B[:, ck], add_eps_lsq[:, ck],
                                     add_eps_lsq[:, ck]))
                ins = v.scalar_tensor_tensor(VARPB[:, ck], T1B[:, ck], LN_EPS,
                                             VARPB[:, ck], op0=ALU.mult,
                                             op1=ALU.add)
            ins.then_inc(sv, 1)
            c["v"] += 1

        def rstd_scalar(s, nchunk):
            s.wait_ge(sv, c["v"])
            ck = slice(0, nchunk)
            chop(s, s.activation(T1B[:, ck], VARPB[:, ck], ACT.Ln))
            s.activation(RSTDB[:, ck], T1B[:, ck], ACT.Exp,
                         scale=-0.5).then_inc(scm, 1)
            c["c"] += 1

        def nbias_vec(v, nchunk):
            v.wait_ge(scm, c["c"])
            ck = slice(0, nchunk)
            v.scalar_tensor_tensor(NBSB[:, ck], MEANB[:, ck], -1.0, RSTDB[:, ck],
                                   op0=ALU.mult, op1=ALU.mult).then_inc(sv, 1)
            c["v"] += 1

        TRX_cur = [None]

        with contextlib.ExitStack() as _esA:
            _a = _esA.enter_context
            IOTF = _a(nc.sbuf_tensor("IOTF", [P, P], F32))
            Mstep = _a(nc.sbuf_tensor("Mstep", [P, WS], BF16))
            WkB = _a(nc.sbuf_tensor("WkB", [P, DC * D], BF16))
            WvB = _a(nc.sbuf_tensor("WvB", [P, DC * D], BF16))
            QFM = _a(nc.sbuf_tensor("QFM", [P, DC * OWN], BF16))
            TMP = _a(nc.sbuf_tensor("TMP", [P, KN * D], BF16))
            KBUF = _a(nc.sbuf_tensor("KBUF", [P, KN * D], BF16))
            LNFM = _a(nc.sbuf_tensor("LNFM", [P, DC * BT], BF16))
            PT = _a(nc.sbuf_tensor("PT", [P, KN * BT], BF16))
            SQT = _a(nc.sbuf_tensor("SQT", [P, D], BF16))
            psT = _a(nc.psum_tensor("psT", [P, 512], BF16))
            pp0 = _a(nc.psum_tensor("pp0", [P, 512], F32))
            pp1 = _a(nc.psum_tensor("pp1", [P, 512], F32))
            pv0 = _a(nc.psum_tensor("pv0", [P, min(D, 1024)], F32))
            pv1 = _a(nc.psum_tensor("pv1", [P, min(D, 1024)], F32))
            psL = _a(nc.psum_tensor("psL", [P, OC], F32))
            pp = [pp0, pp1]
            pv = [pv0, pv1]

            with nc.Block() as blk:
                @blk.gpsimd
                def _(g):
                    g.iota(IOTF[:, :], [[1, P]], base=0, channel_multiplier=-1,
                           allow_small_or_imprecise_dtypes=True)
                    g.iota(ERMP[:, :], [[1, NE]], base=0, channel_multiplier=0,
                           allow_small_or_imprecise_dtypes=True)
                    g.iota(TOKC[:, :], [[P, OC]], base=0, channel_multiplier=1,
                           allow_small_or_imprecise_dtypes=True).then_inc(sgc, 1)
                    c["gc"] += 1

                @blk.sync
                def _(s):
                    sdma(s, B1all[:, :], b1)
                    sdma(s, WrF[:, :].rearrange("p (dc e) -> p dc e", e=NE),
                         rw.rearrange("(dc p) e -> p dc e", p=P))
                    sdma(s, RBF[:, :], rb)
                    s.wait_ge(ssm, c["s"])
                s_init = c["s"]

                @blk.vector
                def _(v):
                    v.wait_ge(sgc, 1)
                    v.tensor_single_scalar(IDENT[:, :], IOTF[:, :], 0.0,
                                           op=ALU.is_equal)
                    vhop(v, v.tensor_single_scalar(LTS[:, :], IOTF[:, :], 0.0,
                                                   op=ALU.is_gt))
                    v.tensor_copy(LTS8F[:, :], LTS[0:NE, 0:NE])
                    v.memset(ONES8R[:, :], 1.0)
                    vhop(v, v.tensor_copy(TOKCH[:, :], TOKC[:, :]))
                    v.tensor_sub(TOKCL[:, :], TOKC[:, :], TOKCH[:, :])
                    v.memset(ones1[:, :], 1.0)
                    v.memset(onescol[:, :], 1.0)
                    v.memset(NSH[:, :], -SHIFT)
                    v.memset(O_acc[:, :], 0.0)
                    v.memset(Lb[:, :], 0.0)
                    v.wait_ge(ssm, s_init)
                    vhop(v, v.tensor_copy(WrB[:, :], WrF[:, :]))
                    v.tensor_sub(WrLO[:, :], WrF[:, :], WrB[:, :])
                    vhop(v, v.tensor_copy(RB[:, :], RBF[:, :]))
                    v.tensor_sub(RBLO[:, :], RBF[:, :], RB[:, :])

            def ln_stats_chunk_ops(g_wait, XB):
                ops = []

                def mk(kn):
                    def f(v):
                        if kn == 0:
                            v.wait_ge(sg, g_wait)
                        sl = XB[:, kn * D:(kn + 1) * D]
                        v.reduce_sum(SS[:, kn:kn + 1], sl, axis=AX)
                        v.scalar_tensor_tensor(SQT[:, 0:D], sl, 1.0, sl,
                                               op0=ALU.mult, op1=ALU.mult,
                                               accum_out=SQ[:, kn:kn + 1])
                    return f
                for kn in range(KN):
                    ops.append(mk(kn))
                return ops

            def ln_stats_tail(v):
                v.wait_ge(sv, c["v"])
                v.engine_nop().then_inc(sv, 1)
                c["v"] += 1
                v.wait_ge(sv, c["v"])
                ck = slice(0, KN)
                vhop(v, v.tensor_scalar_mul(MEANB[:, ck], SS[:, ck], invD))
                vhop(v, v.tensor_mul(T1B[:, ck], MEANB[:, ck], MEANB[:, ck]))
                vhop(v, v.scalar_tensor_tensor(VARPB[:, ck], SQ[:, ck], invD,
                                               T1B[:, ck], op0=ALU.mult,
                                               op1=ALU.subtract))
                v.tensor_scalar_add(VARPB[:, ck], VARPB[:, ck],
                                    LN_EPS).then_inc(sv, 1)
                c["v"] += 1

            def ln_stats_part(blk, g_wait, XB):
                ops = ln_stats_chunk_ops(g_wait, XB)

                @blk.vector
                def _(v):
                    for f in ops:
                        f(v)
                    ln_stats_tail(v)

            def ln_and_transpose(blk, g_wait, XB, LB, pv_guard=None,
                                 stats_done=False):
                if not stats_done:
                    ln_stats_part(blk, g_wait, XB)

                @blk.scalar
                def _(s):
                    rstd_scalar(s, KN)

                @blk.vector
                def _(v):
                    nbias_vec(v, KN)
                v_nb = c["v"]

                c_apply0 = c["c"]

                @blk.scalar
                def _(s):
                    s.wait_ge(sv, v_nb)
                    if pv_guard is not None:
                        s.wait_ge(st, pv_guard)
                    for kn in range(KN):
                        s.activation(LB[:, kn * D:(kn + 1) * D],
                                     XB[:, kn * D:(kn + 1) * D], ACT.Identity,
                                     bias=NBSB[:, kn:kn + 1],
                                     scale=RSTDB[:, kn:kn + 1]).then_inc(scm, 1)
                        c["c"] += 1

                ngrp = KN // G4
                t_base = c["t"]
                v_base = c["v"]

                @blk.tensor
                def _(t):
                    i = 0
                    for dc in range(DC):
                        for gi in range(ngrp):
                            t.wait_ge(scm, c_apply0 + (gi + 1) * G4)
                            t.wait_ge(sv, v_base + i)
                            for k4 in range(G4):
                                kn = gi * G4 + k4
                                ins = t.transpose(
                                    psT[:, k4 * P:(k4 + 1) * P],
                                    LB[:, kn * D + dc * P:kn * D + (dc + 1) * P],
                                    IDENT[:, :])
                            ins.then_inc(st, 1)
                            c["t"] += 1
                            i += 1

                @blk.vector
                def _(v):
                    i = 0
                    for dc in range(DC):
                        for gi in range(ngrp):
                            v.wait_ge(st, t_base + i + 1)
                            v.tensor_copy(
                                LNFM[:, dc * BT + gi * G4 * P:
                                     dc * BT + (gi + 1) * G4 * P],
                                psT[:, 0:G4 * P]).then_inc(sv, 1)
                            c["v"] += 1
                            i += 1

            def proj_fills(blk, fills, dsts, sg_wait=None):
                t_base = c["t"]
                c_base = c["c"]
                v_pre = c["v"]

                @blk.tensor
                def _(t):
                    if sg_wait is not None:
                        t.wait_ge(sg, sg_wait)
                    t.wait_ge(sv, v_pre)
                    for i, (mms, w) in enumerate(fills):
                        if c_base + i - 1 > 0:
                            t.wait_ge(scm, c_base + i - 1)
                        dst = pp[i % 2][:, 0:w]
                        for k, (lh, rh) in enumerate(mms):
                            ins = t.matmul(dst, lh, rh, start=(k == 0),
                                           stop=(k == len(mms) - 1))
                        ins.then_inc(st, 1)
                        c["t"] += 1

                @blk.scalar
                def _(s):
                    for i, (mms, w) in enumerate(fills):
                        s.wait_ge(st, t_base + i + 1)
                        s.activation(dsts[i], pp[i % 2][:, 0:w],
                                     ACT.Copy).then_inc(scm, 1)
                        c["c"] += 1

            wgt = {}
            tr_prev = [0]
            for ob in range(OWNB):
                with nc.Block() as blk:
                    if ob == 0:
                        @blk.gpsimd
                        def _(g):
                            for h in range(2):
                                gdma(g, XQR[:, h * KN * D:(h + 1) * KN * D]
                                     .rearrange("p (n d) -> p n d", d=D),
                                     xq[h * BT:(h + 1) * BT, :].rearrange(
                                         "(n p) d -> p n d", p=P))
                                wgt[f"x{h}"] = c["g"]
                                if h == 0:
                                    gdma(g, PT[:, 0:DC * D].rearrange(
                                             "p (dc e) -> p dc e", e=D),
                                         wq.rearrange("(dc p) e -> p dc e",
                                                      p=P))
                                    wgt["q"] = c["g"]
                                else:
                                    gdma(g, TMP[:, :].rearrange(
                                             "p (n d) -> p n d", d=D),
                                         xb[0:BT, :].rearrange(
                                             "(n p) d -> p n d", p=P))
                                    wgt["xkv"] = c["g"]
                                    gdma(g, WkB[:, :].rearrange(
                                             "p (dc e) -> p dc e", e=D),
                                         wk.rearrange("(dc p) e -> p dc e",
                                                      p=P))
                                    wgt["k"] = c["g"]
                                    gdma(g, WvB[:, :].rearrange(
                                             "p (dc e) -> p dc e", e=D),
                                         wv.rearrange("(dc p) e -> p dc e",
                                                      p=P))
                                    wgt["v"] = c["g"]
                    ln_and_transpose(blk, wgt[f"x{ob}"],
                                     XQR[:, ob * KN * D:(ob + 1) * KN * D],
                                     KBUF,
                                     pv_guard=(tr_prev[0] if ob else None))
                    tr_prev[0] = c["t"]

                    fills, dsts = [], []
                    for dco in range(DC):
                        for qt in range(NQT):
                            fills.append((
                                [(PT[:, dci * D + dco * P:dci * D + (dco + 1) * P],
                                  LNFM[:, dci * BT + qt * QTW:
                                       dci * BT + (qt + 1) * QTW])
                                 for dci in range(DC)], QTW))
                            dsts.append(QFM[:, dco * OWN + ob * BT + qt * QTW:
                                             dco * OWN + ob * BT + (qt + 1) * QTW])
                    proj_fills(blk, fills, dsts,
                               sg_wait=(wgt["q"] if ob == 0 else None))
            tr_prev = None

            Sdone = {}
            PVdone = {}
            Mdone = {}
            gx_next = [0]
            with nc.Block() as blk:
              for j in range(NB):
                    XB = TMP if j % 2 == 0 else KBUF
                    LB = KBUF if j % 2 == 0 else TMP

                    if j >= 1:
                        @blk.gpsimd
                        def _(g):
                            g.wait_ge(st, Sdone[j - 1])
                            gdma(g, XB[:, :].rearrange("p (n d) -> p n d", d=D),
                                 xb[j * BT:(j + 1) * BT, :].rearrange(
                                     "(n p) d -> p n d", p=P))
                    g_x = c["g"] if j >= 1 else wgt["xkv"]
                    gx_next[0] = c["g"] + 16

                    @blk.sync
                    def _(s):
                        if j >= 1:
                            s.wait_ge(sv, Mdone[j - 1])
                        base = SMAX - j * BT - (KN - 1) * P
                        sdma(s, Mstep[:, :], mk[:, base:base + WS])
                    s_m = c["s"]

                    ln_and_transpose(blk, g_x, XB, LB,
                                     pv_guard=PVdone.get(j - 1),
                                     stats_done=(j >= 1))

                    fills, dsts = [], []
                    for dco in range(DC):
                        for kt in range(NQT):
                            fills.append((
                                [(WkB[:, dci * D + dco * P:dci * D + (dco + 1) * P],
                                  LNFM[:, dci * BT + kt * QTW:
                                       dci * BT + (kt + 1) * QTW])
                                 for dci in range(DC)], QTW))
                            dsts.append(LB[:, dco * BT + kt * QTW:
                                           dco * BT + (kt + 1) * QTW])
                    proj_fills(blk, fills, dsts,
                               sg_wait=(wgt["k"] if j == 0 else None))

                    fills, dsts = [], []
                    for kn in range(KN):
                        for dt in range(ND):
                            fills.append((
                                [(LNFM[:, dci * BT + kn * P:dci * BT + (kn + 1) * P],
                                  WvB[:, dci * D + dt * DTW:
                                       dci * D + (dt + 1) * DTW])
                                 for dci in range(DC)], DTW))
                            dsts.append(XB[:, kn * D + dt * DTW:
                                           kn * D + (dt + 1) * DTW])
                    proj_fills(blk, fills, dsts,
                               sg_wait=(wgt["v"] if j == 0 else None))

                    for qh in range(OWNB):
                        t_base = c["t"]
                        c_base = c["c"]
                        v_base = c["v"]

                        @blk.tensor
                        def _(t):
                            i = 0
                            for kc in range(KN):
                                for qt in range(NQT):
                                    if c_base + i - 1 > 0:
                                        t.wait_ge(scm, c_base + i - 1)
                                    dst = pp[i % 2][:, 0:QTW]
                                    for dc in range(DC):
                                        ins = t.matmul(
                                            dst,
                                            LB[:, dc * BT + kc * P:
                                               dc * BT + (kc + 1) * P],
                                            QFM[:, dc * OWN + qh * BT + qt * QTW:
                                                dc * OWN + qh * BT + (qt + 1) * QTW],
                                            start=(dc == 0), stop=(dc == DC - 1))
                                    ins.then_inc(st, 1)
                                    c["t"] += 1
                                    i += 1
                            if qh == OWNB - 1:
                                Sdone[j] = c["t"]

                        @blk.scalar
                        def _(s):
                            i = 0
                            for kc in range(KN):
                                for qt in range(NQT):
                                    s.wait_ge(st, t_base + i + 1)
                                    s.activation(
                                        PT[:, kc * BT + qt * QTW:
                                           kc * BT + (qt + 1) * QTW],
                                        pp[i % 2][:, 0:QTW], ACT.Exp,
                                        bias=NSH[:, :],
                                        scale=att_scale).then_inc(scm, 1)
                                    c["c"] += 1
                                    i += 1

                        @blk.vector
                        def _(v):
                            v.wait_ge(ssm, s_m)
                            for kc in range(KN):
                                v.wait_ge(scm, c_base + (kc + 1) * NQT)
                                moff = (KN - 1 - kc) * P + qh * BT
                                v.tensor_mul(PT[:, kc * BT:(kc + 1) * BT],
                                             PT[:, kc * BT:(kc + 1) * BT],
                                             Mstep[:, moff:moff + BT]
                                             ).then_inc(sv, 1)
                                c["v"] += 1
                            if qh == OWNB - 1:
                                Mdone[j] = c["v"]
                        pend_stats = None
                        if qh == OWNB - 1 and j + 1 < NB:
                            XB2 = TMP if (j + 1) % 2 == 0 else KBUF
                            pend_stats = ln_stats_chunk_ops(gx_next[0], XB2)
                        last_step = (j == NB - 1)

                        def o_sum(v, qcg):
                            sl = O_acc[:, qcg * D:(qcg + 1) * D]
                            v.reduce_sum(SS[:, qcg:qcg + 1], sl, axis=AX)
                            v.scalar_tensor_tensor(SQT[:, 0:D], sl, 1.0, sl,
                                                   op0=ALU.mult, op1=ALU.mult,
                                                   accum_out=SQ[:, qcg:qcg + 1])
                        v_mask = c["v"]

                        t_base2 = c["t"]
                        v_base2 = c["v"]

                        @blk.tensor
                        def _(t):
                            t.wait_ge(sv, v_mask)
                            for qc in range(KN):
                                qcg = qh * KN + qc
                                if qc >= 2:
                                    t.wait_ge(sv, v_base2 + 2 * qc - 2)
                                if qc >= 1:
                                    t.wait_ge(sv, v_base2 + 2 * qc - 1)
                                for kc in range(KN):
                                    lh = PT[:, kc * BT + qc * P:
                                            kc * BT + (qc + 1) * P]
                                    for dt in range(ND):
                                        t.matmul(
                                            pv[qc % 2][:, dt * DTW:(dt + 1) * DTW],
                                            lh, XB[:, kc * D + dt * DTW:
                                                   kc * D + (dt + 1) * DTW],
                                            start=(kc == 0), stop=(kc == KN - 1))
                                    ins = t.matmul(psL[:, qcg:qcg + 1], lh,
                                                   onescol[:, :], start=(kc == 0),
                                                   stop=(kc == KN - 1))
                                ins.then_inc(st, 1)
                                c["t"] += 1
                            if qh == OWNB - 1:
                                PVdone[j] = c["t"]

                        @blk.vector
                        def _(v):
                            for qc in range(KN):
                                qcg = qh * KN + qc
                                v.wait_ge(st, t_base2 + qc + 1)
                                v.tensor_add(Lb[:, qcg:qcg + 1],
                                             Lb[:, qcg:qcg + 1],
                                             psL[:, qcg:qcg + 1]).then_inc(sv, 1)
                                c["v"] += 1
                                for dt in range(ND):
                                    sl = O_acc[:, qcg * D + dt * DTW:
                                               qcg * D + (dt + 1) * DTW]
                                    ins = v.tensor_add(sl, sl,
                                                       pv[qc % 2][:, dt * DTW:
                                                                  (dt + 1) * DTW])
                                ins.then_inc(sv, 1)
                                c["v"] += 1
                                if pend_stats and qc >= 2:
                                    for _ in range(3):
                                        if pend_stats:
                                            pend_stats.pop(0)(v)
                                if last_step and qc >= 1:
                                    o_sum(v, qh * KN + qc - 1)
                            if pend_stats is not None:
                                while pend_stats:
                                    pend_stats.pop(0)(v)
                                ln_stats_tail(v)
                            if last_step:
                                o_sum(v, qh * KN + KN - 1)
                                if qh == OWNB - 1:
                                    v.wait_ge(sv, c["v"])
                                    v.engine_nop().then_inc(sv, 1)
                                    c["v"] += 1
                                    v.wait_ge(sv, c["v"])
                                    ck = slice(0, OC)
                                    vhop(v, v.tensor_scalar_mul(MEANB[:, ck],
                                                                SS[:, ck], invD))
                                    vhop(v, v.tensor_mul(T1B[:, ck], MEANB[:, ck],
                                                         MEANB[:, ck]))
                                    vhop(v, v.scalar_tensor_tensor(
                                        VARPB[:, ck], SQ[:, ck], invD, T1B[:, ck],
                                        op0=ALU.mult, op1=ALU.subtract))
                                    vhop(v, v.tensor_mul(T1B[:, ck], Lb[:, ck],
                                                         Lb[:, ck]))
                                    v.scalar_tensor_tensor(
                                        VARPB[:, ck], T1B[:, ck], LN_EPS,
                                        VARPB[:, ck], op0=ALU.mult,
                                        op1=ALU.add).then_inc(sv, 1)
                                    c["v"] += 1

        with contextlib.ExitStack() as _esB:
            _b = _esB.enter_context
            X3FM = _b(nc.sbuf_tensor("X3FM", [P, DC * OWN], BF16))
            JIOT = _b(nc.sbuf_tensor("JIOT", [P, OWN], F32))
            PRMB = _b(nc.sbuf_tensor("PRMB", [P, 2 * 512], BF16))
            STG = _b(nc.sbuf_tensor("STG", [P, 2 * D], F32))
            TRX = _b(nc.sbuf_tensor("TRX", [P, D], BF16))
            PRB = _b(nc.sbuf_tensor("PRB", [P, NE], F32))
            PRN = _b(nc.sbuf_tensor("PRN", [P, NE], F32))
            M8 = _b(nc.sbuf_tensor("M8", [P, NE], F32))
            SMB = _b(nc.sbuf_tensor("SMB", [P, 1], F32))
            NMB = _b(nc.sbuf_tensor("NMB", [P, 1], F32))
            RSB = _b(nc.sbuf_tensor("RSB", [P, 1], F32))
            psTE = _b(nc.psum_tensor("psTE", [P, 512], BF16))
            ph0 = _b(nc.psum_tensor("ph0", [P, TGW], F32))
            ph1 = _b(nc.psum_tensor("ph1", [P, TGW], F32))
            psR = _b(nc.psum_tensor("psR", [P, NE], F32))
            py0 = _b(nc.psum_tensor("py0", [P, 512], F32))
            py1 = _b(nc.psum_tensor("py1", [P, 512], F32))
            py2 = _b(nc.psum_tensor("py2", [P, 512], F32))
            py3 = _b(nc.psum_tensor("py3", [P, 512], F32))
            ph = [ph0, ph1]
            py = [py0, py1, py2, py3]
            TRX_cur[0] = TRX

            X1 = XQR
            with (nc.sbuf_tensor("X3LO", [P, DC * OWN], BF16) as X3LO,
                  nc.sbuf_tensor("X2TM", [P, OC * D], BF16) as X2TM,
                  nc.sbuf_tensor("TRXF", [P, D], F32) as TRXF,
                  nc.sbuf_tensor("TRXLO", [P, D], BF16) as TRXLO):
                _blkE = nc.Block()
                blk = _blkE.__enter__()
                if True:
                    @blk.gpsimd
                    def _(g):
                        g.wait_ge(st, PVdone[NB - 1])
                        g.iota(JIOT[:, :], [[1, OWN]], base=0,
                               channel_multiplier=0,
                               allow_small_or_imprecise_dtypes=True
                               ).then_inc(sgc, 1)
                        c["gc"] += 1

                    @blk.scalar
                    def _(s):
                        rstd_scalar(s, OC)

                    @blk.vector
                    def _(v):
                        nbias_vec(v, OC)

                if True:
                    v_base = c["v"]
                    c_base = c["c"]
                    ap_inc = {}

                    @blk.scalar
                    def _(s):
                        s.wait_ge(sv, v_base)
                        for tc in range(OC):
                            if tc >= 2:
                                s.wait_ge(sv, v_base + tc - 1)
                            s.activation(STG[:, (tc % 2) * D:(tc % 2 + 1) * D],
                                         O_acc[:, tc * D:(tc + 1) * D], ACT.Identity,
                                         bias=NBSB[:, tc:tc + 1],
                                         scale=RSTDB[:, tc:tc + 1]
                                         ).then_inc(scm, 1)
                            c["c"] += 1
                            ap_inc[tc] = c["c"]
                            if tc >= 2:
                                s.wait_ge(sv, v_base + tc - 1)
                                s.activation(TRXF[:, 0:D],
                                             X1[:, (tc - 2) * D:(tc - 1) * D],
                                             ACT.Square,
                                             accum_out=SQ[:, tc - 2:tc - 1]
                                             ).then_inc(scm, 1)
                                c["c"] += 1
                        for tc in (OC - 2, OC - 1):
                            s.wait_ge(sv, v_base + tc + 1)
                            s.activation(TRXF[:, 0:D],
                                         X1[:, tc * D:(tc + 1) * D], ACT.Square,
                                         accum_out=SQ[:, tc:tc + 1]
                                         ).then_inc(scm, 1)
                            c["c"] += 1

                    @blk.vector
                    def _(v):
                        for tc in range(OC):
                            v.wait_ge(scm, ap_inc[tc])
                            v.scalar_tensor_tensor(
                                X1[:, tc * D:(tc + 1) * D],
                                XQR[:, tc * D:(tc + 1) * D], 1.0,
                                STG[:, (tc % 2) * D:(tc % 2 + 1) * D],
                                op0=ALU.mult, op1=ALU.add,
                                accum_out=SS[:, tc:tc + 1]).then_inc(sv, 1)
                            c["v"] += 1
                        v.wait_ge(scm, c["c"])
                        v.wait_ge(sv, c["v"])
                        v.engine_nop().then_inc(sv, 1)
                        c["v"] += 1
                        v.wait_ge(sv, c["v"])
                        ck = slice(0, OC)
                        vhop(v, v.tensor_scalar_mul(MEANB[:, ck], SS[:, ck],
                                                    invD))
                        vhop(v, v.tensor_mul(T1B[:, ck], MEANB[:, ck],
                                             MEANB[:, ck]))
                        vhop(v, v.scalar_tensor_tensor(VARPB[:, ck], SQ[:, ck],
                                                       invD, T1B[:, ck],
                                                       op0=ALU.mult,
                                                       op1=ALU.subtract))
                        v.tensor_scalar_add(VARPB[:, ck], VARPB[:, ck],
                                            LN_EPS).then_inc(sv, 1)
                        c["v"] += 1

                if True:
                    @blk.scalar
                    def _(s):
                        rstd_scalar(s, OC)

                    @blk.vector
                    def _(v):
                        nbias_vec(v, OC)

                if True:
                    v_l3 = c["v"]
                    c_base = c["c"]
                    t_base = c["t"]
                    v_base = c["v"]
                    NG2 = 2 * NG

                    @blk.scalar
                    def _(s):
                        s.wait_ge(sv, v_base)
                        for tc in range(OC):
                            if tc >= 1:
                                s.wait_ge(sv, v_base + (tc - 1) * (NG2 + 1) + 1)
                            s.activation(TRXF[:, 0:D], X1[:, tc * D:(tc + 1) * D],
                                         ACT.Identity, bias=NBSB[:, tc:tc + 1],
                                         scale=RSTDB[:, tc:tc + 1]
                                         ).then_inc(scm, 1)
                            c["c"] += 1

                    @blk.vector
                    def _(v):
                        x3r = X3FM[:, :].rearrange("p (dc q) -> p dc q", q=OWN)
                        x3l = X3LO[:, :].rearrange("p (dc q) -> p dc q", q=OWN)
                        for tc in range(OC):
                            v.wait_ge(scm, c_base + tc + 1)
                            if tc >= 1:
                                v.wait_ge(st, t_base + tc * NG2)
                            v.tensor_copy(TRX[:, 0:D], TRXF[:, 0:D])
                            v.tensor_copy(X2TM[:, tc * D:(tc + 1) * D],
                                          TRXF[:, 0:D])
                            v.tensor_sub(TRXLO[:, 0:D], TRXF[:, 0:D],
                                         TRX[:, 0:D]).then_inc(sv, 1)
                            c["v"] += 1
                            for half in range(2):
                                dstr = x3r if half == 0 else x3l
                                for ng in range(NG):
                                    fi = tc * NG2 + half * NG + ng
                                    v.wait_ge(st, t_base + fi + 1)
                                    v.tensor_copy(
                                        dstr[:, ng * GD:(ng + 1) * GD,
                                             tc * P:(tc + 1) * P],
                                        psTE[:, 0:GD * P].rearrange(
                                            "p (k q) -> p k q", q=P)
                                    ).then_inc(sv, 1)
                                    c["v"] += 1

                    @blk.tensor
                    def _(t):
                        for tc in range(OC):
                            for half in range(2):
                                src = TRX if half == 0 else TRXLO
                                for ng in range(NG):
                                    fi = tc * NG2 + half * NG + ng
                                    t.wait_ge(sv, v_base + tc * (NG2 + 1) + 1
                                              + (fi - tc * NG2))
                                    for k4 in range(GD):
                                        dc = ng * GD + k4
                                        ins = t.transpose(
                                            psTE[:, k4 * P:(k4 + 1) * P],
                                            src[:, dc * P:(dc + 1) * P],
                                            IDENT[:, :])
                                    ins.then_inc(st, 1)
                                    c["t"] += 1
                if True:
                    t_base = c["t"]
                    v_base = c["v"]
                    c_base = c["c"]

                    @blk.tensor
                    def _(t):
                        for tc in range(OC):
                            t.wait_ge(sv, v_l3 + (tc + 1) * (NG2 + 1))
                            if tc >= 1:
                                t.wait_ge(sv, v_base + tc * 6)
                            for dc in range(DC):
                                t.matmul(psR[:, :],
                                         X3FM[:, dc * OWN + tc * P:
                                              dc * OWN + (tc + 1) * P],
                                         WrB[:, dc * NE:(dc + 1) * NE],
                                         start=(dc == 0), stop=False)
                            for dc in range(DC):
                                t.matmul(psR[:, :],
                                         X3LO[:, dc * OWN + tc * P:
                                              dc * OWN + (tc + 1) * P],
                                         WrB[:, dc * NE:(dc + 1) * NE],
                                         start=False, stop=False)
                            for dc in range(DC):
                                t.matmul(psR[:, :],
                                         X3FM[:, dc * OWN + tc * P:
                                              dc * OWN + (tc + 1) * P],
                                         WrLO[:, dc * NE:(dc + 1) * NE],
                                         start=False, stop=False)
                            t.matmul(psR[:, :], ones1[:, :], RB[:, :],
                                     start=False, stop=False)
                            t.matmul(psR[:, :], ones1[:, :], RBLO[:, :],
                                     start=False, stop=True).then_inc(st, 1)
                            c["t"] += 1

                    @blk.vector
                    def _(v):
                        for tc in range(OC):
                            v.wait_ge(st, t_base + tc + 1)
                            v.reduce_max(NMB[:, :], psR[:, :], axis=AX,
                                         negate=True).then_inc(sv, 1)
                            c["v"] += 1
                            v.wait_ge(scm, c_base + tc + 1)
                            vhop(v, v.reciprocal(RSB[:, :], SMB[:, :]))
                            vhop(v, v.tensor_scalar_mul(PRN[:, :], PRB[:, :],
                                                        RSB[:, :]))
                            vhop(v, v.reduce_max(NMB[:, :], PRN[:, :], axis=AX))
                            vhop(v, v.tensor_single_scalar(M8[:, :], PRN[:, :],
                                                           NMB[:, :],
                                                           op=ALU.is_equal))
                            v.tensor_mul(GATES[:, tc * NE:(tc + 1) * NE],
                                         M8[:, :], PRN[:, :]).then_inc(sv, 1)
                            c["v"] += 1
                        v.memset(O_acc[:, :], 0.0)

                    @blk.scalar
                    def _(s):
                        for tc in range(OC):
                            s.wait_ge(sv, v_base + tc * 6 + 1)
                            s.activation(PRB[:, :], psR[:, :], ACT.Exp,
                                         bias=NMB[:, :],
                                         accum_out=SMB[:, :]).then_inc(scm, 1)
                            c["c"] += 1
                _blkE.__exit__(None, None, None)

                with nc.Block() as blk:
                    @blk.vector
                    def _(v):
                        vhop(v, v.tensor_single_scalar(IND[:, :], GATES[:, :], 0.0,
                                                       op=ALU.is_gt))
                        for tc in range(OC):
                            ins = v.reduce_sum(GSEL[:, tc:tc + 1],
                                               GATES[:, tc * NE:(tc + 1) * NE],
                                               axis=AX)
                        ins.then_inc(sv, 1)
                        c["v"] += 1
                    V1 = c["v"]

                    @blk.tensor
                    def _(t):
                        t.wait_ge(sv, V1)
                        for tc in range(OC):
                            ins = t.matmul(py1[0:NE, tc:tc + 1],
                                           IND[:, tc * NE:(tc + 1) * NE],
                                           onescol[:, :], start=(tc == 0),
                                           stop=(tc == OC - 1))
                        ins.then_inc(st, 1)
                        c["t"] += 1
                    T1 = c["t"]

                    @blk.vector
                    def _(v):
                        v.wait_ge(st, T1)
                        vhop(v, v.tensor_copy(CNT[:, :], py1[0:NE, 0:OC]))
                        vhop(v, v.tensor_tensor_scan(CUMI[:, :], ONES8R[:, :],
                                                     CNT[:, :], 0.0, op0=ALU.mult,
                                                     op1=ALU.add))
                        vhop(v, v.tensor_sub(CUMTOT[:, :], CUMI[:, :], CNT[:, :]))
                        vhop(v, v.tensor_copy(TOT[:, :], CUMI[:, OC - 1:OC]))
                    V2 = c["v"]

                    @blk.tensor
                    def _(t):
                        t.wait_ge(sv, V2)
                        t.matmul(py2[0:NE, 0:1], LTS8F[:, :], TOT[:, :],
                                 start=True, stop=True).then_inc(st, 1)
                        c["t"] += 1
                    T2 = c["t"]

                    @blk.vector
                    def _(v):
                        v.wait_ge(st, T2)
                        vhop(v, v.tensor_copy(BASEC[:, :], py2[0:NE, 0:1]))
                        vhop(v, v.tensor_scalar_add(CUMTOT[:, :], CUMTOT[:, :],
                                                    BASEC[:, :]))
                        vhop(v, v.tensor_add(ENDC[:, :], BASEC[:, :], TOT[:, :]))
                        for tg in range(NTG):
                            v.tensor_single_scalar(T81[:, :], BASEC[:, :],
                                                   float((tg + 1) * TGW + KMARG),
                                                   op=ALU.is_lt)
                            vhop(v, v.tensor_single_scalar(T82[:, :], ENDC[:, :],
                                                           float(tg * TGW - KMARG),
                                                           op=ALU.is_gt))
                            vhop(v, v.tensor_mul(FL8[:, tg:tg + 1], T81[:, :],
                                                 T82[:, :]))
                        vhop(v, v.tensor_copy(FL8I[:, :], FL8[:, :]))
                        vhop(v, v.reduce_max(T81[:, :], FL8[:, :], axis=AX))
                        vhop(v, v.tensor_copy(EFL8I[:, :], T81[:, :]))
                    V3 = c["v"]

                    @blk.sync
                    def _(sy):
                        sy.wait_ge(sv, V3)
                        sdma(sy, rowd, CUMTOT[:, :])
                        sdma(sy, flid, FL8I[:, :])
                        sdma(sy, efld, EFL8I[:, :])
                        sy.wait_ge(ssm, c["s"])
                        sy.dma_start(ROWSF[0:1, :].rearrange(
                            "o (t e) -> o t e", e=NE),
                            rowd.rearrange("e t -> t e"),
                            allow_slow_non_contiguous=True).then_inc(ssm, 16)
                        c["s"] += 16
                        sdma(sy, FLROW[0:1, :].rearrange("o (e t) -> o e t",
                                                         t=NTG), flid)
                        sdma(sy, EFROW[0:1, :].rearrange("o (e t) -> o e t", t=1),
                             efld)
                    S1 = c["s"]

                    @blk.vector
                    def _(v):
                        v.wait_ge(ssm, S1)
                        vhop(v, v.tensor_copy(ROWSH[:, :], ROWSF[:, :]))
                        vhop(v, v.tensor_sub(ROWSL[:, :], ROWSF[:, :],
                                             ROWSH[:, :]))
                    V4 = c["v"]
                    t_base3 = c["t"]
                    v_base4 = c["v"]

                    @blk.tensor
                    def _(t):
                        t.wait_ge(sv, V4)
                        for tc in range(OC):
                            if tc >= 2:
                                t.wait_ge(sv, v_base4 + tc - 1)
                            pb = [py0, py2][tc % 2]
                            t.matmul(pb[0:P, 0:NE], LTS[:, :],
                                     IND[:, tc * NE:(tc + 1) * NE],
                                     start=True, stop=False)
                            t.matmul(pb[0:P, 0:NE], ones1[:, :],
                                     ROWSH[0:1, tc * NE:(tc + 1) * NE],
                                     start=False, stop=False)
                            t.matmul(pb[0:P, 0:NE], ones1[:, :],
                                     ROWSL[0:1, tc * NE:(tc + 1) * NE],
                                     start=False, stop=True).then_inc(st, 1)
                            c["t"] += 1

                    @blk.vector
                    def _(v):
                        for tc in range(OC):
                            v.wait_ge(st, t_base3 + tc + 1)
                            pb = [py0, py2][tc % 2]
                            v.scalar_tensor_tensor(
                                T8[:, :], pb[0:P, 0:NE], 1.0,
                                IND[:, tc * NE:(tc + 1) * NE], op0=ALU.mult,
                                op1=ALU.mult,
                                accum_out=DESTF[:, tc:tc + 1]).then_inc(sv, 1)
                            c["v"] += 1
                        v.wait_ge(sv, c["v"])
                        v.engine_nop().then_inc(sv, 1)
                        c["v"] += 1

                with nc.Block() as blk:
                    t0 = c["t"]

                    @blk.tensor
                    def _(t):
                        t.matmul(psR[:, 0:NE], ones1[:, :], ROWSH[0:1, 0:NE],
                                 start=True, stop=False)
                        t.matmul(psR[:, 0:NE], ones1[:, :], ROWSL[0:1, 0:NE],
                                 start=False, stop=True).then_inc(st, 1)
                        c["t"] += 1
                    T1p = c["t"]

                    @blk.vector
                    def _(v):
                        v.wait_ge(st, T1p)
                        vhop(v, v.tensor_copy(BASEB[:, :], psR[0:P, 0:NE]))
                        vhop(v, v.tensor_single_scalar(ESRT[:, :], TOKC[:, :],
                                                       BASEB[:, 0:1],
                                                       op=ALU.is_ge))
                        for e2 in range(1, NE):
                            vhop(v, v.tensor_single_scalar(T8B[:, :], TOKC[:, :],
                                                           BASEB[:, e2:e2 + 1],
                                                           op=ALU.is_ge))
                            vhop(v, v.tensor_add(ESRT[:, :], ESRT[:, :],
                                                 T8B[:, :]))
                        v.tensor_scalar_add(ESRT[:, :], ESRT[:, :],
                                            -1.0).then_inc(sv, 1)
                        c["v"] += 1

                with nc.Block() as blk:
                    NST = OWN // 512
                    NSUB = 512 // P
                    DCH = DC // 2
                    t_b = c["t"]
                    v_b = c["v"]
                    c_b = c["c"]

                    VPP = OC + NSUB // 2

                    @blk.vector
                    def _(v):
                        v.wait_ge(sgc, 2)
                        k = 0
                        for stile in range(NST):
                            for half in range(2):
                                p = stile * 2 + half
                                for tc in range(OC):
                                    if k >= 2:
                                        v.wait_ge(st, t_b + k - 1)
                                    v.tensor_single_scalar(
                                        PRMB[:, (k % 2) * 512:(k % 2) * 512 + 512],
                                        JIOT[:, stile * 512:(stile + 1) * 512],
                                        DESTF[:, tc:tc + 1],
                                        op=ALU.is_equal).then_inc(sv, 1)
                                    c["v"] += 1
                                    k += 1
                                v.wait_ge(st, t_b + (p + 1) * OC)
                                for sb in range(NSUB // 2):
                                    sub = half * (NSUB // 2) + sb
                                    v.tensor_copy(
                                        INVF[:, stile * NSUB + sub:
                                             stile * NSUB + sub + 1],
                                        ph[sb][0:P, 0:1]).then_inc(sv, 1)
                                    c["v"] += 1

                    @blk.tensor
                    def _(t):
                        k = 0
                        for stile in range(NST):
                            for half in range(2):
                                p = stile * 2 + half
                                for tc in range(OC):
                                    t.wait_ge(sv, v_b + p * VPP + tc + 1)
                                    if tc == 0:
                                        if p > 0:
                                            t.wait_ge(scm, c_b + p * DCH)
                                            t.wait_ge(sv, v_b + p * VPP)
                                    sl = PRMB[:, (k % 2) * 512:
                                              (k % 2) * 512 + 512]
                                    for dcq in range(DCH):
                                        dc = half * DCH + dcq
                                        t.matmul(py[dcq % 4][0:P, 0:512],
                                                 X2TM[:, tc * D + dc * P:
                                                      tc * D + (dc + 1) * P],
                                                 sl, start=(tc == 0),
                                                 stop=(tc == OC - 1))
                                    for sb in range(NSUB // 2):
                                        sub = half * (NSUB // 2) + sb
                                        slc = PRMB[:, (k % 2) * 512 + sub * P:
                                                   (k % 2) * 512 + (sub + 1) * P]
                                        t.matmul(ph[sb][0:P, 0:1], slc,
                                                 TOKCH[:, tc:tc + 1],
                                                 start=(tc == 0), stop=False)
                                        ins = t.matmul(ph[sb][0:P, 0:1], slc,
                                                       TOKCL[:, tc:tc + 1],
                                                       start=False,
                                                       stop=(tc == OC - 1))
                                    ins.then_inc(st, 1)
                                    c["t"] += 1
                                    k += 1

                    @blk.scalar
                    def _(s):
                        for stile in range(NST):
                            for half in range(2):
                                p = stile * 2 + half
                                s.wait_ge(st, t_b + (p + 1) * OC)
                                for dcq in range(DCH):
                                    dc = half * DCH + dcq
                                    s.activation(
                                        X3FM[:, dc * OWN + stile * 512:
                                             dc * OWN + (stile + 1) * 512],
                                        py[dcq % 4][0:P, 0:512],
                                        ACT.Copy).then_inc(scm, 1)
                                    c["c"] += 1

            with contextlib.ExitStack() as _esW:
                _w = _esW.enter_context
                NWB = 2
                W1B = [_w(nc.sbuf_tensor(f"W1B{i}", [P, DC * FSUB], BF16))
                       for i in range(NWB)]
                W2B = [_w(nc.sbuf_tensor(f"W2B{i}", [P, FCS * D], BF16))
                       for i in range(NWB)]
                HB = _w(nc.sbuf_tensor("HB", [P, FCS * TGW], BF16))
                B2B = _w(nc.sbuf_tensor("B2B", [1, D], BF16))

                UPT = NTG * (FCS + 1)
                t0f = c["t"]
                gwu = {}
                with nc.Block() as blk:
                    @blk.gpsimd
                    def _(g):
                        for u in range(NE * SC):
                            eu, scu = divmod(u, SC)
                            if u >= NWB:
                                g.wait_ge(st, t0f + (u - NWB + 1) * UPT)
                            wb = u % NWB

                            def _wload(gg):
                                gg.dma_start(
                                    W1B[wb][:, :].rearrange(
                                        "p (dc f) -> p dc f", f=FSUB),
                                    w1[eu:eu + 1, :,
                                       scu * FSUB:(scu + 1) * FSUB].rearrange(
                                        "o (dc p) f -> p (o dc) f", p=P)
                                ).then_inc(sg, 16)
                                gg.dma_start(
                                    W2B[wb][:, :].rearrange(
                                        "p (fc d) -> p fc d", d=D),
                                    w2[eu:eu + 1,
                                       scu * FSUB:(scu + 1) * FSUB, :].rearrange(
                                        "o (fc p) d -> p (o fc) d", p=P)
                                ).then_inc(sg, 16)
                            if USE_BRANCHES and BR_G:
                                with g.register(f"rgw{u}") as rgw:
                                    g.reg_load(rgw, EFROW[0:1, eu:eu + 1])
                                    with g.If_ne(rgw, 0):
                                        _wload(g)
                                    with g.If_eq(rgw, 0):
                                        g.dma_start(DUMB[:, :], b2[0:1, 0:8]
                                                    ).then_inc(sg, 16)
                                        g.dma_start(DUMB[:, :], b2[0:1, 0:8]
                                                    ).then_inc(sg, 16)
                            else:
                                _wload(g)
                            c["g"] += 32
                            gwu[u] = c["g"]
                            if u == 0:
                                gdma(g, B2B[:, :], b2[0:1, :])
                            if scu == min(1, SC - 1) and eu + 1 < NE:
                                g.wait_ge(st, t0f + (eu * SC + 1) * UPT)
                                gdma(g, B2B[:, :], b2[eu + 1:eu + 2, :])

                    for e in range(NE):
                        t_e0 = c["t"]
                        c_e0 = c["c"]
                        v_e0 = c["v"]

                        @blk.tensor
                        def _(t):
                            t.wait_ge(sg, gwu[e * SC])
                            if USE_BRANCHES and BR_T:
                                efv = nc.values_load(
                                    EFROW[0:1, e:e + 1],
                                    engines=[mybir.EngineType.PE])
                                arm_iter = t.Switch(efv, 2)
                                _rgt = t.register(f"rtf{e}")
                                rtf = _rgt.__enter__()
                            else:
                                arm_iter = iter([1])
                            for arm in arm_iter:
                                if arm == 0:
                                    t.sem_inc(st, SC * UPT)
                                    continue
                                for sci in range(SC):
                                    i = e * SC + sci
                                    buf = i % NWB
                                    c_base = c_e0 + sci * NTG * FCS
                                    v_base = v_e0 + sci * NTG * (TGC + 1)
                                    if sci >= 1:
                                        t.wait_ge(sg, gwu[i])
                                    for tg in range(NTG):
                                        if USE_BRANCHES and BR_T:
                                            t.reg_load(rtf,
                                                       FLROW[0:1, e * NTG + tg:
                                                             e * NTG + tg + 1])
                                            t.drain()
                                            _if = t.If_ne(rtf, 0)
                                            _if.__enter__()
                                        for fc in range(FCS):
                                            k = tg * FCS + fc
                                            if c_base + k - 1 > 0:
                                                t.wait_ge(scm, c_base + k - 1)
                                            dst = ph[k % 2][:, :]
                                            for dci in range(DC):
                                                ins = t.matmul(
                                                    dst,
                                                    W1B[buf][:,
                                                             dci * FSUB + fc * P:
                                                             dci * FSUB
                                                             + (fc + 1) * P],
                                                    X3FM[:,
                                                         dci * OWN + tg * TGW:
                                                         dci * OWN
                                                         + (tg + 1) * TGW],
                                                    start=(dci == 0),
                                                    stop=(dci == DC - 1))
                                            ins.then_inc(st, 1)
                                            c["t"] += 1
                                        t.wait_ge(scm, c_base + (tg + 1) * FCS)
                                        if sci > 0 or tg >= 1:
                                            t.wait_ge(sv,
                                                      v_base + tg * (TGC + 1))
                                        elif e >= 1:
                                            t.wait_ge(sv, v_e0)
                                        if sci == 0:
                                            for tc in range(TGC):
                                                for dt in range(ND):
                                                    t.matmul(
                                                        py[tc * ND + dt][0:P,
                                                                         0:DTW],
                                                        ones1[:, :],
                                                        B2B[:, dt * DTW:
                                                            (dt + 1) * DTW],
                                                        start=True, stop=False)
                                        for fc in range(FCS):
                                            for tc in range(TGC):
                                                lh = HB[:, fc * TGW + tc * P:
                                                        fc * TGW + (tc + 1) * P]
                                                for dt in range(ND):
                                                    ins = t.matmul(
                                                        py[tc * ND + dt][0:P,
                                                                         0:DTW],
                                                        lh,
                                                        W2B[buf][:,
                                                                 fc * D
                                                                 + dt * DTW:
                                                                 fc * D
                                                                 + (dt + 1)
                                                                 * DTW],
                                                        start=(sci > 0
                                                               and fc == 0),
                                                        stop=(fc == FCS - 1))
                                        ins.then_inc(st, 1)
                                        c["t"] += 1
                                        if USE_BRANCHES and BR_T:
                                            _if.__exit__(None, None, None)
                                            with t.If_eq(rtf, 0):
                                                t.sem_inc(st, FCS + 1)
                            if USE_BRANCHES and BR_T:
                                _rgt.__exit__(None, None, None)

                        @blk.scalar
                        def _(s):
                            if USE_BRANCHES and BR_C:
                                _rgc = s.register(f"rcf{e}")
                                rcf = _rgc.__enter__()
                            for sci in range(SC):
                                t_base = t_e0 + sci * UPT
                                for tg in range(NTG):
                                    if USE_BRANCHES and BR_C:
                                        s.reg_load(rcf,
                                                   FLROW[0:1, e * NTG + tg:
                                                         e * NTG + tg + 1])
                                        _if = s.If_ne(rcf, 0)
                                        _if.__enter__()
                                    for fc in range(FCS):
                                        k = tg * FCS + fc
                                        s.wait_ge(st, t_base + tg + k + 1)
                                        bc = e * FCT + sci * FCS + fc
                                        s.activation(
                                            HB[:, fc * TGW:(fc + 1) * TGW],
                                            ph[k % 2][:, :], ACT.Relu,
                                            bias=B1all[:, bc:bc + 1]
                                        ).then_inc(scm, 1)
                                        c["c"] += 1
                                    if USE_BRANCHES and BR_C:
                                        _if.__exit__(None, None, None)
                                        with s.If_eq(rcf, 0):
                                            s.sem_inc(scm, FCS)
                            if USE_BRANCHES and BR_C:
                                _rgc.__exit__(None, None, None)

                        @blk.vector
                        def _(v):
                            if USE_BRANCHES and BR_V:
                                _rgv = v.register(f"rvf{e}")
                                rvf = _rgv.__enter__()
                            for sci in range(SC):
                                t_base = t_e0 + sci * UPT
                                for tg in range(NTG):
                                    if USE_BRANCHES and BR_V:
                                        v.reg_load(rvf,
                                                   FLROW[0:1, e * NTG + tg:
                                                         e * NTG + tg + 1])
                                        _if = v.If_ne(rvf, 0)
                                        _if.__enter__()
                                    v.wait_ge(st,
                                              t_base + (tg + 1) * (FCS + 1))
                                    for tc in range(TGC):
                                        tcg = tg * TGC + tc
                                        vhop(v, v.tensor_single_scalar(
                                            SCL[:, tc:tc + 1],
                                            ESRT[:, tcg:tcg + 1], float(e),
                                            op=ALU.is_equal))
                                        for dt in range(ND):
                                            sl = O_acc[:, tcg * D + dt * DTW:
                                                       tcg * D + (dt + 1) * DTW]
                                            ins = v.scalar_tensor_tensor(
                                                sl, py[tc * ND + dt][0:P, 0:DTW],
                                                SCL[:, tc:tc + 1],
                                                sl, op0=ALU.mult, op1=ALU.add)
                                    ins.then_inc(sv, 1)
                                    c["v"] += 1
                                    if USE_BRANCHES and BR_V:
                                        _if.__exit__(None, None, None)
                                        with v.If_eq(rvf, 0):
                                            v.sem_inc(sv, TGC + 1)
                            if USE_BRANCHES and BR_V:
                                _rgv.__exit__(None, None, None)


                with nc.Block() as blk:
                    NTT = OWN // 512
                    SCC = OWN // P
                    t_b2 = c["t"]
                    v_b2 = c["v"]
                    c_b2 = c["c"]

                    @blk.vector
                    def _(v):
                        k = 0
                        for tt in range(NTT):
                            for dt in range(ND):
                                for sc in range(SCC):
                                    if k >= 2:
                                        v.wait_ge(st, t_b2 + k - 1)
                                    v.tensor_single_scalar(
                                        PRMB[:, (k % 2) * 512:
                                             (k % 2) * 512 + 512],
                                        JIOT[:, tt * 512:(tt + 1) * 512],
                                        INVF[:, sc:sc + 1],
                                        op=ALU.is_equal).then_inc(sv, 1)
                                    c["v"] += 1
                                    k += 1

                    @blk.tensor
                    def _(t):
                        k = 0
                        for tt in range(NTT):
                            for dt in range(ND):
                                p = tt * ND + dt
                                for sc in range(SCC):
                                    t.wait_ge(sv, v_b2 + k + 1)
                                    if sc == 0 and p >= 1:
                                        t.wait_ge(scm, c_b2 + p * 4)
                                    sl = PRMB[:, (k % 2) * 512:
                                              (k % 2) * 512 + 512]
                                    for tq in range(4):
                                        ins = t.matmul(
                                            py[tq][0:P, 0:DTW],
                                            sl[:, tq * P:(tq + 1) * P],
                                            O_acc[:, sc * D + dt * DTW:
                                                  sc * D + dt * DTW + DTW],
                                            start=(sc == 0),
                                            stop=(sc == SCC - 1))
                                    ins.then_inc(st, 1)
                                    c["t"] += 1
                                    k += 1

                    @blk.scalar
                    def _(s):
                        for tt in range(NTT):
                            for dt in range(ND):
                                p = tt * ND + dt
                                s.wait_ge(st, t_b2 + (p + 1) * SCC)
                                for tq in range(4):
                                    tcg = tt * 4 + tq
                                    s.activation(
                                        X3FM[:, tcg * D + dt * DTW:
                                             tcg * D + dt * DTW + DTW],
                                        py[tq][0:P, 0:DTW], ACT.Identity,
                                        scale=GSEL[:, tcg:tcg + 1]
                                    ).then_inc(scm, 1)
                                    c["c"] += 1

                    c_u0 = c_b2
                    v_l40 = c["v"]

                    @blk.vector
                    def _(v):
                        for tt in range(NTT):
                            v.wait_ge(scm, c_u0 + (tt + 1) * ND * 4)
                            for tq in range(4):
                                i2 = tt * 4 + tq
                                sl = X3FM[:, i2 * D:(i2 + 1) * D]
                                v.reduce_sum(SS[:, i2:i2 + 1], sl, axis=AX)
                                v.scalar_tensor_tensor(
                                    TRX[:, 0:D], sl, 1.0, sl, op0=ALU.mult,
                                    op1=ALU.mult, accum_out=SQ[:, i2:i2 + 1])
                        v.wait_ge(sv, c["v"])
                        v.engine_nop().then_inc(sv, 1)
                        c["v"] += 1
                        v.wait_ge(sv, c["v"])
                        ck = slice(0, OC)
                        vhop(v, v.tensor_scalar_mul(MEANB[:, ck], SS[:, ck],
                                                    invD))
                        vhop(v, v.tensor_mul(T1B[:, ck], MEANB[:, ck],
                                             MEANB[:, ck]))
                        vhop(v, v.scalar_tensor_tensor(VARPB[:, ck], SQ[:, ck],
                                                       invD, T1B[:, ck],
                                                       op0=ALU.mult,
                                                       op1=ALU.subtract))
                        v.tensor_scalar_add(VARPB[:, ck], VARPB[:, ck],
                                            LN_EPS).then_inc(sv, 1)
                        c["v"] += 1

                    @blk.scalar
                    def _(s):
                        rstd_scalar(s, OC)

                    @blk.vector
                    def _(v):
                        nbias_vec(v, OC)

                    c_base = c["c"]
                    v_base = c["v"]
                    s_base = c["s"]

                    @blk.sync
                    def _(s):
                        for tc in range(OC):
                            s.wait_ge(sv, v_base + tc + 1)
                            sdma(s, yo[tc * P:(tc + 1) * P, :],
                                 STG[:, (tc % 2) * D:(tc % 2 + 1) * D])
                        s.wait_ge(ssm, c["s"])

                    @blk.scalar
                    def _(s):
                        s.wait_ge(sv, v_base)
                        for tc in range(OC):
                            if tc >= 2:
                                s.wait_ge(ssm, s_base + (tc - 1) * 16)
                            s.activation(STG[:, (tc % 2) * D:(tc % 2 + 1) * D],
                                         X3FM[:, tc * D:(tc + 1) * D], ACT.Identity,
                                         bias=NBSB[:, tc:tc + 1],
                                         scale=RSTDB[:, tc:tc + 1]
                                         ).then_inc(scm, 1)
                            c["c"] += 1

                    @blk.vector
                    def _(v):
                        for tc in range(OC):
                            v.wait_ge(scm, c_base + tc + 1)
                            sl = STG[:, (tc % 2) * D:(tc % 2 + 1) * D]
                            v.tensor_add(sl, sl,
                                         XQR[:, tc * D:(tc + 1) * D]
                                         ).then_inc(sv, 1)
                            c["v"] += 1

                    @blk.gpsimd
                    def _(g):
                        g.wait_ge(sg, c["g"])

    nc.finalize()
    return nc


_MASK_CACHE = {}


def _make_mask(qstart, T, OWN):
    key = (qstart, T, OWN)
    if key not in _MASK_CACHE:
        SMAX = T - P
        MW = SMAX + OWN
        u = np.arange(MW)[None, :]
        pr = np.arange(P)[:, None]
        _MASK_CACHE[key] = np.ascontiguousarray(
            (u >= pr + SMAX - qstart).astype(ml_dtypes.bfloat16))
    return _MASK_CACHE[key]


def kernel(L0_attn_x, L0_attn_Wq, L0_attn_Wk, L0_attn_Wv, L0_ffn_router_W,
           L0_ffn_router_b, L0_ffn_W1, L0_ffn_b1, L0_ffn_W2, L0_ffn_b2):
    x = np.asarray(L0_attn_x, dtype=np.float32)
    B, T, D = x.shape
    wq = np.ascontiguousarray(np.asarray(L0_attn_Wq, dtype=np.float32))
    wk = np.ascontiguousarray(np.asarray(L0_attn_Wk, dtype=np.float32))
    wv = np.ascontiguousarray(np.asarray(L0_attn_Wv, dtype=np.float32))
    rw = np.ascontiguousarray(np.asarray(L0_ffn_router_W, dtype=np.float32))
    rb = np.ascontiguousarray(np.asarray(L0_ffn_router_b,
                                         dtype=np.float32)).reshape(1, -1)
    w1 = np.ascontiguousarray(np.asarray(L0_ffn_W1, dtype=np.float32))
    b1 = np.asarray(L0_ffn_b1, dtype=np.float32)
    ne_, f_ = b1.shape
    b1 = np.ascontiguousarray(
        b1.reshape(ne_, f_ // 128, 128).transpose(2, 0, 1).reshape(128, -1))
    w2 = np.ascontiguousarray(np.asarray(L0_ffn_W2, dtype=np.float32))
    b2 = np.ascontiguousarray(np.asarray(L0_ffn_b2, dtype=np.float32))
    NE, _, F = w1.shape
    OWN = (B * T) // 8

    nc = build_program(D=D, F=F, T=T, OWN=OWN, NE=NE)

    in_maps = []
    for core in range(8):
        b, a = divmod(core, 4)
        qs = a * OWN
        in_maps.append({
            "xb": np.ascontiguousarray(x[b]),
            "xq": np.ascontiguousarray(x[b, qs:qs + OWN]),
            "mk": _make_mask(qs, T, OWN),
            "wq": wq, "wk": wk, "wv": wv, "rw": rw, "rb": rb,
            "w1": w1, "b1": b1, "w2": w2, "b2": b2,
        })
    res = run_bass_kernel_spmd(nc, in_maps, core_ids=list(range(8)))
    out = np.empty((B, T, D), dtype=np.float32)
    for core in range(8):
        b, a = divmod(core, 4)
        qs = a * OWN
        out[b, qs:qs + OWN] = np.asarray(res.results[core]["y"],
                                         dtype=np.float32).reshape(OWN, D)
    return out


# version 19
# speedup vs baseline: 1.0843x; 1.0047x over previous
import contextlib
import numpy as np
import ml_dtypes
import concourse.bass as bass
from concourse.bass import IndirectOffsetOnAxis
import concourse.mybir as mybir
from concourse.bass_utils import run_bass_kernel_spmd

F32 = mybir.dt.float32
BF16 = mybir.dt.bfloat16
I32 = mybir.dt.int32
AX = mybir.AxisListType.X
ALU = mybir.AluOpType
ACT = mybir.ActivationFunctionType

LN_EPS = 1e-5
SHIFT = 12.0
P = 128
USE_BRANCHES = True
KMARG = 64
BR_G = True
BR_T = True
BR_C = True
BR_V = True


def build_program(D=1024, F=4096, T=8192, BT=1024, OWN=2048, TGW=256, FSUB=1024,
                  NE=8):
    DC = D // P
    KN = BT // P
    NB = T // BT
    OC = OWN // P
    OWNB = OWN // BT
    QTW = min(512, BT)
    NQT = BT // QTW
    DTW = min(512, D)
    ND = D // DTW
    FCS = FSUB // P
    SC = F // FSUB
    FCT = F // P
    NTG = OWN // TGW
    TGC = TGW // P
    G4 = min(4, KN)
    GD = min(4, DC)
    NG = DC // GD
    SMAX = T - P
    MW = SMAX + OWN
    WS = OWN + (KN - 1) * P
    invD = 1.0 / D
    att_scale = 1.0 / float(np.sqrt(D))
    assert KN * BT >= DC * D

    nc = bass.Bass("TRN2", target_bir_lowering=False, detect_race_conditions=False)

    xb = nc.dram_tensor("xb", [T, D], F32, kind="ExternalInput").ap()
    xq = nc.dram_tensor("xq", [OWN, D], F32, kind="ExternalInput").ap()
    mk = nc.dram_tensor("mk", [P, MW], BF16, kind="ExternalInput").ap()
    wq = nc.dram_tensor("wq", [D, D], F32, kind="ExternalInput").ap()
    wk = nc.dram_tensor("wk", [D, D], F32, kind="ExternalInput").ap()
    wv = nc.dram_tensor("wv", [D, D], F32, kind="ExternalInput").ap()
    rw = nc.dram_tensor("rw", [D, NE], F32, kind="ExternalInput").ap()
    rb = nc.dram_tensor("rb", [1, NE], F32, kind="ExternalInput").ap()
    w1 = nc.dram_tensor("w1", [NE, D, F], F32, kind="ExternalInput").ap()
    b1 = nc.dram_tensor("b1", [P, NE * FCT], F32, kind="ExternalInput").ap()
    w2 = nc.dram_tensor("w2", [NE, F, D], F32, kind="ExternalInput").ap()
    b2 = nc.dram_tensor("b2", [NE, D], F32, kind="ExternalInput").ap()
    yo = nc.dram_tensor("y", [OWN, D], F32, kind="ExternalOutput").ap()
    x1d = nc.dram_tensor("x1d", [OWN, D], BF16).ap()
    rowd = nc.dram_tensor("rowd", [NE, OC], F32).ap()
    flid = nc.dram_tensor("flid", [NE, NTG], I32).ap()
    efld = nc.dram_tensor("efld", [NE, 1], I32).ap()

    c = {"g": 0, "s": 0, "t": 0, "v": 0, "c": 0, "gc": 0}

    with contextlib.ExitStack() as _es0:
        _e = _es0.enter_context
        O_acc = _e(nc.sbuf_tensor("O_acc", [P, OC * D], BF16))
        XQR = _e(nc.sbuf_tensor("XQR", [P, OC * D], BF16))
        Lb = _e(nc.sbuf_tensor("Lb", [P, OC], F32))
        SS = _e(nc.sbuf_tensor("SS", [P, max(KN, OC)], F32))
        SQ = _e(nc.sbuf_tensor("SQ", [P, max(KN, OC)], F32))
        MEANB = _e(nc.sbuf_tensor("MEANB", [P, max(KN, OC)], F32))
        T1B = _e(nc.sbuf_tensor("T1B", [P, max(KN, OC)], F32))
        VARPB = _e(nc.sbuf_tensor("VARPB", [P, max(KN, OC)], F32))
        RSTDB = _e(nc.sbuf_tensor("RSTDB", [P, max(KN, OC)], F32))
        NBSB = _e(nc.sbuf_tensor("NBSB", [P, max(KN, OC)], F32))
        IDENT = _e(nc.sbuf_tensor("IDENT", [P, P], BF16))
        ones1 = _e(nc.sbuf_tensor("ones1", [1, P], BF16))
        onescol = _e(nc.sbuf_tensor("onescol", [P, 1], BF16))
        GATES = _e(nc.sbuf_tensor("GATES", [P, OC * NE], F32))
        B1all = _e(nc.sbuf_tensor("B1all", [P, NE * FCT], F32))
        WrB = _e(nc.sbuf_tensor("WrB", [P, DC * NE], BF16))
        WrLO = _e(nc.sbuf_tensor("WrLO", [P, DC * NE], BF16))
        WrF = _e(nc.sbuf_tensor("WrF", [P, DC * NE], F32))
        RB = _e(nc.sbuf_tensor("RB", [1, NE], BF16))
        RBLO = _e(nc.sbuf_tensor("RBLO", [1, NE], BF16))
        RBF = _e(nc.sbuf_tensor("RBF", [1, NE], F32))
        NSH = _e(nc.sbuf_tensor("NSH", [P, 1], F32))
        LTS = _e(nc.sbuf_tensor("LTS", [P, P], BF16))
        LTS8F = _e(nc.sbuf_tensor("LTS8F", [NE, NE], F32))
        ERMP = _e(nc.sbuf_tensor("ERMP", [P, NE], F32))
        TOKC = _e(nc.sbuf_tensor("TOKC", [P, OC], F32))
        ONES8R = _e(nc.sbuf_tensor("ONES8R", [NE, OC], F32))
        IND = _e(nc.sbuf_tensor("IND", [P, OC * NE], BF16))
        TOKCH = _e(nc.sbuf_tensor("TOKCH", [P, OC], BF16))
        TOKCL = _e(nc.sbuf_tensor("TOKCL", [P, OC], BF16))
        INVF = _e(nc.sbuf_tensor("INVF", [P, OC], F32))
        BASEB = _e(nc.sbuf_tensor("BASEB", [P, NE], F32))
        CNT = _e(nc.sbuf_tensor("CNT", [NE, OC], F32))
        CUMI = _e(nc.sbuf_tensor("CUMI", [NE, OC], F32))
        CUMTOT = _e(nc.sbuf_tensor("CUMTOT", [NE, OC], F32))
        TOT = _e(nc.sbuf_tensor("TOT", [NE, 1], F32))
        BASEC = _e(nc.sbuf_tensor("BASEC", [NE, 1], F32))
        ENDC = _e(nc.sbuf_tensor("ENDC", [NE, 1], F32))
        FL8 = _e(nc.sbuf_tensor("FL8", [NE, NTG], F32))
        FL8I = _e(nc.sbuf_tensor("FL8I", [NE, NTG], I32))
        EFL8I = _e(nc.sbuf_tensor("EFL8I", [NE, 1], I32))
        T81 = _e(nc.sbuf_tensor("T81", [NE, 1], F32))
        T82 = _e(nc.sbuf_tensor("T82", [NE, 1], F32))
        FLROW = _e(nc.sbuf_tensor("FLROW", [1, NE * NTG], I32))
        EFROW = _e(nc.sbuf_tensor("EFROW", [1, NE], I32))
        ROWSF = _e(nc.sbuf_tensor("ROWSF", [1, OC * NE], F32))
        ROWSH = _e(nc.sbuf_tensor("ROWSH", [1, OC * NE], BF16))
        ROWSL = _e(nc.sbuf_tensor("ROWSL", [1, OC * NE], BF16))
        T8 = _e(nc.sbuf_tensor("T8", [P, NE], F32))
        T8B = _e(nc.sbuf_tensor("T8B", [P, OC], F32))
        GSEL = _e(nc.sbuf_tensor("GSEL", [P, OC], F32))
        DESTF = _e(nc.sbuf_tensor("DESTF", [P, OC], F32))
        ESRT = _e(nc.sbuf_tensor("ESRT", [P, OC], F32))
        SCL = _e(nc.sbuf_tensor("SCL", [P, TGC], F32))
        DUMB = _e(nc.sbuf_tensor("DUMB", [1, 8], F32))
        sg = _e(nc.semaphore("sg"))
        ssm = _e(nc.semaphore("ssm"))
        st = _e(nc.semaphore("st"))
        sv = _e(nc.semaphore("sv"))
        scm = _e(nc.semaphore("scm"))
        sgc = _e(nc.semaphore("sgc"))
        def gdma(eng, dst, src):
            eng.dma_start(dst, src).then_inc(sg, 16)
            c["g"] += 16

        def sdma(eng, dst, src):
            eng.dma_start(dst, src).then_inc(ssm, 16)
            c["s"] += 16

        def vhop(v, ins):
            ins.then_inc(sv, 1)
            c["v"] += 1
            v.wait_ge(sv, c["v"])

        def chop(s, ins):
            ins.then_inc(scm, 1)
            c["c"] += 1
            s.wait_ge(scm, c["c"])

        def batch_stats(v, src_buf, nchunk, add_eps_lsq=None):
            for i in range(nchunk):
                sl = src_buf(i)
                v.reduce_sum(SS[:, i:i + 1], sl, axis=AX)
                ins = v.scalar_tensor_tensor(TRX_cur[0][:, 0:D], sl, 1.0, sl,
                                             op0=ALU.mult, op1=ALU.mult,
                                             accum_out=SQ[:, i:i + 1])
            vhop(v, ins)
            ck = slice(0, nchunk)
            vhop(v, v.tensor_scalar_mul(MEANB[:, ck], SS[:, ck], invD))
            vhop(v, v.tensor_mul(T1B[:, ck], MEANB[:, ck], MEANB[:, ck]))
            vhop(v, v.scalar_tensor_tensor(VARPB[:, ck], SQ[:, ck], invD,
                                           T1B[:, ck], op0=ALU.mult,
                                           op1=ALU.subtract))
            if add_eps_lsq is None:
                ins = v.tensor_scalar_add(VARPB[:, ck], VARPB[:, ck], LN_EPS)
            else:
                vhop(v, v.tensor_mul(T1B[:, ck], add_eps_lsq[:, ck],
                                     add_eps_lsq[:, ck]))
                ins = v.scalar_tensor_tensor(VARPB[:, ck], T1B[:, ck], LN_EPS,
                                             VARPB[:, ck], op0=ALU.mult,
                                             op1=ALU.add)
            ins.then_inc(sv, 1)
            c["v"] += 1

        def rstd_scalar(s, nchunk):
            s.wait_ge(sv, c["v"])
            ck = slice(0, nchunk)
            chop(s, s.activation(T1B[:, ck], VARPB[:, ck], ACT.Ln))
            s.activation(RSTDB[:, ck], T1B[:, ck], ACT.Exp,
                         scale=-0.5).then_inc(scm, 1)
            c["c"] += 1

        def nbias_vec(v, nchunk):
            v.wait_ge(scm, c["c"])
            ck = slice(0, nchunk)
            v.scalar_tensor_tensor(NBSB[:, ck], MEANB[:, ck], -1.0, RSTDB[:, ck],
                                   op0=ALU.mult, op1=ALU.mult).then_inc(sv, 1)
            c["v"] += 1

        TRX_cur = [None]

        with contextlib.ExitStack() as _esA:
            _a = _esA.enter_context
            IOTF = _a(nc.sbuf_tensor("IOTF", [P, P], F32))
            Mstep = _a(nc.sbuf_tensor("Mstep", [P, WS], BF16))
            WkB = _a(nc.sbuf_tensor("WkB", [P, DC * D], BF16))
            WvB = _a(nc.sbuf_tensor("WvB", [P, DC * D], BF16))
            QFM = _a(nc.sbuf_tensor("QFM", [P, DC * OWN], BF16))
            TMP = _a(nc.sbuf_tensor("TMP", [P, KN * D], BF16))
            KBUF = _a(nc.sbuf_tensor("KBUF", [P, KN * D], BF16))
            LNFM = _a(nc.sbuf_tensor("LNFM", [P, DC * BT], BF16))
            PT = _a(nc.sbuf_tensor("PT", [P, KN * BT], BF16))
            SQT = _a(nc.sbuf_tensor("SQT", [P, D], BF16))
            psT = _a(nc.psum_tensor("psT", [P, 512], BF16))
            pp0 = _a(nc.psum_tensor("pp0", [P, 512], F32))
            pp1 = _a(nc.psum_tensor("pp1", [P, 512], F32))
            pv0 = _a(nc.psum_tensor("pv0", [P, min(D, 1024)], F32))
            pv1 = _a(nc.psum_tensor("pv1", [P, min(D, 1024)], F32))
            psL = _a(nc.psum_tensor("psL", [P, OC], F32))
            pp = [pp0, pp1]
            pv = [pv0, pv1]

            with nc.Block() as blk:
                @blk.gpsimd
                def _(g):
                    g.iota(IOTF[:, :], [[1, P]], base=0, channel_multiplier=-1,
                           allow_small_or_imprecise_dtypes=True)
                    g.iota(ERMP[:, :], [[1, NE]], base=0, channel_multiplier=0,
                           allow_small_or_imprecise_dtypes=True)
                    g.iota(TOKC[:, :], [[P, OC]], base=0, channel_multiplier=1,
                           allow_small_or_imprecise_dtypes=True).then_inc(sgc, 1)
                    c["gc"] += 1

                @blk.sync
                def _(s):
                    sdma(s, B1all[:, :], b1)
                    sdma(s, WrF[:, :].rearrange("p (dc e) -> p dc e", e=NE),
                         rw.rearrange("(dc p) e -> p dc e", p=P))
                    sdma(s, RBF[:, :], rb)
                    s.wait_ge(ssm, c["s"])
                s_init = c["s"]

                @blk.vector
                def _(v):
                    v.wait_ge(sgc, 1)
                    v.tensor_single_scalar(IDENT[:, :], IOTF[:, :], 0.0,
                                           op=ALU.is_equal)
                    vhop(v, v.tensor_single_scalar(LTS[:, :], IOTF[:, :], 0.0,
                                                   op=ALU.is_gt))
                    v.tensor_copy(LTS8F[:, :], LTS[0:NE, 0:NE])
                    v.memset(ONES8R[:, :], 1.0)
                    vhop(v, v.tensor_copy(TOKCH[:, :], TOKC[:, :]))
                    v.tensor_sub(TOKCL[:, :], TOKC[:, :], TOKCH[:, :])
                    v.memset(ones1[:, :], 1.0)
                    v.memset(onescol[:, :], 1.0)
                    v.memset(NSH[:, :], -SHIFT)
                    v.memset(O_acc[:, :], 0.0)
                    v.memset(Lb[:, :], 0.0)
                    v.wait_ge(ssm, s_init)
                    vhop(v, v.tensor_copy(WrB[:, :], WrF[:, :]))
                    v.tensor_sub(WrLO[:, :], WrF[:, :], WrB[:, :])
                    vhop(v, v.tensor_copy(RB[:, :], RBF[:, :]))
                    v.tensor_sub(RBLO[:, :], RBF[:, :], RB[:, :])

            def ln_stats_chunk_ops(g_wait, XB):
                ops = []

                def mk(kn):
                    def f(v):
                        if isinstance(g_wait, dict):
                            if kn in g_wait:
                                v.wait_ge(sg, g_wait[kn])
                        elif kn == 0:
                            v.wait_ge(sg, g_wait)
                        sl = XB[:, kn * D:(kn + 1) * D]
                        v.reduce_sum(SS[:, kn:kn + 1], sl, axis=AX)
                        v.scalar_tensor_tensor(SQT[:, 0:D], sl, 1.0, sl,
                                               op0=ALU.mult, op1=ALU.mult,
                                               accum_out=SQ[:, kn:kn + 1])
                    return f
                for kn in range(KN):
                    ops.append(mk(kn))
                return ops

            def ln_stats_tail(v):
                v.wait_ge(sv, c["v"])
                v.engine_nop().then_inc(sv, 1)
                c["v"] += 1
                v.wait_ge(sv, c["v"])
                ck = slice(0, KN)
                vhop(v, v.tensor_scalar_mul(MEANB[:, ck], SS[:, ck], invD))
                vhop(v, v.tensor_mul(T1B[:, ck], MEANB[:, ck], MEANB[:, ck]))
                vhop(v, v.scalar_tensor_tensor(VARPB[:, ck], SQ[:, ck], invD,
                                               T1B[:, ck], op0=ALU.mult,
                                               op1=ALU.subtract))
                v.tensor_scalar_add(VARPB[:, ck], VARPB[:, ck],
                                    LN_EPS).then_inc(sv, 1)
                c["v"] += 1

            def ln_stats_part(blk, g_wait, XB):
                ops = ln_stats_chunk_ops(g_wait, XB)

                @blk.vector
                def _(v):
                    for f in ops:
                        f(v)
                    ln_stats_tail(v)

            def ln_and_transpose(blk, g_wait, XB, LB, pv_guard=None,
                                 stats_done=False):
                if not stats_done:
                    ln_stats_part(blk, g_wait, XB)

                @blk.scalar
                def _(s):
                    rstd_scalar(s, KN)

                @blk.vector
                def _(v):
                    nbias_vec(v, KN)
                v_nb = c["v"]

                c_apply0 = c["c"]

                @blk.scalar
                def _(s):
                    s.wait_ge(sv, v_nb)
                    if pv_guard is not None:
                        s.wait_ge(st, pv_guard)
                    for kn in range(KN):
                        s.activation(LB[:, kn * D:(kn + 1) * D],
                                     XB[:, kn * D:(kn + 1) * D], ACT.Identity,
                                     bias=NBSB[:, kn:kn + 1],
                                     scale=RSTDB[:, kn:kn + 1]).then_inc(scm, 1)
                        c["c"] += 1

                ngrp = KN // G4
                t_base = c["t"]
                v_base = c["v"]

                @blk.tensor
                def _(t):
                    i = 0
                    for dc in range(DC):
                        for gi in range(ngrp):
                            t.wait_ge(scm, c_apply0 + (gi + 1) * G4)
                            t.wait_ge(sv, v_base + i)
                            for k4 in range(G4):
                                kn = gi * G4 + k4
                                ins = t.transpose(
                                    psT[:, k4 * P:(k4 + 1) * P],
                                    LB[:, kn * D + dc * P:kn * D + (dc + 1) * P],
                                    IDENT[:, :])
                            ins.then_inc(st, 1)
                            c["t"] += 1
                            i += 1

                @blk.vector
                def _(v):
                    i = 0
                    for dc in range(DC):
                        for gi in range(ngrp):
                            v.wait_ge(st, t_base + i + 1)
                            v.tensor_copy(
                                LNFM[:, dc * BT + gi * G4 * P:
                                     dc * BT + (gi + 1) * G4 * P],
                                psT[:, 0:G4 * P]).then_inc(sv, 1)
                            c["v"] += 1
                            i += 1

            def proj_fills(blk, fills, dsts, sg_wait=None):
                t_base = c["t"]
                c_base = c["c"]
                v_pre = c["v"]

                @blk.tensor
                def _(t):
                    if sg_wait is not None:
                        t.wait_ge(sg, sg_wait)
                    t.wait_ge(sv, v_pre)
                    for i, (mms, w) in enumerate(fills):
                        if c_base + i - 1 > 0:
                            t.wait_ge(scm, c_base + i - 1)
                        dst = pp[i % 2][:, 0:w]
                        for k, (lh, rh) in enumerate(mms):
                            ins = t.matmul(dst, lh, rh, start=(k == 0),
                                           stop=(k == len(mms) - 1))
                        ins.then_inc(st, 1)
                        c["t"] += 1

                @blk.scalar
                def _(s):
                    for i, (mms, w) in enumerate(fills):
                        s.wait_ge(st, t_base + i + 1)
                        s.activation(dsts[i], pp[i % 2][:, 0:w],
                                     ACT.Copy).then_inc(scm, 1)
                        c["c"] += 1

            wgt = {}
            tr_prev = [0]
            for ob in range(OWNB):
                with nc.Block() as blk:
                    if ob == 0:
                        @blk.gpsimd
                        def _(g):
                            for h in range(2):
                                nq = 2 if h == 0 else 1
                                qk = KN // nq
                                for qq in range(nq):
                                    k0 = h * KN + qq * qk
                                    gdma(g, XQR[:, k0 * D:(k0 + qk) * D]
                                         .rearrange("p (n d) -> p n d", d=D),
                                         xq[k0 * P:(k0 + qk) * P, :].rearrange(
                                             "(n p) d -> p n d", p=P))
                                    wgt[f"x{h}q{qq}"] = c["g"]
                                wgt[f"x{h}"] = c["g"]
                                if h == 0:
                                    gdma(g, PT[:, 0:DC * D].rearrange(
                                             "p (dc e) -> p dc e", e=D),
                                         wq.rearrange("(dc p) e -> p dc e",
                                                      p=P))
                                    wgt["q"] = c["g"]
                                else:
                                    gdma(g, TMP[:, :].rearrange(
                                             "p (n d) -> p n d", d=D),
                                         xb[0:BT, :].rearrange(
                                             "(n p) d -> p n d", p=P))
                                    wgt["xkv"] = c["g"]
                                    gdma(g, WkB[:, :].rearrange(
                                             "p (dc e) -> p dc e", e=D),
                                         wk.rearrange("(dc p) e -> p dc e",
                                                      p=P))
                                    wgt["k"] = c["g"]
                                    gdma(g, WvB[:, :].rearrange(
                                             "p (dc e) -> p dc e", e=D),
                                         wv.rearrange("(dc p) e -> p dc e",
                                                      p=P))
                                    wgt["v"] = c["g"]
                    gwq = ({0: wgt["x0q0"], KN // 2: wgt["x0q1"]}
                           if ob == 0 else wgt["x1"])
                    ln_and_transpose(blk, gwq,
                                     XQR[:, ob * KN * D:(ob + 1) * KN * D],
                                     KBUF,
                                     pv_guard=(tr_prev[0] if ob else None))
                    tr_prev[0] = c["t"]

                    fills, dsts = [], []
                    for dco in range(DC):
                        for qt in range(NQT):
                            fills.append((
                                [(PT[:, dci * D + dco * P:dci * D + (dco + 1) * P],
                                  LNFM[:, dci * BT + qt * QTW:
                                       dci * BT + (qt + 1) * QTW])
                                 for dci in range(DC)], QTW))
                            dsts.append(QFM[:, dco * OWN + ob * BT + qt * QTW:
                                             dco * OWN + ob * BT + (qt + 1) * QTW])
                    proj_fills(blk, fills, dsts,
                               sg_wait=(wgt["q"] if ob == 0 else None))
            tr_prev = None

            Sdone = {}
            PVdone = {}
            Mdone = {}
            gx_next = [0]
            with nc.Block() as blk:
              for j in range(NB):
                    XB = TMP if j % 2 == 0 else KBUF
                    LB = KBUF if j % 2 == 0 else TMP

                    if j >= 1:
                        @blk.gpsimd
                        def _(g):
                            g.wait_ge(st, Sdone[j - 1])
                            gdma(g, XB[:, :].rearrange("p (n d) -> p n d", d=D),
                                 xb[j * BT:(j + 1) * BT, :].rearrange(
                                     "(n p) d -> p n d", p=P))
                    g_x = c["g"] if j >= 1 else wgt["xkv"]
                    gx_next[0] = c["g"] + 16

                    @blk.sync
                    def _(s):
                        if j >= 1:
                            s.wait_ge(sv, Mdone[j - 1])
                        base = SMAX - j * BT - (KN - 1) * P
                        sdma(s, Mstep[:, :], mk[:, base:base + WS])
                    s_m = c["s"]

                    ln_and_transpose(blk, g_x, XB, LB,
                                     pv_guard=PVdone.get(j - 1),
                                     stats_done=(j >= 1))

                    fills, dsts = [], []
                    for dco in range(DC):
                        for kt in range(NQT):
                            fills.append((
                                [(WkB[:, dci * D + dco * P:dci * D + (dco + 1) * P],
                                  LNFM[:, dci * BT + kt * QTW:
                                       dci * BT + (kt + 1) * QTW])
                                 for dci in range(DC)], QTW))
                            dsts.append(LB[:, dco * BT + kt * QTW:
                                           dco * BT + (kt + 1) * QTW])
                    proj_fills(blk, fills, dsts,
                               sg_wait=(wgt["k"] if j == 0 else None))

                    fills, dsts = [], []
                    for kn in range(KN):
                        for dt in range(ND):
                            fills.append((
                                [(LNFM[:, dci * BT + kn * P:dci * BT + (kn + 1) * P],
                                  WvB[:, dci * D + dt * DTW:
                                       dci * D + (dt + 1) * DTW])
                                 for dci in range(DC)], DTW))
                            dsts.append(XB[:, kn * D + dt * DTW:
                                           kn * D + (dt + 1) * DTW])
                    proj_fills(blk, fills, dsts,
                               sg_wait=(wgt["v"] if j == 0 else None))

                    for qh in range(OWNB):
                        t_base = c["t"]
                        c_base = c["c"]
                        v_base = c["v"]

                        @blk.tensor
                        def _(t):
                            i = 0
                            for kc in range(KN):
                                for qt in range(NQT):
                                    if c_base + i - 1 > 0:
                                        t.wait_ge(scm, c_base + i - 1)
                                    dst = pp[i % 2][:, 0:QTW]
                                    for dc in range(DC):
                                        ins = t.matmul(
                                            dst,
                                            LB[:, dc * BT + kc * P:
                                               dc * BT + (kc + 1) * P],
                                            QFM[:, dc * OWN + qh * BT + qt * QTW:
                                                dc * OWN + qh * BT + (qt + 1) * QTW],
                                            start=(dc == 0), stop=(dc == DC - 1))
                                    ins.then_inc(st, 1)
                                    c["t"] += 1
                                    i += 1
                            if qh == OWNB - 1:
                                Sdone[j] = c["t"]

                        @blk.scalar
                        def _(s):
                            i = 0
                            for kc in range(KN):
                                for qt in range(NQT):
                                    s.wait_ge(st, t_base + i + 1)
                                    s.activation(
                                        PT[:, kc * BT + qt * QTW:
                                           kc * BT + (qt + 1) * QTW],
                                        pp[i % 2][:, 0:QTW], ACT.Exp,
                                        bias=NSH[:, :],
                                        scale=att_scale).then_inc(scm, 1)
                                    c["c"] += 1
                                    i += 1

                        @blk.vector
                        def _(v):
                            v.wait_ge(ssm, s_m)
                            for kc in range(KN):
                                v.wait_ge(scm, c_base + (kc + 1) * NQT)
                                moff = (KN - 1 - kc) * P + qh * BT
                                v.tensor_mul(PT[:, kc * BT:(kc + 1) * BT],
                                             PT[:, kc * BT:(kc + 1) * BT],
                                             Mstep[:, moff:moff + BT]
                                             ).then_inc(sv, 1)
                                c["v"] += 1
                            if qh == OWNB - 1:
                                Mdone[j] = c["v"]
                        pend_stats = None
                        if qh == OWNB - 1 and j + 1 < NB:
                            XB2 = TMP if (j + 1) % 2 == 0 else KBUF
                            pend_stats = ln_stats_chunk_ops(gx_next[0], XB2)
                        last_step = (j == NB - 1)

                        def o_sum(v, qcg):
                            sl = O_acc[:, qcg * D:(qcg + 1) * D]
                            v.reduce_sum(SS[:, qcg:qcg + 1], sl, axis=AX)
                            v.scalar_tensor_tensor(SQT[:, 0:D], sl, 1.0, sl,
                                                   op0=ALU.mult, op1=ALU.mult,
                                                   accum_out=SQ[:, qcg:qcg + 1])
                        v_mask = c["v"]

                        t_base2 = c["t"]
                        v_base2 = c["v"]

                        @blk.tensor
                        def _(t):
                            t.wait_ge(sv, v_mask)
                            for qc in range(KN):
                                qcg = qh * KN + qc
                                if qc >= 2:
                                    t.wait_ge(sv, v_base2 + 2 * qc - 2)
                                if qc >= 1:
                                    t.wait_ge(sv, v_base2 + 2 * qc - 1)
                                for kc in range(KN):
                                    lh = PT[:, kc * BT + qc * P:
                                            kc * BT + (qc + 1) * P]
                                    for dt in range(ND):
                                        t.matmul(
                                            pv[qc % 2][:, dt * DTW:(dt + 1) * DTW],
                                            lh, XB[:, kc * D + dt * DTW:
                                                   kc * D + (dt + 1) * DTW],
                                            start=(kc == 0), stop=(kc == KN - 1))
                                    ins = t.matmul(psL[:, qcg:qcg + 1], lh,
                                                   onescol[:, :], start=(kc == 0),
                                                   stop=(kc == KN - 1))
                                ins.then_inc(st, 1)
                                c["t"] += 1
                            if qh == OWNB - 1:
                                PVdone[j] = c["t"]

                        @blk.vector
                        def _(v):
                            for qc in range(KN):
                                qcg = qh * KN + qc
                                v.wait_ge(st, t_base2 + qc + 1)
                                v.tensor_add(Lb[:, qcg:qcg + 1],
                                             Lb[:, qcg:qcg + 1],
                                             psL[:, qcg:qcg + 1]).then_inc(sv, 1)
                                c["v"] += 1
                                for dt in range(ND):
                                    sl = O_acc[:, qcg * D + dt * DTW:
                                               qcg * D + (dt + 1) * DTW]
                                    ins = v.tensor_add(sl, sl,
                                                       pv[qc % 2][:, dt * DTW:
                                                                  (dt + 1) * DTW])
                                ins.then_inc(sv, 1)
                                c["v"] += 1
                                if pend_stats and qc >= 2:
                                    for _ in range(3):
                                        if pend_stats:
                                            pend_stats.pop(0)(v)
                                if last_step and qc >= 1:
                                    o_sum(v, qh * KN + qc - 1)
                            if pend_stats is not None:
                                while pend_stats:
                                    pend_stats.pop(0)(v)
                                ln_stats_tail(v)
                            if last_step:
                                o_sum(v, qh * KN + KN - 1)
                                if qh == OWNB - 1:
                                    v.wait_ge(sv, c["v"])
                                    v.engine_nop().then_inc(sv, 1)
                                    c["v"] += 1
                                    v.wait_ge(sv, c["v"])
                                    ck = slice(0, OC)
                                    vhop(v, v.tensor_scalar_mul(MEANB[:, ck],
                                                                SS[:, ck], invD))
                                    vhop(v, v.tensor_mul(T1B[:, ck], MEANB[:, ck],
                                                         MEANB[:, ck]))
                                    vhop(v, v.scalar_tensor_tensor(
                                        VARPB[:, ck], SQ[:, ck], invD, T1B[:, ck],
                                        op0=ALU.mult, op1=ALU.subtract))
                                    vhop(v, v.tensor_mul(T1B[:, ck], Lb[:, ck],
                                                         Lb[:, ck]))
                                    v.scalar_tensor_tensor(
                                        VARPB[:, ck], T1B[:, ck], LN_EPS,
                                        VARPB[:, ck], op0=ALU.mult,
                                        op1=ALU.add).then_inc(sv, 1)
                                    c["v"] += 1

        with contextlib.ExitStack() as _esB:
            _b = _esB.enter_context
            X3FM = _b(nc.sbuf_tensor("X3FM", [P, DC * OWN], BF16))
            JIOT = _b(nc.sbuf_tensor("JIOT", [P, OWN], F32))
            PRMB = _b(nc.sbuf_tensor("PRMB", [P, 2 * 512], BF16))
            STG = _b(nc.sbuf_tensor("STG", [P, 2 * D], F32))
            TRX = _b(nc.sbuf_tensor("TRX", [P, D], BF16))
            PRB = _b(nc.sbuf_tensor("PRB", [P, NE], F32))
            PRN = _b(nc.sbuf_tensor("PRN", [P, NE], F32))
            M8 = _b(nc.sbuf_tensor("M8", [P, NE], F32))
            SMB = _b(nc.sbuf_tensor("SMB", [P, 1], F32))
            NMB = _b(nc.sbuf_tensor("NMB", [P, 1], F32))
            RSB = _b(nc.sbuf_tensor("RSB", [P, 1], F32))
            psTE = _b(nc.psum_tensor("psTE", [P, 512], BF16))
            ph0 = _b(nc.psum_tensor("ph0", [P, TGW], F32))
            ph1 = _b(nc.psum_tensor("ph1", [P, TGW], F32))
            psR = _b(nc.psum_tensor("psR", [P, NE], F32))
            py0 = _b(nc.psum_tensor("py0", [P, 512], F32))
            py1 = _b(nc.psum_tensor("py1", [P, 512], F32))
            py2 = _b(nc.psum_tensor("py2", [P, 512], F32))
            py3 = _b(nc.psum_tensor("py3", [P, 512], F32))
            ph = [ph0, ph1]
            py = [py0, py1, py2, py3]
            TRX_cur[0] = TRX

            X1 = XQR
            with (nc.sbuf_tensor("X3LO", [P, DC * OWN], BF16) as X3LO,
                  nc.sbuf_tensor("X2TM", [P, OC * D], BF16) as X2TM,
                  nc.sbuf_tensor("TRXF", [P, D], F32) as TRXF,
                  nc.sbuf_tensor("TRXLO", [P, D], BF16) as TRXLO):
                _blkE = nc.Block()
                blk = _blkE.__enter__()
                if True:
                    @blk.gpsimd
                    def _(g):
                        g.wait_ge(st, PVdone[NB - 1])
                        g.iota(JIOT[:, :], [[1, OWN]], base=0,
                               channel_multiplier=0,
                               allow_small_or_imprecise_dtypes=True
                               ).then_inc(sgc, 1)
                        c["gc"] += 1

                    @blk.scalar
                    def _(s):
                        rstd_scalar(s, OC)

                    @blk.vector
                    def _(v):
                        nbias_vec(v, OC)

                if True:
                    v_base = c["v"]
                    c_base = c["c"]
                    ap_inc = {}

                    @blk.scalar
                    def _(s):
                        s.wait_ge(sv, v_base)
                        for tc in range(OC):
                            if tc >= 2:
                                s.wait_ge(sv, v_base + tc - 1)
                            s.activation(STG[:, (tc % 2) * D:(tc % 2 + 1) * D],
                                         O_acc[:, tc * D:(tc + 1) * D], ACT.Identity,
                                         bias=NBSB[:, tc:tc + 1],
                                         scale=RSTDB[:, tc:tc + 1]
                                         ).then_inc(scm, 1)
                            c["c"] += 1
                            ap_inc[tc] = c["c"]
                            if tc >= 2 and (tc - 2) % 2 == 0:
                                s.wait_ge(sv, v_base + tc - 1)
                                s.activation(TRXF[:, 0:D],
                                             X1[:, (tc - 2) * D:(tc - 1) * D],
                                             ACT.Square,
                                             accum_out=SQ[:, tc - 2:tc - 1]
                                             ).then_inc(scm, 1)
                                c["c"] += 1
                        for tc in (OC - 2, OC - 1):
                            if tc % 2 == 0:
                                s.wait_ge(sv, v_base + tc + 1)
                                s.activation(TRXF[:, 0:D],
                                             X1[:, tc * D:(tc + 1) * D],
                                             ACT.Square,
                                             accum_out=SQ[:, tc:tc + 1]
                                             ).then_inc(scm, 1)
                                c["c"] += 1

                    @blk.vector
                    def _(v):
                        for tc in range(OC):
                            v.wait_ge(scm, ap_inc[tc])
                            v.scalar_tensor_tensor(
                                X1[:, tc * D:(tc + 1) * D],
                                XQR[:, tc * D:(tc + 1) * D], 1.0,
                                STG[:, (tc % 2) * D:(tc % 2 + 1) * D],
                                op0=ALU.mult, op1=ALU.add,
                                accum_out=SS[:, tc:tc + 1]).then_inc(sv, 1)
                            c["v"] += 1
                            if tc >= 1 and (tc - 1) % 2 == 1:
                                sl = X1[:, (tc - 1) * D:tc * D]
                                v.scalar_tensor_tensor(
                                    TRX[:, 0:D], sl, 1.0, sl, op0=ALU.mult,
                                    op1=ALU.mult,
                                    accum_out=SQ[:, tc - 1:tc])
                        sl = X1[:, (OC - 1) * D:OC * D]
                        v.scalar_tensor_tensor(TRX[:, 0:D], sl, 1.0, sl,
                                               op0=ALU.mult, op1=ALU.mult,
                                               accum_out=SQ[:, OC - 1:OC])
                        v.wait_ge(scm, c["c"])
                        v.wait_ge(sv, c["v"])
                        v.engine_nop().then_inc(sv, 1)
                        c["v"] += 1
                        v.wait_ge(sv, c["v"])
                        ck = slice(0, OC)
                        vhop(v, v.tensor_scalar_mul(MEANB[:, ck], SS[:, ck],
                                                    invD))
                        vhop(v, v.tensor_mul(T1B[:, ck], MEANB[:, ck],
                                             MEANB[:, ck]))
                        vhop(v, v.scalar_tensor_tensor(VARPB[:, ck], SQ[:, ck],
                                                       invD, T1B[:, ck],
                                                       op0=ALU.mult,
                                                       op1=ALU.subtract))
                        v.tensor_scalar_add(VARPB[:, ck], VARPB[:, ck],
                                            LN_EPS).then_inc(sv, 1)
                        c["v"] += 1

                if True:
                    @blk.scalar
                    def _(s):
                        rstd_scalar(s, OC)

                    @blk.vector
                    def _(v):
                        nbias_vec(v, OC)

                if True:
                    v_l3 = c["v"]
                    c_base = c["c"]
                    t_base = c["t"]
                    v_base = c["v"]
                    NG2 = 2 * NG

                    @blk.scalar
                    def _(s):
                        s.wait_ge(sv, v_base)
                        for tc in range(OC):
                            if tc >= 1:
                                s.wait_ge(sv, v_base + (tc - 1) * (NG2 + 1) + 1)
                            s.activation(TRXF[:, 0:D], X1[:, tc * D:(tc + 1) * D],
                                         ACT.Identity, bias=NBSB[:, tc:tc + 1],
                                         scale=RSTDB[:, tc:tc + 1]
                                         ).then_inc(scm, 1)
                            c["c"] += 1

                    @blk.vector
                    def _(v):
                        x3r = X3FM[:, :].rearrange("p (dc q) -> p dc q", q=OWN)
                        x3l = X3LO[:, :].rearrange("p (dc q) -> p dc q", q=OWN)
                        for tc in range(OC):
                            v.wait_ge(scm, c_base + tc + 1)
                            if tc >= 1:
                                v.wait_ge(st, t_base + tc * NG2)
                            v.tensor_copy(TRX[:, 0:D], TRXF[:, 0:D])
                            v.tensor_copy(X2TM[:, tc * D:(tc + 1) * D],
                                          TRXF[:, 0:D])
                            v.tensor_sub(TRXLO[:, 0:D], TRXF[:, 0:D],
                                         TRX[:, 0:D]).then_inc(sv, 1)
                            c["v"] += 1
                            for half in range(2):
                                dstr = x3r if half == 0 else x3l
                                for ng in range(NG):
                                    fi = tc * NG2 + half * NG + ng
                                    v.wait_ge(st, t_base + fi + 1)
                                    v.tensor_copy(
                                        dstr[:, ng * GD:(ng + 1) * GD,
                                             tc * P:(tc + 1) * P],
                                        psTE[:, 0:GD * P].rearrange(
                                            "p (k q) -> p k q", q=P)
                                    ).then_inc(sv, 1)
                                    c["v"] += 1

                    @blk.tensor
                    def _(t):
                        for tc in range(OC):
                            for half in range(2):
                                src = TRX if half == 0 else TRXLO
                                for ng in range(NG):
                                    fi = tc * NG2 + half * NG + ng
                                    t.wait_ge(sv, v_base + tc * (NG2 + 1) + 1
                                              + (fi - tc * NG2))
                                    for k4 in range(GD):
                                        dc = ng * GD + k4
                                        ins = t.transpose(
                                            psTE[:, k4 * P:(k4 + 1) * P],
                                            src[:, dc * P:(dc + 1) * P],
                                            IDENT[:, :])
                                    ins.then_inc(st, 1)
                                    c["t"] += 1
                if True:
                    t_base = c["t"]
                    v_base = c["v"]
                    c_base = c["c"]

                    @blk.tensor
                    def _(t):
                        for tc in range(OC):
                            t.wait_ge(sv, v_l3 + (tc + 1) * (NG2 + 1))
                            if tc >= 1:
                                t.wait_ge(sv, v_base + tc * 6)
                            for dc in range(DC):
                                t.matmul(psR[:, :],
                                         X3FM[:, dc * OWN + tc * P:
                                              dc * OWN + (tc + 1) * P],
                                         WrB[:, dc * NE:(dc + 1) * NE],
                                         start=(dc == 0), stop=False)
                            for dc in range(DC):
                                t.matmul(psR[:, :],
                                         X3LO[:, dc * OWN + tc * P:
                                              dc * OWN + (tc + 1) * P],
                                         WrB[:, dc * NE:(dc + 1) * NE],
                                         start=False, stop=False)
                            for dc in range(DC):
                                t.matmul(psR[:, :],
                                         X3FM[:, dc * OWN + tc * P:
                                              dc * OWN + (tc + 1) * P],
                                         WrLO[:, dc * NE:(dc + 1) * NE],
                                         start=False, stop=False)
                            t.matmul(psR[:, :], ones1[:, :], RB[:, :],
                                     start=False, stop=False)
                            t.matmul(psR[:, :], ones1[:, :], RBLO[:, :],
                                     start=False, stop=True).then_inc(st, 1)
                            c["t"] += 1

                    @blk.vector
                    def _(v):
                        for tc in range(OC):
                            v.wait_ge(st, t_base + tc + 1)
                            v.reduce_max(NMB[:, :], psR[:, :], axis=AX,
                                         negate=True).then_inc(sv, 1)
                            c["v"] += 1
                            v.wait_ge(scm, c_base + tc + 1)
                            vhop(v, v.reciprocal(RSB[:, :], SMB[:, :]))
                            vhop(v, v.tensor_scalar_mul(PRN[:, :], PRB[:, :],
                                                        RSB[:, :]))
                            vhop(v, v.reduce_max(NMB[:, :], PRN[:, :], axis=AX))
                            vhop(v, v.tensor_single_scalar(M8[:, :], PRN[:, :],
                                                           NMB[:, :],
                                                           op=ALU.is_equal))
                            v.tensor_mul(GATES[:, tc * NE:(tc + 1) * NE],
                                         M8[:, :], PRN[:, :]).then_inc(sv, 1)
                            c["v"] += 1
                        v.memset(O_acc[:, :], 0.0)

                    @blk.scalar
                    def _(s):
                        for tc in range(OC):
                            s.wait_ge(sv, v_base + tc * 6 + 1)
                            s.activation(PRB[:, :], psR[:, :], ACT.Exp,
                                         bias=NMB[:, :],
                                         accum_out=SMB[:, :]).then_inc(scm, 1)
                            c["c"] += 1
                _blkE.__exit__(None, None, None)

                with nc.Block() as blk:
                    @blk.vector
                    def _(v):
                        vhop(v, v.tensor_single_scalar(IND[:, :], GATES[:, :], 0.0,
                                                       op=ALU.is_gt))
                        for tc in range(OC):
                            ins = v.reduce_sum(GSEL[:, tc:tc + 1],
                                               GATES[:, tc * NE:(tc + 1) * NE],
                                               axis=AX)
                        ins.then_inc(sv, 1)
                        c["v"] += 1
                    V1 = c["v"]

                    @blk.tensor
                    def _(t):
                        t.wait_ge(sv, V1)
                        for tc in range(OC):
                            ins = t.matmul(py1[0:NE, tc:tc + 1],
                                           IND[:, tc * NE:(tc + 1) * NE],
                                           onescol[:, :], start=(tc == 0),
                                           stop=(tc == OC - 1))
                        ins.then_inc(st, 1)
                        c["t"] += 1
                    T1 = c["t"]

                    @blk.vector
                    def _(v):
                        v.wait_ge(st, T1)
                        vhop(v, v.tensor_copy(CNT[:, :], py1[0:NE, 0:OC]))
                        vhop(v, v.tensor_tensor_scan(CUMI[:, :], ONES8R[:, :],
                                                     CNT[:, :], 0.0, op0=ALU.mult,
                                                     op1=ALU.add))
                        vhop(v, v.tensor_sub(CUMTOT[:, :], CUMI[:, :], CNT[:, :]))
                        vhop(v, v.tensor_copy(TOT[:, :], CUMI[:, OC - 1:OC]))
                    V2 = c["v"]

                    @blk.tensor
                    def _(t):
                        t.wait_ge(sv, V2)
                        t.matmul(py2[0:NE, 0:1], LTS8F[:, :], TOT[:, :],
                                 start=True, stop=True).then_inc(st, 1)
                        c["t"] += 1
                    T2 = c["t"]

                    @blk.vector
                    def _(v):
                        v.wait_ge(st, T2)
                        vhop(v, v.tensor_copy(BASEC[:, :], py2[0:NE, 0:1]))
                        vhop(v, v.tensor_scalar_add(CUMTOT[:, :], CUMTOT[:, :],
                                                    BASEC[:, :]))
                        vhop(v, v.tensor_add(ENDC[:, :], BASEC[:, :], TOT[:, :]))
                        for tg in range(NTG):
                            v.tensor_single_scalar(T81[:, :], BASEC[:, :],
                                                   float((tg + 1) * TGW + KMARG),
                                                   op=ALU.is_lt)
                            vhop(v, v.tensor_single_scalar(T82[:, :], ENDC[:, :],
                                                           float(tg * TGW - KMARG),
                                                           op=ALU.is_gt))
                            vhop(v, v.tensor_mul(FL8[:, tg:tg + 1], T81[:, :],
                                                 T82[:, :]))
                        vhop(v, v.tensor_copy(FL8I[:, :], FL8[:, :]))
                        vhop(v, v.reduce_max(T81[:, :], FL8[:, :], axis=AX))
                        vhop(v, v.tensor_copy(EFL8I[:, :], T81[:, :]))
                    V3 = c["v"]

                    @blk.sync
                    def _(sy):
                        sy.wait_ge(sv, V3)
                        sdma(sy, rowd, CUMTOT[:, :])
                        sdma(sy, flid, FL8I[:, :])
                        sdma(sy, efld, EFL8I[:, :])
                        sy.wait_ge(ssm, c["s"])
                        sy.dma_start(ROWSF[0:1, :].rearrange(
                            "o (t e) -> o t e", e=NE),
                            rowd.rearrange("e t -> t e"),
                            allow_slow_non_contiguous=True).then_inc(ssm, 16)
                        c["s"] += 16
                        sdma(sy, FLROW[0:1, :].rearrange("o (e t) -> o e t",
                                                         t=NTG), flid)
                        sdma(sy, EFROW[0:1, :].rearrange("o (e t) -> o e t", t=1),
                             efld)
                    S1 = c["s"]

                    @blk.vector
                    def _(v):
                        v.wait_ge(ssm, S1)
                        vhop(v, v.tensor_copy(ROWSH[:, :], ROWSF[:, :]))
                        vhop(v, v.tensor_sub(ROWSL[:, :], ROWSF[:, :],
                                             ROWSH[:, :]))
                    V4 = c["v"]
                    t_base3 = c["t"]
                    v_base4 = c["v"]

                    @blk.tensor
                    def _(t):
                        t.wait_ge(sv, V4)
                        for tc in range(OC):
                            if tc >= 2:
                                t.wait_ge(sv, v_base4 + tc - 1)
                            pb = [py0, py2][tc % 2]
                            t.matmul(pb[0:P, 0:NE], LTS[:, :],
                                     IND[:, tc * NE:(tc + 1) * NE],
                                     start=True, stop=False)
                            t.matmul(pb[0:P, 0:NE], ones1[:, :],
                                     ROWSH[0:1, tc * NE:(tc + 1) * NE],
                                     start=False, stop=False)
                            t.matmul(pb[0:P, 0:NE], ones1[:, :],
                                     ROWSL[0:1, tc * NE:(tc + 1) * NE],
                                     start=False, stop=True).then_inc(st, 1)
                            c["t"] += 1

                    @blk.vector
                    def _(v):
                        for tc in range(OC):
                            v.wait_ge(st, t_base3 + tc + 1)
                            pb = [py0, py2][tc % 2]
                            v.scalar_tensor_tensor(
                                T8[:, :], pb[0:P, 0:NE], 1.0,
                                IND[:, tc * NE:(tc + 1) * NE], op0=ALU.mult,
                                op1=ALU.mult,
                                accum_out=DESTF[:, tc:tc + 1]).then_inc(sv, 1)
                            c["v"] += 1
                        v.wait_ge(sv, c["v"])
                        v.engine_nop().then_inc(sv, 1)
                        c["v"] += 1

                with nc.Block() as blk:
                    t0 = c["t"]

                    @blk.tensor
                    def _(t):
                        t.matmul(psR[:, 0:NE], ones1[:, :], ROWSH[0:1, 0:NE],
                                 start=True, stop=False)
                        t.matmul(psR[:, 0:NE], ones1[:, :], ROWSL[0:1, 0:NE],
                                 start=False, stop=True).then_inc(st, 1)
                        c["t"] += 1
                    T1p = c["t"]

                    @blk.vector
                    def _(v):
                        v.wait_ge(st, T1p)
                        vhop(v, v.tensor_copy(BASEB[:, :], psR[0:P, 0:NE]))
                        vhop(v, v.tensor_single_scalar(ESRT[:, :], TOKC[:, :],
                                                       BASEB[:, 0:1],
                                                       op=ALU.is_ge))
                        for e2 in range(1, NE):
                            vhop(v, v.tensor_single_scalar(T8B[:, :], TOKC[:, :],
                                                           BASEB[:, e2:e2 + 1],
                                                           op=ALU.is_ge))
                            vhop(v, v.tensor_add(ESRT[:, :], ESRT[:, :],
                                                 T8B[:, :]))
                        v.tensor_scalar_add(ESRT[:, :], ESRT[:, :],
                                            -1.0).then_inc(sv, 1)
                        c["v"] += 1

                with nc.Block() as blk:
                    NST = OWN // 512
                    NSUB = 512 // P
                    DCH = DC // 2
                    t_b = c["t"]
                    v_b = c["v"]
                    c_b = c["c"]

                    VPP = OC + NSUB // 2

                    @blk.vector
                    def _(v):
                        v.wait_ge(sgc, 2)
                        k = 0
                        for stile in range(NST):
                            for half in range(2):
                                p = stile * 2 + half
                                for tc in range(OC):
                                    if k >= 2:
                                        v.wait_ge(st, t_b + k - 1)
                                    v.tensor_single_scalar(
                                        PRMB[:, (k % 2) * 512:(k % 2) * 512 + 512],
                                        JIOT[:, stile * 512:(stile + 1) * 512],
                                        DESTF[:, tc:tc + 1],
                                        op=ALU.is_equal).then_inc(sv, 1)
                                    c["v"] += 1
                                    k += 1
                                v.wait_ge(st, t_b + (p + 1) * OC)
                                for sb in range(NSUB // 2):
                                    sub = half * (NSUB // 2) + sb
                                    v.tensor_copy(
                                        INVF[:, stile * NSUB + sub:
                                             stile * NSUB + sub + 1],
                                        ph[sb][0:P, 0:1]).then_inc(sv, 1)
                                    c["v"] += 1

                    @blk.tensor
                    def _(t):
                        k = 0
                        for stile in range(NST):
                            for half in range(2):
                                p = stile * 2 + half
                                for tc in range(OC):
                                    t.wait_ge(sv, v_b + p * VPP + tc + 1)
                                    if tc == 0:
                                        if p > 0:
                                            t.wait_ge(scm, c_b + p * DCH)
                                            t.wait_ge(sv, v_b + p * VPP)
                                    sl = PRMB[:, (k % 2) * 512:
                                              (k % 2) * 512 + 512]
                                    for dcq in range(DCH):
                                        dc = half * DCH + dcq
                                        t.matmul(py[dcq % 4][0:P, 0:512],
                                                 X2TM[:, tc * D + dc * P:
                                                      tc * D + (dc + 1) * P],
                                                 sl, start=(tc == 0),
                                                 stop=(tc == OC - 1))
                                    for sb in range(NSUB // 2):
                                        sub = half * (NSUB // 2) + sb
                                        slc = PRMB[:, (k % 2) * 512 + sub * P:
                                                   (k % 2) * 512 + (sub + 1) * P]
                                        t.matmul(ph[sb][0:P, 0:1], slc,
                                                 TOKCH[:, tc:tc + 1],
                                                 start=(tc == 0), stop=False)
                                        ins = t.matmul(ph[sb][0:P, 0:1], slc,
                                                       TOKCL[:, tc:tc + 1],
                                                       start=False,
                                                       stop=(tc == OC - 1))
                                    ins.then_inc(st, 1)
                                    c["t"] += 1
                                    k += 1

                    @blk.scalar
                    def _(s):
                        for stile in range(NST):
                            for half in range(2):
                                p = stile * 2 + half
                                s.wait_ge(st, t_b + (p + 1) * OC)
                                for dcq in range(DCH):
                                    dc = half * DCH + dcq
                                    s.activation(
                                        X3FM[:, dc * OWN + stile * 512:
                                             dc * OWN + (stile + 1) * 512],
                                        py[dcq % 4][0:P, 0:512],
                                        ACT.Copy).then_inc(scm, 1)
                                    c["c"] += 1

            with contextlib.ExitStack() as _esW:
                _w = _esW.enter_context
                NWB = 2
                W1B = [_w(nc.sbuf_tensor(f"W1B{i}", [P, DC * FSUB], BF16))
                       for i in range(NWB)]
                W2B = [_w(nc.sbuf_tensor(f"W2B{i}", [P, FCS * D], BF16))
                       for i in range(NWB)]
                HB = _w(nc.sbuf_tensor("HB", [P, FCS * TGW], BF16))
                B2B = _w(nc.sbuf_tensor("B2B", [1, D], BF16))

                UPT = NTG * (FCS + 1)
                t0f = c["t"]
                gwu = {}
                with nc.Block() as blk:
                    @blk.gpsimd
                    def _(g):
                        for u in range(NE * SC):
                            eu, scu = divmod(u, SC)
                            if u >= NWB:
                                g.wait_ge(st, t0f + (u - NWB + 1) * UPT)
                            wb = u % NWB

                            def _wload(gg):
                                gg.dma_start(
                                    W1B[wb][:, :].rearrange(
                                        "p (dc f) -> p dc f", f=FSUB),
                                    w1[eu:eu + 1, :,
                                       scu * FSUB:(scu + 1) * FSUB].rearrange(
                                        "o (dc p) f -> p (o dc) f", p=P)
                                ).then_inc(sg, 16)
                                gg.dma_start(
                                    W2B[wb][:, :].rearrange(
                                        "p (fc d) -> p fc d", d=D),
                                    w2[eu:eu + 1,
                                       scu * FSUB:(scu + 1) * FSUB, :].rearrange(
                                        "o (fc p) d -> p (o fc) d", p=P)
                                ).then_inc(sg, 16)
                            if USE_BRANCHES and BR_G:
                                with g.register(f"rgw{u}") as rgw:
                                    g.reg_load(rgw, EFROW[0:1, eu:eu + 1])
                                    with g.If_ne(rgw, 0):
                                        _wload(g)
                                    with g.If_eq(rgw, 0):
                                        g.dma_start(DUMB[:, :], b2[0:1, 0:8]
                                                    ).then_inc(sg, 16)
                                        g.dma_start(DUMB[:, :], b2[0:1, 0:8]
                                                    ).then_inc(sg, 16)
                            else:
                                _wload(g)
                            c["g"] += 32
                            gwu[u] = c["g"]
                            if u == 0:
                                gdma(g, B2B[:, :], b2[0:1, :])
                            if scu == min(1, SC - 1) and eu + 1 < NE:
                                g.wait_ge(st, t0f + (eu * SC + 1) * UPT)
                                gdma(g, B2B[:, :], b2[eu + 1:eu + 2, :])

                    for e in range(NE):
                        t_e0 = c["t"]
                        c_e0 = c["c"]
                        v_e0 = c["v"]

                        @blk.tensor
                        def _(t):
                            t.wait_ge(sg, gwu[e * SC])
                            if USE_BRANCHES and BR_T:
                                efv = nc.values_load(
                                    EFROW[0:1, e:e + 1],
                                    engines=[mybir.EngineType.PE])
                                arm_iter = t.Switch(efv, 2)
                                _rgt = t.register(f"rtf{e}")
                                rtf = _rgt.__enter__()
                            else:
                                arm_iter = iter([1])
                            for arm in arm_iter:
                                if arm == 0:
                                    t.sem_inc(st, SC * UPT)
                                    continue
                                for sci in range(SC):
                                    i = e * SC + sci
                                    buf = i % NWB
                                    c_base = c_e0 + sci * NTG * FCS
                                    v_base = v_e0 + sci * NTG * (TGC + 1)
                                    if sci >= 1:
                                        t.wait_ge(sg, gwu[i])
                                    for tg in range(NTG):
                                        if USE_BRANCHES and BR_T:
                                            t.reg_load(rtf,
                                                       FLROW[0:1, e * NTG + tg:
                                                             e * NTG + tg + 1])
                                            t.drain()
                                            _if = t.If_ne(rtf, 0)
                                            _if.__enter__()
                                        for fc in range(FCS):
                                            k = tg * FCS + fc
                                            if c_base + k - 1 > 0:
                                                t.wait_ge(scm, c_base + k - 1)
                                            dst = ph[k % 2][:, :]
                                            for dci in range(DC):
                                                ins = t.matmul(
                                                    dst,
                                                    W1B[buf][:,
                                                             dci * FSUB + fc * P:
                                                             dci * FSUB
                                                             + (fc + 1) * P],
                                                    X3FM[:,
                                                         dci * OWN + tg * TGW:
                                                         dci * OWN
                                                         + (tg + 1) * TGW],
                                                    start=(dci == 0),
                                                    stop=(dci == DC - 1))
                                            ins.then_inc(st, 1)
                                            c["t"] += 1
                                        t.wait_ge(scm, c_base + (tg + 1) * FCS)
                                        if sci > 0 or tg >= 1:
                                            t.wait_ge(sv,
                                                      v_base + tg * (TGC + 1))
                                        elif e >= 1:
                                            t.wait_ge(sv, v_e0)
                                        if sci == 0:
                                            for tc in range(TGC):
                                                for dt in range(ND):
                                                    t.matmul(
                                                        py[tc * ND + dt][0:P,
                                                                         0:DTW],
                                                        ones1[:, :],
                                                        B2B[:, dt * DTW:
                                                            (dt + 1) * DTW],
                                                        start=True, stop=False)
                                        for fc in range(FCS):
                                            for tc in range(TGC):
                                                lh = HB[:, fc * TGW + tc * P:
                                                        fc * TGW + (tc + 1) * P]
                                                for dt in range(ND):
                                                    ins = t.matmul(
                                                        py[tc * ND + dt][0:P,
                                                                         0:DTW],
                                                        lh,
                                                        W2B[buf][:,
                                                                 fc * D
                                                                 + dt * DTW:
                                                                 fc * D
                                                                 + (dt + 1)
                                                                 * DTW],
                                                        start=(sci > 0
                                                               and fc == 0),
                                                        stop=(fc == FCS - 1))
                                        ins.then_inc(st, 1)
                                        c["t"] += 1
                                        if USE_BRANCHES and BR_T:
                                            _if.__exit__(None, None, None)
                                            with t.If_eq(rtf, 0):
                                                t.sem_inc(st, FCS + 1)
                            if USE_BRANCHES and BR_T:
                                _rgt.__exit__(None, None, None)

                        @blk.scalar
                        def _(s):
                            if USE_BRANCHES and BR_C:
                                _rgc = s.register(f"rcf{e}")
                                rcf = _rgc.__enter__()
                            for sci in range(SC):
                                t_base = t_e0 + sci * UPT
                                for tg in range(NTG):
                                    if USE_BRANCHES and BR_C:
                                        s.reg_load(rcf,
                                                   FLROW[0:1, e * NTG + tg:
                                                         e * NTG + tg + 1])
                                        _if = s.If_ne(rcf, 0)
                                        _if.__enter__()
                                    for fc in range(FCS):
                                        k = tg * FCS + fc
                                        s.wait_ge(st, t_base + tg + k + 1)
                                        bc = e * FCT + sci * FCS + fc
                                        s.activation(
                                            HB[:, fc * TGW:(fc + 1) * TGW],
                                            ph[k % 2][:, :], ACT.Relu,
                                            bias=B1all[:, bc:bc + 1]
                                        ).then_inc(scm, 1)
                                        c["c"] += 1
                                    if USE_BRANCHES and BR_C:
                                        _if.__exit__(None, None, None)
                                        with s.If_eq(rcf, 0):
                                            s.sem_inc(scm, FCS)
                            if USE_BRANCHES and BR_C:
                                _rgc.__exit__(None, None, None)

                        @blk.vector
                        def _(v):
                            if USE_BRANCHES and BR_V:
                                _rgv = v.register(f"rvf{e}")
                                rvf = _rgv.__enter__()
                            for sci in range(SC):
                                t_base = t_e0 + sci * UPT
                                for tg in range(NTG):
                                    if USE_BRANCHES and BR_V:
                                        v.reg_load(rvf,
                                                   FLROW[0:1, e * NTG + tg:
                                                         e * NTG + tg + 1])
                                        _if = v.If_ne(rvf, 0)
                                        _if.__enter__()
                                    v.wait_ge(st,
                                              t_base + (tg + 1) * (FCS + 1))
                                    for tc in range(TGC):
                                        tcg = tg * TGC + tc
                                        vhop(v, v.tensor_single_scalar(
                                            SCL[:, tc:tc + 1],
                                            ESRT[:, tcg:tcg + 1], float(e),
                                            op=ALU.is_equal))
                                        for dt in range(ND):
                                            sl = O_acc[:, tcg * D + dt * DTW:
                                                       tcg * D + (dt + 1) * DTW]
                                            ins = v.scalar_tensor_tensor(
                                                sl, py[tc * ND + dt][0:P, 0:DTW],
                                                SCL[:, tc:tc + 1],
                                                sl, op0=ALU.mult, op1=ALU.add)
                                    ins.then_inc(sv, 1)
                                    c["v"] += 1
                                    if USE_BRANCHES and BR_V:
                                        _if.__exit__(None, None, None)
                                        with v.If_eq(rvf, 0):
                                            v.sem_inc(sv, TGC + 1)
                            if USE_BRANCHES and BR_V:
                                _rgv.__exit__(None, None, None)


                with nc.Block() as blk:
                    NTT = OWN // 512
                    SCC = OWN // P
                    t_b2 = c["t"]
                    v_b2 = c["v"]
                    c_b2 = c["c"]

                    @blk.vector
                    def _(v):
                        k = 0
                        for tt in range(NTT):
                            for dt in range(ND):
                                for sc in range(SCC):
                                    if k >= 2:
                                        v.wait_ge(st, t_b2 + k - 1)
                                    v.tensor_single_scalar(
                                        PRMB[:, (k % 2) * 512:
                                             (k % 2) * 512 + 512],
                                        JIOT[:, tt * 512:(tt + 1) * 512],
                                        INVF[:, sc:sc + 1],
                                        op=ALU.is_equal).then_inc(sv, 1)
                                    c["v"] += 1
                                    k += 1

                    @blk.tensor
                    def _(t):
                        k = 0
                        for tt in range(NTT):
                            for dt in range(ND):
                                p = tt * ND + dt
                                for sc in range(SCC):
                                    t.wait_ge(sv, v_b2 + k + 1)
                                    if sc == 0 and p >= 1:
                                        t.wait_ge(scm, c_b2 + p * 4)
                                    sl = PRMB[:, (k % 2) * 512:
                                              (k % 2) * 512 + 512]
                                    for tq in range(4):
                                        ins = t.matmul(
                                            py[tq][0:P, 0:DTW],
                                            sl[:, tq * P:(tq + 1) * P],
                                            O_acc[:, sc * D + dt * DTW:
                                                  sc * D + dt * DTW + DTW],
                                            start=(sc == 0),
                                            stop=(sc == SCC - 1))
                                    ins.then_inc(st, 1)
                                    c["t"] += 1
                                    k += 1

                    @blk.scalar
                    def _(s):
                        for tt in range(NTT):
                            for dt in range(ND):
                                p = tt * ND + dt
                                s.wait_ge(st, t_b2 + (p + 1) * SCC)
                                for tq in range(4):
                                    tcg = tt * 4 + tq
                                    s.activation(
                                        X3FM[:, tcg * D + dt * DTW:
                                             tcg * D + dt * DTW + DTW],
                                        py[tq][0:P, 0:DTW], ACT.Identity,
                                        scale=GSEL[:, tcg:tcg + 1]
                                    ).then_inc(scm, 1)
                                    c["c"] += 1

                    c_u0 = c_b2
                    v_l40 = c["v"]

                    @blk.vector
                    def _(v):
                        for tt in range(NTT):
                            v.wait_ge(scm, c_u0 + (tt + 1) * ND * 4)
                            for tq in range(4):
                                i2 = tt * 4 + tq
                                sl = X3FM[:, i2 * D:(i2 + 1) * D]
                                v.reduce_sum(SS[:, i2:i2 + 1], sl, axis=AX)
                                v.scalar_tensor_tensor(
                                    TRX[:, 0:D], sl, 1.0, sl, op0=ALU.mult,
                                    op1=ALU.mult, accum_out=SQ[:, i2:i2 + 1])
                        v.wait_ge(sv, c["v"])
                        v.engine_nop().then_inc(sv, 1)
                        c["v"] += 1
                        v.wait_ge(sv, c["v"])
                        ck = slice(0, OC)
                        vhop(v, v.tensor_scalar_mul(MEANB[:, ck], SS[:, ck],
                                                    invD))
                        vhop(v, v.tensor_mul(T1B[:, ck], MEANB[:, ck],
                                             MEANB[:, ck]))
                        vhop(v, v.scalar_tensor_tensor(VARPB[:, ck], SQ[:, ck],
                                                       invD, T1B[:, ck],
                                                       op0=ALU.mult,
                                                       op1=ALU.subtract))
                        v.tensor_scalar_add(VARPB[:, ck], VARPB[:, ck],
                                            LN_EPS).then_inc(sv, 1)
                        c["v"] += 1

                    @blk.scalar
                    def _(s):
                        rstd_scalar(s, OC)

                    @blk.vector
                    def _(v):
                        nbias_vec(v, OC)

                    c_base = c["c"]
                    v_base = c["v"]
                    s_base = c["s"]

                    @blk.sync
                    def _(s):
                        for tc in range(OC):
                            s.wait_ge(sv, v_base + tc + 1)
                            sdma(s, yo[tc * P:(tc + 1) * P, :],
                                 STG[:, (tc % 2) * D:(tc % 2 + 1) * D])
                        s.wait_ge(ssm, c["s"])

                    @blk.scalar
                    def _(s):
                        s.wait_ge(sv, v_base)
                        for tc in range(OC):
                            if tc >= 2:
                                s.wait_ge(ssm, s_base + (tc - 1) * 16)
                            s.activation(STG[:, (tc % 2) * D:(tc % 2 + 1) * D],
                                         X3FM[:, tc * D:(tc + 1) * D], ACT.Identity,
                                         bias=NBSB[:, tc:tc + 1],
                                         scale=RSTDB[:, tc:tc + 1]
                                         ).then_inc(scm, 1)
                            c["c"] += 1

                    @blk.vector
                    def _(v):
                        for tc in range(OC):
                            v.wait_ge(scm, c_base + tc + 1)
                            sl = STG[:, (tc % 2) * D:(tc % 2 + 1) * D]
                            v.tensor_add(sl, sl,
                                         XQR[:, tc * D:(tc + 1) * D]
                                         ).then_inc(sv, 1)
                            c["v"] += 1

                    @blk.gpsimd
                    def _(g):
                        g.wait_ge(sg, c["g"])

    nc.finalize()
    return nc


_MASK_CACHE = {}


def _make_mask(qstart, T, OWN):
    key = (qstart, T, OWN)
    if key not in _MASK_CACHE:
        SMAX = T - P
        MW = SMAX + OWN
        u = np.arange(MW)[None, :]
        pr = np.arange(P)[:, None]
        _MASK_CACHE[key] = np.ascontiguousarray(
            (u >= pr + SMAX - qstart).astype(ml_dtypes.bfloat16))
    return _MASK_CACHE[key]


def kernel(L0_attn_x, L0_attn_Wq, L0_attn_Wk, L0_attn_Wv, L0_ffn_router_W,
           L0_ffn_router_b, L0_ffn_W1, L0_ffn_b1, L0_ffn_W2, L0_ffn_b2):
    x = np.asarray(L0_attn_x, dtype=np.float32)
    B, T, D = x.shape
    wq = np.ascontiguousarray(np.asarray(L0_attn_Wq, dtype=np.float32))
    wk = np.ascontiguousarray(np.asarray(L0_attn_Wk, dtype=np.float32))
    wv = np.ascontiguousarray(np.asarray(L0_attn_Wv, dtype=np.float32))
    rw = np.ascontiguousarray(np.asarray(L0_ffn_router_W, dtype=np.float32))
    rb = np.ascontiguousarray(np.asarray(L0_ffn_router_b,
                                         dtype=np.float32)).reshape(1, -1)
    w1 = np.ascontiguousarray(np.asarray(L0_ffn_W1, dtype=np.float32))
    b1 = np.asarray(L0_ffn_b1, dtype=np.float32)
    ne_, f_ = b1.shape
    b1 = np.ascontiguousarray(
        b1.reshape(ne_, f_ // 128, 128).transpose(2, 0, 1).reshape(128, -1))
    w2 = np.ascontiguousarray(np.asarray(L0_ffn_W2, dtype=np.float32))
    b2 = np.ascontiguousarray(np.asarray(L0_ffn_b2, dtype=np.float32))
    NE, _, F = w1.shape
    OWN = (B * T) // 8

    nc = build_program(D=D, F=F, T=T, OWN=OWN, NE=NE)

    in_maps = []
    for core in range(8):
        b, a = divmod(core, 4)
        qs = a * OWN
        in_maps.append({
            "xb": np.ascontiguousarray(x[b]),
            "xq": np.ascontiguousarray(x[b, qs:qs + OWN]),
            "mk": _make_mask(qs, T, OWN),
            "wq": wq, "wk": wk, "wv": wv, "rw": rw, "rb": rb,
            "w1": w1, "b1": b1, "w2": w2, "b2": b2,
        })
    res = run_bass_kernel_spmd(nc, in_maps, core_ids=list(range(8)))
    out = np.empty((B, T, D), dtype=np.float32)
    for core in range(8):
        b, a = divmod(core, 4)
        qs = a * OWN
        out[b, qs:qs + OWN] = np.asarray(res.results[core]["y"],
                                         dtype=np.float32).reshape(OWN, D)
    return out
